# version 08
# speedup vs baseline: 1.3560x; 1.0428x over previous
import contextlib
import numpy as np
import ml_dtypes
import concourse.bass as bass
import concourse.mybir as mybir
from concourse.bass_utils import run_bass_kernel_spmd

F32 = mybir.dt.float32
BF16 = mybir.dt.bfloat16
I32 = mybir.dt.int32
AF = mybir.ActivationFunctionType
ALU = mybir.AluOpType
AX = mybir.AxisListType
BF16NP = ml_dtypes.bfloat16

P = 128
T = 2048
D = 1024
NT = T // P
DC = D // P
NST = 16
EPS = 1e-5

ENGINES = ("tensor", "vector", "scalar", "gpsimd", "sync")


class Sched:

    def __init__(self, nc):
        self.nc = nc
        self.items = []
        self.cur = 0

    def op(self, engine, fn):
        assert engine in ENGINES
        self.items.append((self.cur, engine, fn))

    def bar(self):
        self.cur += 1

    def build(self, ph):
        nph = self.cur + 1
        per = {e: {} for e in ENGINES}
        for p, e, f in self.items:
            per[e].setdefault(p, []).append(f)

        def mk(ename):
            def body(eng):
                for p in range(nph):
                    if p:
                        eng.wait_ge(ph, len(ENGINES) * p)
                    last = None
                    for f in per[ename].get(p, []):
                        r = f(eng)
                        if r is not None:
                            last = r
                    if last is not None and ename != "sync":
                        last.then_inc(ph, 1)
                    else:
                        eng.nop().then_inc(ph, 1)
            return body
        return {e: mk(e) for e in ENGINES}


class VSync:

    def __init__(self, sem):
        self.sem = sem
        self.n = 0

    def edge(self, eng, inst):
        inst.then_inc(self.sem, 1)
        self.n += 1
        eng.wait_ge(self.sem, self.n)
        return inst


class Dma:

    def __init__(self, sem):
        self.sem = sem
        self.count = 0

    def start(self, eng, out, in_, **kw):
        eng.dma_start(out=out, in_=in_, **kw).then_inc(self.sem, 16)
        self.count += 16
        return self.count

    def marker(self):
        return [0]

    def fill(self, m):
        m[0] = self.count




def ln_stats_v(v, src, ST, VS):
    VS.edge(v, v.bn_stats(out=ST[:, 56:62], in_=src[:, 0:512]))
    VS.edge(v, v.bn_stats(out=ST[:, 64:70], in_=src[:, 512:1024]))
    VS.edge(v, v.bn_aggr(out=ST[:, 0:2], in_=ST[:, 56:62]))
    VS.edge(v, v.bn_aggr(out=ST[:, 8:10], in_=ST[:, 64:70]))
    VS.edge(v, v.tensor_tensor(out=ST[:, 16:17], in0=ST[:, 0:1], in1=ST[:, 8:9],
                               op=ALU.subtract))
    VS.edge(v, v.tensor_tensor(out=ST[:, 16:17], in0=ST[:, 16:17], in1=ST[:, 16:17],
                               op=ALU.mult))
    VS.edge(v, v.tensor_tensor(out=ST[:, 24:25], in0=ST[:, 0:1], in1=ST[:, 8:9],
                               op=ALU.add))
    VS.edge(v, v.tensor_scalar(out=ST[:, 24:25], in0=ST[:, 24:25], scalar1=0.5,
                               scalar2=None, op0=ALU.mult))
    VS.edge(v, v.tensor_tensor(out=ST[:, 32:33], in0=ST[:, 1:2], in1=ST[:, 9:10],
                               op=ALU.add))
    VS.edge(v, v.tensor_scalar(out=ST[:, 16:17], in0=ST[:, 16:17], scalar1=0.25,
                               scalar2=None, op0=ALU.mult))
    VS.edge(v, v.scalar_tensor_tensor(out=ST[:, 32:33], in0=ST[:, 32:33], scalar=0.5,
                                      in1=ST[:, 16:17], op0=ALU.mult, op1=ALU.add))
    return VS.edge(v, v.tensor_scalar(out=ST[:, 40:41], in0=ST[:, 32:33], scalar1=EPS,
                                      scalar2=None, op0=ALU.add))


def ln_sqrt_s(sc, ST):
    return sc.activation(out=ST[:, 40:41], in_=ST[:, 40:41], func=AF.Sqrt)


def ln_coefs_v(v, ST, VS):
    VS.edge(v, v.reciprocal(out=ST[:, 40:41], in_=ST[:, 40:41]))
    VS.edge(v, v.tensor_tensor(out=ST[:, 48:49], in0=ST[:, 24:25], in1=ST[:, 40:41],
                               op=ALU.mult))
    return VS.edge(v, v.tensor_scalar(out=ST[:, 48:49], in0=ST[:, 48:49], scalar1=-1.0,
                                      scalar2=None, op0=ALU.mult))




def build_l0(nc, n_cores=8, store_out=True, debug=False, fused=False):

    def din(name, shape, dt=F32):
        return nc.dram_tensor(name, shape, dt, kind="ExternalInput").ap()

    def dint(name, shape, dt=F32):
        return nc.dram_tensor(name, shape, dt, kind="Internal").ap()

    x = din("x", [T, D])
    x1 = nc.dram_tensor("x1", [T, D], F32,
                        kind="ExternalOutput" if store_out else "Internal").ap()
    alog = din("alog_r", [P, P])
    bwT = din("bwT", [D, NST])
    cwT = din("cwT", [D, NST])
    d_row = din("d_row", [1, D])
    gwT = din("gate_wT", [D, D])
    gb_row = din("gate_b_row", [1, D])
    lng_row = din("ln_g_row", [1, D])
    lnb_row = din("ln_b_row", [1, D])
    l0wT = din("l0w_T", [D, D])
    g2_row = din("g2_row", [1, D])
    b2_row = din("b2_row", [1, D])
    ident_f = din("ident_f", [P, P])
    ident_b = din("ident_b", [P, P], BF16)
    ones_row = din("ones_row", [1, P])

    dsc = (lambda n, s, dt=F32: nc.dram_tensor(n, s, dt, kind="ExternalOutput").ap()) if debug else dint
    if debug:
        dbg_st = nc.dram_tensor("dbg_st", [P, 72], F32, kind="ExternalOutput").ap()
        dbg_xn = nc.dram_tensor("dbg_xn", [P, D], F32, kind="ExternalOutput").ap()
        dbg_xnT = nc.dram_tensor("dbg_xnT", [P, T], BF16, kind="ExternalOutput").ap()
        dbg_st1a = nc.dram_tensor("dbg_st1a", [P, 72], F32, kind="ExternalOutput").ap()
        dbg_st1b = nc.dram_tensor("dbg_st1b", [P, 72], F32, kind="ExternalOutput").ap()
        dbg_y = nc.dram_tensor("dbg_y", [P, T], BF16, kind="ExternalOutput").ap()
        dbg_sg = nc.dram_tensor("dbg_sg", [P, D], F32, kind="ExternalOutput").ap()
    xn_sp = dsc("xn_sp", [T, D])
    xn2_sp = dsc("xn2_sp", [T, D])
    x1a_sp = dsc("x1a_sp", [T, D])
    bxT_d = dsc("bxT_d", [NST, T], BF16)
    cxT_d = dsc("cxT_d", [NST, T], BF16)
    sc_d = dint("sc_d", [1, 4])
    gmax_row = din("gmax_row", [1, 8])

    x_t = x.rearrange("(n p) d -> n p d", p=P)
    x1_t = x1.rearrange("(n p) d -> n p d", p=P)
    xn_t = xn_sp.rearrange("(n p) d -> n p d", p=P)
    xn2_t = xn2_sp.rearrange("(n p) d -> n p d", p=P)
    x1a_t = x1a_sp.rearrange("(n p) d -> n p d", p=P)

    ctx = contextlib.ExitStack()
    sb = lambda name, shape, dt=F32: ctx.enter_context(nc.sbuf_tensor(name, shape, dt))

    xnT = [sb(f"xnT{c}", [P, T], BF16) for c in range(DC)]
    y_sb = [sb(f"ysb{c}", [P, T], BF16) for c in range(DC)]
    wqT = [sb(f"wqT{c}", [P, D], BF16) for c in range(DC)]
    gwTb = [sb(f"gwTb{c}", [P, D], BF16) for c in range(DC)]
    idf = sb("idf", [P, P])
    idb = sb("idb", [P, P], BF16)
    ones_r = sb("ones_r", [1, P])
    aexp = sb("aexp", [P, P])
    aexb = sb("aexb", [P, P], BF16)
    dm1b = sb("dm1b", [P, D])
    gbb = sb("gbb", [1, D])
    lngb = sb("lngb", [P, D])
    lnbb = sb("lnbb", [P, D])
    g2b = sb("g2b", [P, D])
    b2b = sb("b2b", [P, D])
    NW = 2
    W4 = [sb(f"w4_{i}", [P, D]) for i in range(NW)]
    XN4 = [sb(f"xn4_{i}", [P, D]) for i in range(NW)]
    XB4 = [sb(f"xb4_{i}", [P, D], BF16) for i in range(NW)]
    U4 = [sb(f"u4_{i}", [P, D]) for i in range(NW)]
    V4 = [sb(f"v4_{i}", [P, D]) for i in range(NW)]
    SG4 = [sb(f"sg4_{i}", [P, D]) for i in range(NW)]
    YT2 = [sb(f"yt2_{i}", [P, D]) for i in range(NW)]
    ST4 = [sb(f"st4_{i}", [P, 72]) for i in range(NW)]
    mark = sb("mark", [P, 8])
    mkr = mark
    qT_t = [sb(f"qTt{c}", [P, NW * P], BF16) for c in range(DC)]
    amax = sb("amax", [P, 16])
    wsum = sb("wsum", [P, 64])
    wmsk = sb("wmsk", [P, 16])
    wcnt = sb("wcnt", [P, 64])
    scal = sb("scal", [1, P])
    scol = sb("scol", [P, 4])
    bw_sb = sb("bw_sb", [P, P])
    bwb = sb("bwb", [P, P], BF16)
    cwb = sb("cwb", [P, P], BF16)
    TBW = 512
    bc2 = [sb(f"bc{i}", [P, TBW], BF16) for i in range(2)]
    cc2 = [sb(f"cc{i}", [P, TBW], BF16) for i in range(2)]
    gsc = [sb("gsc0", [P, TBW], BF16)] * 2
    hsc = [sb("hsc0", [P, TBW], BF16)] * 2
    psc = [sb("psc0", [P, DC * TBW], BF16)] * 2
    carry = sb("carry", [P, P])
    bx_sb = sb("bx_sb", [NST, T], BF16)
    cx_sb = sb("cx_sb", [NST, T], BF16)

    psA = ctx.enter_context(nc.psum_tensor("psA", [P, T], F32))
    psB = ctx.enter_context(nc.psum_tensor("psB", [P, T], F32))
    psAb = psA[:].bitcast(BF16)

    sched = Sched(nc)
    ld_sem = ctx.enter_context(nc.semaphore("LdL0"))
    st_sem = ctx.enter_context(nc.semaphore("StL0"))
    ph = ctx.enter_context(nc.semaphore("PhL0"))
    vs_sem = ctx.enter_context(nc.semaphore("VsL0"))
    LD = Dma(ld_sem)
    STD = Dma(st_sem)
    VS = VSync(vs_sem)

    sched.op("vector", lambda v: v.memset(mark[:, 0:1], 0.0))

    cw = LD.marker()

    def ph_consts(s):
        LD.start(s, idf[:], ident_f)
        LD.start(s, idb[:], ident_b)
        LD.start(s, ones_r[:], ones_row)
        LD.start(s, aexp[:], alog)
        LD.start(s, dm1b[:], d_row.squeeze(0).partition_broadcast(P))
        LD.start(s, gbb[:], gb_row)
        LD.start(s, lngb[:], lng_row.squeeze(0).partition_broadcast(P))
        LD.start(s, lnbb[:], lnb_row.squeeze(0).partition_broadcast(P))
        LD.start(s, g2b[:], g2_row.squeeze(0).partition_broadcast(P))
        LD.start(s, b2b[:], b2_row.squeeze(0).partition_broadcast(P))
        LD.start(s, bw_sb[:].rearrange("p (c n) -> p c n", n=NST), bwT.rearrange("(c p) n -> p c n", p=P))
        LD.fill(cw)
    sched.op("sync", ph_consts)
    sched.bar()

    def ph_c2v(v):
        v.wait_ge(ld_sem, cw[0])
        v.tensor_scalar(out=dm1b[:], in0=dm1b[:], scalar1=1.0, scalar2=None,
                        op0=ALU.subtract)
        v.tensor_copy(out=bwb[:], in_=bw_sb[:])
        return v.memset(carry[:], 0.0)
    sched.op("vector", ph_c2v)

    def ph_c2s(sc):
        sc.wait_ge(ld_sem, cw[0])
        return sc.activation(out=aexp[:], in_=aexp[:], func=AF.Exp)
    sched.op("scalar", ph_c2s)
    sched.bar()

    cw2 = LD.marker()

    def ph_c2l(s):
        LD.start(s, bw_sb[:].rearrange("p (c n) -> p c n", n=NST), cwT.rearrange("(c p) n -> p c n", p=P))
        LD.fill(cw2)
    sched.op("sync", ph_c2l)
    sched.bar()

    def ph_c3v(v):
        v.wait_ge(ld_sem, cw2[0])
        v.tensor_copy(out=cwb[:], in_=bw_sb[:])
        return v.tensor_copy(out=aexb[:], in_=aexp[:])
    sched.op("vector", ph_c3v)
    sched.bar()

    gwT_t = gwT.rearrange("(c p) e -> c p e", p=P)
    for grp in range(4):
        cnt = LD.marker()

        def ph_gwl(s, grp=grp, cnt=cnt):
            for i in range(2):
                LD.start(s, W4[i][:], gwT_t[grp * 2 + i])
            LD.fill(cnt)
        sched.op("sync", ph_gwl)
        sched.bar()

        def ph_gwc(v, grp=grp, cnt=cnt):
            v.wait_ge(ld_sem, cnt[0])
            r = None
            for i in range(2):
                r = v.tensor_copy(out=gwTb[grp * 2 + i][:], in_=W4[i][:])
            return r
        sched.op("vector", ph_gwc)
        sched.bar()

    prev_store = STD.marker()
    for g in range(8):
        tiles = [2 * g + i for i in range(2)]
        cnt = LD.marker()

        def a0(s, tiles=tiles, pv=prev_store, cnt=cnt):
            s.wait_ge(st_sem, pv[0])
            for i, tt in enumerate(tiles):
                LD.start(s, W4[i][:], x_t[tt])
            LD.fill(cnt)
        sched.op("sync", a0)
        sched.bar()

        def a1(v, cnt=cnt, g=g):
            v.wait_ge(ld_sem, cnt[0])
            if debug and g == 0:
                v.tensor_copy(out=SG4[0][:], in_=W4[0][:])
            for i in range(2):
                ln_stats_v(v, W4[i][:], ST4[i], VS)
            return v.tensor_copy(out=mkr[:, 0:1], in_=mkr[:, 0:1])
        sched.op("vector", a1)
        sched.bar()

        def a2(sc):
            r = None
            for i in range(2):
                r = ln_sqrt_s(sc, ST4[i])
            return r
        sched.op("scalar", a2)
        if debug and g == 0:
            def a2d(s):
                STD.start(s, dbg_st1a, ST4[1][:])
            sched.op("sync", a2d)
        sched.bar()

        def a4(v):
            r = None
            for i in range(2):
                ln_coefs_v(v, ST4[i], VS)
                r = v.tensor_scalar(out=XN4[i][:], in0=W4[i][:],
                                    scalar1=ST4[i][:, 40:41],
                                    scalar2=ST4[i][:, 48:49],
                                    op0=ALU.mult, op1=ALU.add)
            return r
        sched.op("vector", a4)

        if debug and g == 0:
            def a4snap(v):
                return v.tensor_copy(out=SG4[1][:, 0:72], in_=ST4[0][:])
            sched.op("vector", a4snap)
            sched.bar()
            def a4d(s):
                STD.start(s, dbg_st, SG4[1][:, 0:72])
                STD.start(s, dbg_xn, SG4[0][:])
            sched.op("sync", a4d)
        sched.bar()

        def a5(v):
            for i in range(2):
                v.tensor_copy(out=XB4[i][:], in_=XN4[i][:])
                ln_stats_v(v, XN4[i][:], ST4[i], VS)
            return v.tensor_copy(out=mkr[:, 0:1], in_=mkr[:, 0:1])
        sched.op("vector", a5)

        def a5s(s, tiles=tiles):
            for i, tt in enumerate(tiles):
                STD.start(s, xn_t[tt], XN4[i][:])
        sched.op("sync", a5s)
        sched.bar()

        def a6(sc):
            r = None
            for i in range(2):
                r = ln_sqrt_s(sc, ST4[i])
            return r
        sched.op("scalar", a6)

        def a6t(t, g=g):
            r = None
            for dc in range(DC):
                for i in range(2):
                    r = t.transpose(psAb[:, dc * 256 + i * P: dc * 256 + (i + 1) * P],
                                    XB4[i][:, dc * P:(dc + 1) * P], idb[:])
            return r
        sched.op("tensor", a6t)
        sched.bar()

        def a7(v, g=g):
            r = None
            for i in range(2):
                ln_coefs_v(v, ST4[i], VS)
            for dc in range(DC):
                r = v.tensor_copy(out=xnT[dc][:, g * 256:(g + 1) * 256],
                                  in_=psAb[:, dc * 256:(dc + 1) * 256])
            return r
        sched.op("vector", a7)
        sched.bar()

        def a8(v):
            r = None
            for i in range(2):
                r = v.tensor_scalar(out=U4[i][:], in0=XN4[i][:],
                                    scalar1=ST4[i][:, 40:41],
                                    scalar2=ST4[i][:, 48:49],
                                    op0=ALU.mult, op1=ALU.add)
            return r
        sched.op("vector", a8)
        sched.bar()

        def a9(v, g=g):
            r = None
            for i in range(2):
                v.tensor_tensor(out=U4[i][:], in0=U4[i][:], in1=g2b[:], op=ALU.mult)
                r = v.tensor_tensor(out=U4[i][:], in0=U4[i][:], in1=b2b[:], op=ALU.add)
            return r
        sched.op("vector", a9)
        sched.bar()

        prev_store = STD.marker()

        def a9s(s, tiles=tiles, pv=prev_store):
            for i, tt in enumerate(tiles):
                STD.start(s, xn2_t[tt], U4[i][:])
            STD.fill(pv)
        sched.op("sync", a9s)
        sched.bar()

    cnt_gmax = LD.marker()

    def b4(s):
        LD.start(s, scal[:, 0:8], gmax_row)
        LD.fill(cnt_gmax)
    sched.op("sync", b4)
    sched.bar()

    def b5(v):
        v.wait_ge(ld_sem, cnt_gmax[0])
        VS.edge(v, v.tensor_reduce(out=scal[:, 0:1], in_=scal[:, 0:8], axis=AX.X,
                                   op=ALU.max))
        VS.edge(v, v.tensor_scalar(out=scal[:, 0:1], in0=scal[:, 0:1], scalar1=1e-8,
                                   scalar2=1.0 / 127.0, op0=ALU.max, op1=ALU.mult))
        VS.edge(v, v.reciprocal(out=scal[:, 1:2], in_=scal[:, 0:1]))
        return v.tensor_copy(out=scal[:, 16:17], in_=scal[:, 1:2])
    sched.op("vector", b5)
    sched.bar()

    l0wT_t = l0wT.rearrange("(c p) o -> c p o", p=P)
    for grp in range(4):
        cnt = LD.marker()

        def c0(s, grp=grp, cnt=cnt):
            for i in range(2):
                LD.start(s, W4[i][:], l0wT_t[grp * 2 + i])
            LD.fill(cnt)
        sched.op("sync", c0)
        sched.bar()

        def c1(sc, grp=grp, cnt=cnt):
            sc.wait_ge(ld_sem, cnt[0])
            r = None
            for i in range(2):
                r = sc.activation(out=U4[i][:], in_=W4[i][:], func=AF.Abs,
                                  accum_out=wsum[:, (grp * 2 + i) * 8: (grp * 2 + i) * 8 + 1])
            return r
        sched.op("scalar", c1)
        sched.bar()

    def c2(v):
        return v.tensor_reduce(out=wmsk[:, 0:1],
                               in_=wsum[:].rearrange("p (k e) -> p k e", e=8)[:, :, 0:1].squeeze(2),
                               axis=AX.X, op=ALU.add)
    sched.op("vector", c2)
    sched.bar()

    def c3(t):
        return t.transpose(psA[0:1, 0:P], wmsk[:, 0:1], idf[:])
    sched.op("tensor", c3)
    sched.bar()

    def c4(v):
        VS.edge(v, v.tensor_reduce(out=scal[:, 2:3], in_=psA[0:1, 0:P], axis=AX.X,
                                   op=ALU.add))
        VS.edge(v, v.tensor_scalar(out=scal[:, 2:3], in0=scal[:, 2:3],
                                   scalar1=0.7 / (D * D), scalar2=None, op0=ALU.mult))
        return v.tensor_copy(out=scal[:, 16:17], in_=scal[:, 2:3])
    sched.op("vector", c4)
    sched.bar()

    cnt_c5 = STD.marker()

    def c5(s):
        STD.start(s, sc_d[:, 0:1], scal[:, 2:3])
        STD.fill(cnt_c5)
    sched.op("sync", c5)
    sched.bar()
    cnt_delta = LD.marker()

    def c6(s):
        s.wait_ge(st_sem, cnt_c5[0])
        LD.start(s, scol[:, 2:3], sc_d[:, 0:1].squeeze(0).partition_broadcast(P))
        LD.fill(cnt_delta)
    sched.op("sync", c6)
    sched.bar()

    for grp in range(4):
        cnt = LD.marker()

        def c7(s, grp=grp, cnt=cnt):
            for i in range(2):
                LD.start(s, W4[i][:], l0wT_t[grp * 2 + i])
            LD.fill(cnt)
        sched.op("sync", c7)
        sched.bar()

        def c8(sc, cnt=cnt):
            sc.wait_ge(ld_sem, cnt[0])
            sc.wait_ge(ld_sem, cnt_delta[0])
            r = None
            for i in range(2):
                sc.activation(out=U4[i][:], in_=W4[i][:], func=AF.Abs)
                r = sc.activation(out=V4[i][:], in_=W4[i][:], func=AF.Sign)
            return r
        sched.op("scalar", c8)
        sched.bar()

        def c9(v, grp=grp):
            r = None
            for i in range(2):
                k = grp * 2 + i
                v.tensor_scalar(out=SG4[i][:], in0=U4[i][:], scalar1=scol[:, 2:3],
                                scalar2=None, op0=ALU.is_gt)
                r = v.scalar_tensor_tensor(out=W4[i][:], in0=U4[i][:], scalar=1.0,
                                           in1=SG4[i][:], op0=ALU.mult,
                                           op1=ALU.mult, accum_out=wsum[:, k * 8:k * 8 + 1])
            return r
        sched.op("vector", c9)
        sched.bar()

        def c10(sc, grp=grp):
            r = None
            for i in range(2):
                k = grp * 2 + i
                r = sc.activation(out=SG4[i][:], in_=SG4[i][:], func=AF.Copy,
                                  accum_out=wcnt[:, k * 8:k * 8 + 1])
            return r
        sched.op("scalar", c10)
        sched.bar()

        def c11(v, grp=grp):
            r = None
            for i in range(2):
                k = grp * 2 + i
                r = v.tensor_tensor(out=wqT[k][:], in0=V4[i][:], in1=SG4[i][:],
                                    op=ALU.mult)
            return r
        sched.op("vector", c11)
        sched.bar()

    def c12(v):
        VS.edge(v, v.tensor_reduce(out=wmsk[:, 0:1],
                                   in_=wsum[:].rearrange("p (k e) -> p k e", e=8)[:, :, 0:1].squeeze(2),
                                   axis=AX.X, op=ALU.add))
        VS.edge(v, v.tensor_reduce(out=wmsk[:, 8:9],
                                   in_=wcnt[:].rearrange("p (k e) -> p k e", e=8)[:, :, 0:1].squeeze(2),
                                   axis=AX.X, op=ALU.add))
        return v.tensor_copy(out=mkr[:, 0:1], in_=mkr[:, 0:1])
    sched.op("vector", c12)
    sched.bar()

    def c13(t):
        t.transpose(psA[0:1, 0:P], wmsk[:, 0:1], idf[:])
        return t.transpose(psB[0:1, 0:P], wmsk[:, 8:9], idf[:])
    sched.op("tensor", c13)
    sched.bar()

    def c14(v):
        VS.edge(v, v.tensor_reduce(out=scal[:, 2:3], in_=psA[0:1, 0:P], axis=AX.X,
                                   op=ALU.add))
        VS.edge(v, v.tensor_reduce(out=scal[:, 3:4], in_=psB[0:1, 0:P], axis=AX.X,
                                   op=ALU.add))
        VS.edge(v, v.tensor_scalar(out=scal[:, 3:4], in0=scal[:, 3:4], scalar1=1.0,
                                   scalar2=None, op0=ALU.max))
        VS.edge(v, v.reciprocal(out=scal[:, 3:4], in_=scal[:, 3:4]))
        VS.edge(v, v.tensor_tensor(out=scal[:, 3:4], in0=scal[:, 3:4],
                                   in1=scal[:, 2:3], op=ALU.mult))
        VS.edge(v, v.tensor_tensor(out=scal[:, 3:4], in0=scal[:, 3:4],
                                   in1=scal[:, 0:1], op=ALU.mult))
        return v.tensor_copy(out=scal[:, 16:17], in_=scal[:, 3:4])
    sched.op("vector", c14)
    sched.bar()

    cnt_c15 = STD.marker()

    def c15(s):
        STD.start(s, sc_d[:, 1:2], scal[:, 1:2])
        STD.start(s, sc_d[:, 2:3], scal[:, 3:4])
        STD.fill(cnt_c15)
    sched.op("sync", c15)
    sched.bar()
    cnt_scales = LD.marker()

    def c16(s):
        s.wait_ge(st_sem, cnt_c15[0])
        LD.start(s, scol[:, 0:1], sc_d[:, 1:2].squeeze(0).partition_broadcast(P))
        LD.start(s, scol[:, 1:2], sc_d[:, 2:3].squeeze(0).partition_broadcast(P))
        LD.fill(cnt_scales)
    sched.op("sync", c16)
    sched.bar()

    for g in range(8):
        tiles = [2 * g + i for i in range(2)]
        cnt = LD.marker()

        def d0(s, tiles=tiles, pv=prev_store, cnt=cnt):
            s.wait_ge(st_sem, pv[0])
            for i, tt in enumerate(tiles):
                LD.start(s, U4[i][:], xn2_t[tt])
                LD.start(s, W4[i][:], x_t[tt])
            LD.fill(cnt)
        sched.op("sync", d0)
        sched.bar()

        def d1(v, cnt=cnt):
            v.wait_ge(ld_sem, cnt[0])
            v.wait_ge(ld_sem, cnt_scales[0])
            r = None
            for i in range(2):
                v.tensor_scalar(out=U4[i][:], in0=U4[i][:], scalar1=scol[:, 0:1],
                                scalar2=127.0, op0=ALU.mult, op1=ALU.min)
                v.tensor_scalar(out=V4[i][:].bitcast(I32), in0=U4[i][:],
                                scalar1=-127.0, scalar2=None, op0=ALU.max)
                v.tensor_copy(out=U4[i][:], in_=V4[i][:].bitcast(I32))
                r = v.tensor_copy(out=XB4[i][:], in_=U4[i][:])
            return r
        sched.op("vector", d1)
        sched.bar()

        def d2(t):
            r = None
            for i in range(2):
                for dc in range(DC):
                    r = t.transpose(psAb[:, (i * DC + dc) * P:(i * DC + dc + 1) * P],
                                    XB4[i][:, dc * P:(dc + 1) * P], idb[:])
            return r
        sched.op("tensor", d2)
        sched.bar()

        def d3(v):
            r = None
            for i in range(2):
                for dc in range(DC):
                    r = v.tensor_copy(out=qT_t[dc][:, i * P:(i + 1) * P],
                                      in_=psAb[:, (i * DC + dc) * P:(i * DC + dc + 1) * P])
            return r
        sched.op("vector", d3)
        sched.bar()

        def d4(t):
            r = None
            for i in range(2):
                for half_o in range(2):
                    for dc in range(DC):
                        r = t.matmul(psB[:, (i * 2 + half_o) * 512:(i * 2 + half_o + 1) * 512],
                                     qT_t[dc][:, i * P:(i + 1) * P],
                                     wqT[dc][:, half_o * 512:(half_o + 1) * 512],
                                     start=(dc == 0), stop=(dc == DC - 1))
            return r
        sched.op("tensor", d4)
        sched.bar()

        def d5(v):
            r = None
            for i in range(2):
                r = v.scalar_tensor_tensor(
                    out=V4[i][:], in0=psB[:, i * 1024:(i + 1) * 1024],
                    scalar=scol[:, 1:2], in1=W4[i][:],
                    op0=ALU.mult, op1=ALU.add)
            return r
        sched.op("vector", d5)
        sched.bar()

        prev_store = STD.marker()

        def d5s(s, tiles=tiles, pv=prev_store):
            for i, tt in enumerate(tiles):
                STD.start(s, x1a_t[tt], V4[i][:])
            STD.fill(pv)
        sched.op("sync", d5s)
        sched.bar()

    def e1(t):
        r = None
        for four in range(4):
            for dc in range(DC):
                r = t.matmul(psA[0:NST, four * 512:(four + 1) * 512],
                             bwb[:, dc * NST:(dc + 1) * NST],
                             xnT[dc][:, four * 512:(four + 1) * 512],
                             start=(dc == 0), stop=(dc == DC - 1))
        for four in range(4):
            for dc in range(DC):
                r = t.matmul(psB[0:NST, four * 512:(four + 1) * 512],
                             cwb[:, dc * NST:(dc + 1) * NST],
                             xnT[dc][:, four * 512:(four + 1) * 512],
                             start=(dc == 0), stop=(dc == DC - 1))
        return r
    sched.op("tensor", e1)
    sched.bar()

    def e2(v):
        v.tensor_copy(out=bx_sb[:], in_=psA[0:NST, 0:T])
        return v.tensor_copy(out=cx_sb[:], in_=psB[0:NST, 0:T])
    sched.op("vector", e2)
    sched.bar()

    cnt_bxcx = STD.marker()

    def e3(s):
        STD.start(s, bxT_d, bx_sb[:])
        STD.start(s, cxT_d, cx_sb[:])
        STD.fill(cnt_bxcx)
    sched.op("sync", e3)
    sched.bar()

    NTB = T // TBW
    NS = NTB * NST
    fcnt = {0: LD.marker()}

    def fpre(s):
        s.wait_ge(st_sem, cnt_bxcx[0])
        LD.start(s, bc2[0][:], bxT_d[0:1, 0:TBW].squeeze(0).partition_broadcast(P))
        LD.start(s, cc2[0][:], cxT_d[0:1, 0:TBW].squeeze(0).partition_broadcast(P))
        LD.fill(fcnt[0])
    sched.op("sync", fpre)
    sched.bar()

    def f2body(t, n, buf):
        r = None
        for dc in range(DC):
            ps = psA if dc < 4 else psB
            r = t.matmul(ps[:, (dc % 4) * 512:(dc % 4 + 1) * 512],
                         idb[:], psc[buf][:, dc * TBW:(dc + 1) * TBW],
                         start=(n == 0), stop=(n == NST - 1))
        return r

    for st in range(NS):
        tb, n = st // NST, st % NST
        buf = st % 2

        if st + 1 < NS:
            tb1, n1 = (st + 1) // NST, (st + 1) % NST
            fcnt[st + 1] = LD.marker()

            def fld(s, tb1=tb1, n1=n1, b1=(st + 1) % 2, cnt=fcnt[st + 1]):
                LD.start(s, bc2[b1][:],
                         bxT_d[n1:n1 + 1, tb1 * TBW:(tb1 + 1) * TBW].squeeze(0).partition_broadcast(P))
                LD.start(s, cc2[b1][:],
                         cxT_d[n1:n1 + 1, tb1 * TBW:(tb1 + 1) * TBW].squeeze(0).partition_broadcast(P))
                LD.fill(cnt)
            sched.op("sync", fld)

        def f1(v, tb=tb, n=n, buf=buf, cnt=fcnt[st]):
            if n == 0 and tb > 0:
                for dc in range(DC):
                    ps = psA if dc < 4 else psB
                    v.tensor_copy(out=y_sb[dc][:, (tb - 1) * TBW:tb * TBW],
                                  in_=ps[:, (dc % 4) * 512:(dc % 4 + 1) * 512])
            v.wait_ge(ld_sem, cnt[0])
            r = None
            for dc in range(DC):
                col = n * DC + dc
                v.tensor_tensor(out=gsc[buf][:],
                                in0=xnT[dc][:, tb * TBW:(tb + 1) * TBW],
                                in1=bc2[buf][:], op=ALU.mult)
                VS.edge(v, v.tensor_tensor_scan(
                    out=hsc[buf][:],
                    data0=aexb[:, dc * NST + n: dc * NST + n + 1].broadcast_to([P, TBW]),
                    data1=gsc[buf][:],
                    initial=carry[:, col:col + 1],
                    op0=ALU.mult, op1=ALU.add))
                VS.edge(v, v.tensor_copy(out=carry[:, col:col + 1],
                                         in_=hsc[buf][:, TBW - 1:TBW]))
                r = v.tensor_tensor(out=psc[buf][:, dc * TBW:(dc + 1) * TBW],
                                    in0=hsc[buf][:],
                                    in1=cc2[buf][:], op=ALU.mult)
            return r
        sched.op("vector", f1)

        if n > 0:
            def f2(t, n=n - 1, buf=(st - 1) % 2):
                return f2body(t, n, buf)
            sched.op("tensor", f2)
        sched.bar()

        if n == NST - 1:
            def f2e(t, n=n, buf=buf):
                return f2body(t, n, buf)
            sched.op("tensor", f2e)
            sched.bar()

    def f3z(v):
        r = None
        for dc in range(DC):
            ps = psA if dc < 4 else psB
            r = v.tensor_copy(out=y_sb[dc][:, (NTB - 1) * TBW:NTB * TBW],
                              in_=ps[:, (dc % 4) * 512:(dc % 4 + 1) * 512])
        return r
    sched.op("vector", f3z)
    sched.bar()

    for pair in range(8):
        tiles = [2 * pair, 2 * pair + 1]
        cnt = LD.marker()

        def g0(s, tiles=tiles, pv=prev_store, cnt=cnt):
            s.wait_ge(st_sem, pv[0])
            for i, tt in enumerate(tiles):
                LD.start(s, W4[i][:], xn_t[tt])
                LD.start(s, U4[i][:], x1a_t[tt])
            LD.fill(cnt)
        sched.op("sync", g0)
        sched.bar()

        def g1(t, tiles=tiles):
            r = None
            for i, tt in enumerate(tiles):
                for half_o in range(2):
                    for dc in range(DC):
                        t.matmul(psB[:, i * 1024 + half_o * 512: i * 1024 + (half_o + 1) * 512],
                                 xnT[dc][:, tt * P:(tt + 1) * P],
                                 gwTb[dc][:, half_o * 512:(half_o + 1) * 512],
                                 start=(dc == 0), stop=False)
                    r = t.matmul(psB[:, i * 1024 + half_o * 512: i * 1024 + (half_o + 1) * 512],
                                 ones_r[0:1, :],
                                 gbb[0:1, half_o * 512:(half_o + 1) * 512],
                                 start=False, stop=True)
                for dc in range(DC):
                    r = t.transpose(psAb[:, i * 1024 + dc * P: i * 1024 + (dc + 1) * P],
                                    y_sb[dc][:, tt * P:(tt + 1) * P], idb[:])
            return r
        sched.op("tensor", g1)
        sched.bar()

        def g2(sc):
            r = None
            for i in range(2):
                r = sc.activation(out=SG4[i][:], in_=psB[:, i * 1024:(i + 1) * 1024],
                                  func=AF.Sigmoid)
            return r
        sched.op("scalar", g2)
        sched.bar()

        def g3(v, cnt=cnt):
            v.wait_ge(ld_sem, cnt[0])
            r = None
            for i in range(2):
                v.tensor_copy(out=YT2[i][:], in_=psAb[:, i * 1024:(i + 1) * 1024])
                v.tensor_tensor(out=V4[i][:], in0=W4[i][:], in1=dm1b[:], op=ALU.mult)
                v.tensor_tensor(out=V4[i][:], in0=V4[i][:],
                                in1=YT2[i][:], op=ALU.add)
                v.tensor_tensor(out=V4[i][:], in0=V4[i][:], in1=SG4[i][:], op=ALU.mult)
                v.tensor_tensor(out=V4[i][:], in0=V4[i][:], in1=W4[i][:], op=ALU.add)
                ln_stats_v(v, V4[i][:], ST4[i], VS)
            return v.tensor_copy(out=mkr[:, 0:1], in_=mkr[:, 0:1])
        sched.op("vector", g3)
        sched.bar()

        def g4(sc):
            r = None
            for i in range(2):
                r = ln_sqrt_s(sc, ST4[i])
            return r
        sched.op("scalar", g4)
        sched.bar()

        def g6(v):
            r = None
            for i in range(2):
                ln_coefs_v(v, ST4[i], VS)
                r = v.tensor_scalar(out=V4[i][:], in0=V4[i][:],
                                    scalar1=ST4[i][:, 40:41],
                                    scalar2=ST4[i][:, 48:49],
                                    op0=ALU.mult, op1=ALU.add)
            return r
        sched.op("vector", g6)
        sched.bar()

        def g7(v):
            r = None
            for i in range(2):
                v.tensor_tensor(out=V4[i][:], in0=V4[i][:], in1=lngb[:], op=ALU.mult)
                v.tensor_tensor(out=V4[i][:], in0=V4[i][:], in1=lnbb[:], op=ALU.add)
                r = v.tensor_tensor(out=V4[i][:], in0=V4[i][:], in1=U4[i][:], op=ALU.add)
            return r
        sched.op("vector", g7)
        sched.bar()

        prev_store = STD.marker()

        def g8(s, tiles=tiles, pv=prev_store):
            for i, tt in enumerate(tiles):
                STD.start(s, x1_t[tt], V4[i][:])
            STD.fill(pv)
        sched.op("sync", g8)
        sched.bar()

    if debug:
        def dumpT(s):
            STD.start(s, dbg_xnT, xnT[0][:])
            STD.start(s, dbg_y, y_sb[0][:])
            STD.start(s, dbg_sg, SG4[0][:])
        sched.op("sync", dumpT)
        sched.bar()

    def fin(g):
        g.wait_ge(st_sem, STD.count)
    sched.op("gpsimd", fin)

    streams = sched.build(ph)
    with nc.Block() as block:
        block.sync(streams["sync"])
        block.vector(streams["vector"])
        block.scalar(streams["scalar"])
        block.tensor(streams["tensor"])
        block.gpsimd(streams["gpsimd"])
    ctx.close()
    if fused:
        return {"x1": x1, "ident_f": ident_f, "ident_b": ident_b}
    return nc


def build_pmax(nc):
    x = nc.dram_tensor("x", [T, D], F32, kind="ExternalInput").ap()
    g2_row = nc.dram_tensor("g2_row", [1, D], F32, kind="ExternalInput").ap()
    b2_row = nc.dram_tensor("b2_row", [1, D], F32, kind="ExternalInput").ap()
    ident_f = nc.dram_tensor("ident_f", [P, P], F32, kind="ExternalInput").ap()
    pmax = nc.dram_tensor("pmax", [1, 1], F32, kind="ExternalOutput").ap()
    x_t = x.rearrange("(n p) d -> n p d", p=P)

    ctx = contextlib.ExitStack()
    sb = lambda name, shape, dt=F32: ctx.enter_context(nc.sbuf_tensor(name, shape, dt))
    W4 = [[sb(f"pw{pg}_{i}", [P, D]) for i in range(2)] for pg in range(2)]
    U4 = [[sb(f"pu{pg}_{i}", [P, D]) for i in range(2)] for pg in range(2)]
    ST4 = [[sb(f"pst{pg}_{i}", [P, 72]) for i in range(2)] for pg in range(2)]
    mark = sb("pmark", [P, 8])
    mkr = mark
    g2b = sb("pg2b", [P, D])
    b2b = sb("pb2b", [P, D])
    idf = sb("pidf", [P, P])
    amax = sb("pamax", [P, 16])
    scal = sb("pscal", [1, P])
    psA = ctx.enter_context(nc.psum_tensor("ppsA", [P, 512], F32))

    sched = Sched(nc)
    ld_sem = ctx.enter_context(nc.semaphore("LdP"))
    st_sem = ctx.enter_context(nc.semaphore("StP"))
    ph = ctx.enter_context(nc.semaphore("PhP"))
    vs_sem = ctx.enter_context(nc.semaphore("VsP"))
    LD = Dma(ld_sem)
    STD = Dma(st_sem)
    VS = VSync(vs_sem)

    sched.op("vector", lambda v: v.memset(mark[:, 0:1], 0.0))

    pcw = LD.marker()

    def pc(s):
        LD.start(s, g2b[:], g2_row.squeeze(0).partition_broadcast(P))
        LD.start(s, b2b[:], b2_row.squeeze(0).partition_broadcast(P))
        LD.start(s, idf[:], ident_f)
        LD.fill(pcw)
    sched.op("sync", pc)
    sched.bar()

    gcnt = {0: LD.marker()}

    def ppre(s):
        for i in range(2):
            LD.start(s, W4[0][i][:], x_t[i])
        LD.fill(gcnt[0])
    sched.op("sync", ppre)
    sched.bar()

    for g in range(9):
        pg = g % 2
        qg = (g - 1) % 2

        if g < 8:
            if g + 1 < 8:
                gcnt[g + 1] = LD.marker()

                def p0(s, g=g, cnt=gcnt[g + 1]):
                    for i, tt in enumerate([2 * (g + 1), 2 * (g + 1) + 1]):
                        LD.start(s, W4[(g + 1) % 2][i][:], x_t[tt])
                    LD.fill(cnt)
                sched.op("sync", p0)

            def p1(v, pg=pg, cnt=gcnt[g]):
                v.wait_ge(ld_sem, cnt[0])
                v.wait_ge(ld_sem, pcw[0])
                for i in range(2):
                    ln_stats_v(v, W4[pg][i][:], ST4[pg][i], VS)
                return v.tensor_copy(out=mkr[:, 0:1], in_=mkr[:, 0:1])
            sched.op("vector", p1)

        if g > 0:
            def p4(sc, qg=qg):
                r = None
                for i in range(2):
                    r = ln_sqrt_s(sc, ST4[qg][i])
                return r
            sched.op("scalar", p4)
        sched.bar()

        if g > 0:
            def p5(v, g=g - 1, qg=qg):
                for i in range(2):
                    ln_coefs_v(v, ST4[qg][i], VS)
                    v.tensor_scalar(out=U4[qg][i][:], in0=U4[qg][i][:],
                                    scalar1=ST4[qg][i][:, 40:41], scalar2=ST4[qg][i][:, 48:49],
                                    op0=ALU.mult, op1=ALU.add)
                    v.tensor_tensor(out=U4[qg][i][:], in0=U4[qg][i][:], in1=g2b[:], op=ALU.mult)
                    v.tensor_tensor(out=U4[qg][i][:], in0=U4[qg][i][:], in1=b2b[:], op=ALU.add)
                    VS.edge(v, v.tensor_reduce(out=amax[:, 8:9], in_=U4[qg][i][:], axis=AX.X,
                                               op=ALU.max, apply_absolute_value=True))
                    if g == 0 and i == 0:
                        VS.edge(v, v.tensor_copy(out=amax[:, 0:1], in_=amax[:, 8:9]))
                    else:
                        VS.edge(v, v.tensor_tensor(out=amax[:, 0:1], in0=amax[:, 0:1],
                                                   in1=amax[:, 8:9], op=ALU.max))
                return v.tensor_copy(out=mkr[:, 0:1], in_=mkr[:, 0:1])
            sched.op("vector", p5)

        if g < 8:
            def p2(sc, pg=pg):
                r = None
                for i in range(2):
                    r = ln_sqrt_s(sc, ST4[pg][i])
                return r
            sched.op("scalar", p2)
        sched.bar()

        if g < 8:
            def p3(v, pg=pg):
                for i in range(2):
                    ln_coefs_v(v, ST4[pg][i], VS)
                    v.tensor_scalar(out=U4[pg][i][:], in0=W4[pg][i][:],
                                    scalar1=ST4[pg][i][:, 40:41], scalar2=ST4[pg][i][:, 48:49],
                                    op0=ALU.mult, op1=ALU.add)
                    ln_stats_v(v, U4[pg][i][:], ST4[pg][i], VS)
                return v.tensor_copy(out=mkr[:, 0:1], in_=mkr[:, 0:1])
            sched.op("vector", p3)
            sched.bar()

    def pf0(t):
        return t.transpose(psA[0:1, 0:P], amax[:, 0:1], idf[:])
    sched.op("tensor", pf0)
    sched.bar()

    def pf1(v):
        return v.tensor_reduce(out=scal[:, 0:1], in_=psA[0:1, 0:P], axis=AX.X,
                               op=ALU.max)
    sched.op("vector", pf1)
    sched.bar()

    def pf2(s):
        STD.start(s, pmax, scal[:, 0:1])
    sched.op("sync", pf2)
    sched.bar()

    def pfin(g):
        g.wait_ge(st_sem, STD.count)
    sched.op("gpsimd", pfin)

    streams = sched.build(ph)
    with nc.Block() as block:
        block.sync(streams["sync"])
        block.vector(streams["vector"])
        block.scalar(streams["scalar"])
        block.tensor(streams["tensor"])
        block.gpsimd(streams["gpsimd"])
    ctx.close()
    return nc


def host_prep_l0(inputs, n_cores=8):
    f32 = np.float32
    x = np.ascontiguousarray(np.asarray(inputs["L0_attn_x"], f32))
    A_log = np.asarray(inputs["L0_attn_A_log"], f32)
    B_w = np.asarray(inputs["L0_attn_B_w"], f32)
    C_w = np.asarray(inputs["L0_attn_C_w"], f32)
    Dv = np.asarray(inputs["L0_attn_D"], f32)
    gate_w = np.asarray(inputs["L0_attn_gate_w"], f32)
    gate_b = np.asarray(inputs["L0_attn_gate_b"], f32)
    ln_g = np.asarray(inputs["L0_attn_ln_gamma"], f32)
    ln_b = np.asarray(inputs["L0_attn_ln_beta"], f32)
    w0 = np.asarray(inputs["L0_ffn_weight"], f32)
    g2 = np.asarray(inputs["L0_ffn_gamma"], f32)
    b2 = np.asarray(inputs["L0_ffn_beta"], f32)

    alog_r = np.ascontiguousarray(
        A_log.reshape(DC, P, NST).transpose(1, 0, 2).reshape(P, P))
    shared = {
        "alog_r": alog_r,
        "bwT": np.ascontiguousarray(B_w.T),
        "cwT": np.ascontiguousarray(C_w.T),
        "d_row": np.ascontiguousarray(Dv[None, :]),
        "gate_wT": np.ascontiguousarray(gate_w.T),
        "gate_b_row": np.ascontiguousarray(gate_b[None, :]),
        "ln_g_row": np.ascontiguousarray(ln_g[None, :]),
        "ln_b_row": np.ascontiguousarray(ln_b[None, :]),
        "l0w_T": np.ascontiguousarray(w0.T),
        "g2_row": np.ascontiguousarray(g2[None, :]),
        "b2_row": np.ascontiguousarray(b2[None, :]),
        "ident_f": np.eye(P, dtype=f32),
        "ident_b": np.eye(P, dtype=BF16NP),
        "ones_row": np.ones((1, P), dtype=f32),
    }
    maps = []
    for c in range(n_cores):
        m = dict(shared)
        m["x"] = np.ascontiguousarray(x[c])
        maps.append(m)
    return maps


def run_pmax(inputs, n_cores=8):
    f32 = np.float32
    x = np.asarray(inputs["L0_attn_x"], f32)
    nc = bass.Bass("TRN2", target_bir_lowering=False,
                   detect_race_conditions=False)
    build_pmax(nc)
    shared = {
        "g2_row": np.ascontiguousarray(np.asarray(inputs["L0_ffn_gamma"], f32)[None, :]),
        "b2_row": np.ascontiguousarray(np.asarray(inputs["L0_ffn_beta"], f32)[None, :]),
        "ident_f": np.eye(P, dtype=f32),
    }
    maps = [dict(shared, x=np.ascontiguousarray(x[c])) for c in range(n_cores)]
    res = run_bass_kernel_spmd(nc, maps, core_ids=list(range(n_cores)))
    pm = np.array([[float(np.asarray(r["pmax"]).ravel()[0]) for r in res.results]],
                  np.float32)
    return pm


def run_l0(inputs, n_cores=8, debug=False):
    gmax_row = run_pmax(inputs, n_cores)
    nc = bass.Bass("TRN2", target_bir_lowering=False,
                   detect_race_conditions=False)
    build_l0(nc, n_cores=n_cores, debug=debug)
    maps = host_prep_l0(inputs, n_cores)
    for m in maps:
        m["gmax_row"] = gmax_row
    res = run_bass_kernel_spmd(nc, maps, core_ids=list(range(n_cores)))
    if debug:
        return res
    return np.stack([np.asarray(r["x1"]) for r in res.results], 0)


def kernel(**inputs):
    x1 = run_l0(inputs, n_cores=8)
    return x1.astype(np.float32)


HEADS = 16
DH = 64
LATD = 512
LHEADS = 8
NLAT = 64
WIN = 256
INNER = 4096
IC = INNER // P


def build_l1(nc, debug=False, shared=None):

    def din(name, shape, dt=F32):
        return nc.dram_tensor(name, shape, dt, kind="ExternalInput").ap()

    dsc = ((lambda n, s, dt=F32: nc.dram_tensor(n, s, dt, kind="ExternalOutput").ap())
           if debug else
           (lambda n, s, dt=F32: nc.dram_tensor(n, s, dt, kind="Internal").ap()))

    if shared is None:
        x = din("x", [T, D])
    else:
        x = shared["x1"]
    out = nc.dram_tensor("out", [T, D], F32, kind="ExternalOutput").ap()
    wq_d = din("wq", [D, D])
    wk_d = din("wk", [D, D])
    wv_d = din("wv", [D, D])
    wgq_d = din("wgq", [D, LATD])
    wgk_d = din("wgk", [D, LATD])
    wgv_d = din("wgv", [D, LATD])
    rout_d = din("r_out_w", [LATD, D])
    wout_d = din("w_out", [D, D])
    bdgl_d = din("bd_glw", [D, HEADS])
    bdgr_d = din("bd_grw", [D, HEADS])
    wgT_d = din("w_gT", [D, INNER])
    wuT_d = din("w_uT", [D, INNER])
    wdT_d = din("w_dT", [INNER, D])
    if shared is None:
        ident_f = din("ident_f", [P, P])
        ident_b = din("ident_b", [P, P], BF16)
    else:
        ident_f = shared["ident_f"]
        ident_b = shared["ident_b"]
    poolm_d = din("poolm", [T, NLAT], BF16)
    masks_d = din("masks", [4, P, 512])
    lmask_d = din("lmask", [T, NLAT])

    h_sp = dsc("h_sp", [T, INNER])
    x2a_sp = dsc("x2a_sp", [T, D])
    wdq_sp = dsc("wdq_sp", [INNER, D], BF16)
    if debug:
        dbg_hqT = nc.dram_tensor("dbg_hqT", [P, T], BF16, kind="ExternalOutput").ap()
        dbg_rh = nc.dram_tensor("dbg_rh", [P, NT * 8], F32, kind="ExternalOutput").ap()

    x_t = x.rearrange("(n p) d -> n p d", p=P)
    out_t = out.rearrange("(n p) d -> n p d", p=P)
    h_t = h_sp.rearrange("(n p) d -> n p d", p=P)
    x2a_t = x2a_sp.rearrange("(n p) d -> n p d", p=P)
    lmask_t = lmask_d.rearrange("(n p) l -> n p l", p=P)
    poolm_t = poolm_d.rearrange("(n p) l -> n p l", p=P)

    ctx = contextlib.ExitStack()
    sb = lambda name, shape, dt=F32: ctx.enter_context(nc.sbuf_tensor(name, shape, dt))

    xnT = [sb(f"l1xnT{c}", [P, T], BF16) for c in range(DC)]
    VTK = [sb(f"l1v{c}", [P, T], BF16) for c in range(DC)]
    KQT = [sb(f"l1kq{c}", [P, T], BF16) for c in range(DC)]
    QT = [sb(f"l1qt{c}", [P, T], BF16) for c in range(DC)]
    GQT = [sb(f"l1gqt{c}", [P, T], BF16) for c in range(4)]
    LPT = [sb(f"l1lpt{c}", [P, T], BF16) for c in range(4)]
    WB = [sb(f"l1wb{c}", [P, D], BF16) for c in range(DC)]
    W2 = [sb(f"l1w2_{i}", [P, D]) for i in range(2)]
    U2 = [sb(f"l1u2_{i}", [P, D]) for i in range(2)]
    _v2s = sb("l1v2s", [P, D])
    V2 = [_v2s, _v2s]
    XB2 = [sb(f"l1xb{i}", [P, D], BF16) for i in range(2)]
    HB2 = [sb(f"l1hb{i}", [P, D], BF16) for i in range(2)]
    ST2 = [sb(f"l1st{i}", [P, 72]) for i in range(2)]
    mark = sb("l1mark", [P, 48])
    mkr = sb("l1mkr", [P, 8])
    idf = sb("l1idf", [P, P])
    idb = sb("l1idb", [P, P], BF16)
    RS = sb("l1rs", [P, NT * 8])
    RSQ = sb("l1rsq", [P, NT * 8])
    RH = sb("l1rh", [P, NT * 8])
    RHQ = sb("l1rhq", [P, NT * 8])
    CGU = sb("l1cgu", [P, NT * 8])
    scw = sb("l1scw", [1, 32])
    scwb = sb("l1scwb", [P, 8])
    sc_d2 = dsc("l1sc_d", [1, 8])
    wacc = sb("l1wacc", [P, IC * 8])
    pooledT = [sb(f"l1plT{c}", [P, NLAT], BF16) for c in range(DC)]
    GKT = [sb(f"l1gkt{c}", [P, NLAT], BF16) for c in range(4)]
    GV = sb("l1gv", [P, LATD], BF16)

    LPS = sb("l1lps", [NLAT, 512], BF16)
    glt = [sb(f"l1glt{i}", [P, 2 * HEADS]) for i in range(NT)]
    rc = sb("l1rc", [P, HEADS * 8])
    glr = sb("l1glr", [P, HEADS])
    _msk = sb("l1msk", [P, 512])
    msk = [_msk, _msk]
    _lm = sb("l1lmsk", [P, NLAT])
    lmsk2 = [_lm, _lm]
    pmT2 = [sb(f"l1pmT{i}", [P, NLAT], BF16) for i in range(2)]
    sS = sb("l1sS", [P, 512])
    pS = sb("l1pS", [P, 512], BF16)

    glsb = sb("l1glsb", [HEADS, P])

    def PTv(h):
        return xnT[h // 4][:, (h % 4) * 512:(h % 4) * 512 + 512]

    def rem2v(qc):
        return xnT[6][:, qc * D:(qc + 1) * D]

    def combv(qc):
        return xnT[7][:, qc * D:(qc + 1) * D]

    def cTv(qc, dc):
        return xnT[4][:, (qc * DC + dc) * P:(qc * DC + dc + 1) * P]

    def vtile(tt):
        return VTK[tt // 2][:, (tt % 2) * D:(tt % 2 + 1) * D]

    def hqTv(k, set4):
        return QT[k // 8][:, (k % 8) * 256:(k % 8) * 256 + 256]

    psA = ctx.enter_context(nc.psum_tensor("l1psA", [P, T], F32))
    psB = ctx.enter_context(nc.psum_tensor("l1psB", [P, T], F32))
    psAb = psA[:].bitcast(BF16)
    psBb = psB[:].bitcast(BF16)

    sched = Sched(nc)
    ld_sem = ctx.enter_context(nc.semaphore("LdL1"))
    st_sem = ctx.enter_context(nc.semaphore("StL1"))
    ph = ctx.enter_context(nc.semaphore("PhL1"))
    vs_sem = ctx.enter_context(nc.semaphore("VsL1"))
    LD = Dma(ld_sem)
    STD = Dma(st_sem)
    VS = VSync(vs_sem)

    sched.op("vector", lambda v: v.memset(mkr[:, 0:1], 0.0))

    cw = LD.marker()

    def l1c0(s):
        LD.start(s, idf[:], ident_f)
        LD.start(s, idb[:], ident_b)
        LD.fill(cw)
    sched.op("sync", l1c0)
    sched.bar()

    for g in range(8):
        tiles = [2 * g, 2 * g + 1]
        cnt = LD.marker()

        def a0(s, tiles=tiles, cnt=cnt, g=g):
            for i, tt in enumerate(tiles):
                LD.start(s, W2[i][:], x_t[tt])
                LD.start(s, pmT2[i][:], poolm_t[tt])
            LD.fill(cnt)
        sched.op("sync", a0)
        sched.bar()

        def a1(v, cnt=cnt):
            v.wait_ge(ld_sem, cnt[0])
            for i in range(2):
                ln_stats_v(v, W2[i][:], ST2[i], VS)
            return v.tensor_copy(out=mkr[:, 0:1], in_=mkr[:, 0:1])
        sched.op("vector", a1)
        sched.bar()

        def a2(sc):
            r = None
            for i in range(2):
                r = ln_sqrt_s(sc, ST2[i])
            return r
        sched.op("scalar", a2)
        sched.bar()

        def a4(v, tiles=tiles):
            r = None
            for i, tt in enumerate(tiles):
                ln_coefs_v(v, ST2[i], VS)
                v.tensor_scalar(out=U2[i][:], in0=W2[i][:],
                                scalar1=ST2[i][:, 40:41], scalar2=ST2[i][:, 48:49],
                                op0=ALU.mult, op1=ALU.add)
                v.tensor_copy(out=XB2[i][:], in_=U2[i][:])
                VS.edge(v, v.tensor_reduce(out=RS[:, tt * 8:tt * 8 + 1], in_=U2[i][:],
                                           axis=AX.X, op=ALU.max,
                                           apply_absolute_value=True))
                VS.edge(v, v.tensor_scalar(out=RS[:, tt * 8:tt * 8 + 1],
                                           in0=RS[:, tt * 8:tt * 8 + 1],
                                           scalar1=1e-5, scalar2=1.0 / 127.0,
                                           op0=ALU.max, op1=ALU.mult))
                VS.edge(v, v.reciprocal(out=RSQ[:, tt * 8:tt * 8 + 1],
                                        in_=RS[:, tt * 8:tt * 8 + 1]))
            return v.tensor_copy(out=mkr[:, 0:1], in_=mkr[:, 0:1])
        sched.op("vector", a4)
        sched.bar()

        def a5(v, tiles=tiles):
            r = None
            for i, tt in enumerate(tiles):
                v.tensor_scalar(out=V2[i][:], in0=U2[i][:],
                                scalar1=RSQ[:, tt * 8:tt * 8 + 1], scalar2=127.0,
                                op0=ALU.mult, op1=ALU.min)
                v.tensor_scalar(out=U2[i][:].bitcast(I32), in0=V2[i][:],
                                scalar1=-128.0, scalar2=None, op0=ALU.max)
                v.tensor_copy(out=V2[i][:], in_=U2[i][:].bitcast(I32))
                r = v.tensor_copy(out=W2[i][:].bitcast(BF16)[:, 0:D], in_=V2[i][:])
            return r
        sched.op("vector", a5)

        def a5p(t, tiles=tiles):
            r = None
            for i, tt in enumerate(tiles):
                for half in range(2):
                    r = t.matmul(psB[0:NLAT, half * 512:(half + 1) * 512],
                                 pmT2[i][:], XB2[i][:, half * 512:(half + 1) * 512],
                                 start=(tt == 0), stop=(tt == T // P - 1))
            return r
        sched.op("tensor", a5p)
        sched.bar()

        def a8t(t):
            r = None
            for i in range(2):
                for dc in range(DC):
                    r = t.transpose(psAb[:, (i * DC + dc) * P:(i * DC + dc + 1) * P],
                                    XB2[i][:, dc * P:(dc + 1) * P], idb[:])
            return r
        sched.op("tensor", a8t)
        sched.bar()

        def a9e(v, tiles=tiles):
            r = None
            for i, tt in enumerate(tiles):
                for dc in range(DC):
                    r = v.tensor_copy(out=xnT[dc][:, tt * P:(tt + 1) * P],
                                      in_=psAb[:, (i * DC + dc) * P:(i * DC + dc + 1) * P])
            return r
        sched.op("vector", a9e)
        sched.bar()

        def a10t(t):
            r = None
            for i in range(2):
                for dc in range(DC):
                    r = t.transpose(psAb[:, (i * DC + dc) * P:(i * DC + dc + 1) * P],
                                    W2[i][:].bitcast(BF16)[:, dc * P:(dc + 1) * P], idb[:])
            return r
        sched.op("tensor", a10t)
        sched.bar()

        def a11e(v, tiles=tiles):
            r = None
            for i, tt in enumerate(tiles):
                for dc in range(DC):
                    r = v.tensor_copy(out=KQT[dc][:, tt * P:(tt + 1) * P],
                                      in_=psAb[:, (i * DC + dc) * P:(i * DC + dc + 1) * P])
            return r
        sched.op("vector", a11e)
        sched.bar()

    def b0(v):
        return v.tensor_copy(out=W2[0][:].bitcast(BF16)[0:NLAT, 0:D],
                             in_=psB[0:NLAT, 0:D])
    sched.op("vector", b0)
    sched.bar()

    def b1(t):
        r = None
        for dc in range(DC):
            r = t.transpose(psAb[:, dc * NLAT:(dc + 1) * NLAT],
                            W2[0][:].bitcast(BF16)[0:NLAT, dc * P:(dc + 1) * P],
                            idb[0:NLAT, 0:NLAT])
        return r
    sched.op("tensor", b1)
    sched.bar()

    def b2(v):
        r = None
        for dc in range(DC):
            r = v.tensor_copy(out=pooledT[dc][:], in_=psAb[:, dc * NLAT:(dc + 1) * NLAT])
        return r
    sched.op("vector", b2)
    sched.bar()

    def stage_weight(wten, ncols, tag):
        for grp in range(4):
            cnt = LD.marker()

            def wl(s, grp=grp, cnt=cnt, wten=wten, ncols=ncols):
                for i in range(2):
                    LD.start(s, W2[i][:, 0:ncols],
                             wten.rearrange("(c p) o -> c p o", p=P)[grp * 2 + i])
                LD.fill(cnt)
            sched.op("sync", wl)
            sched.bar()

            def wc(v, grp=grp, cnt=cnt, ncols=ncols):
                v.wait_ge(ld_sem, cnt[0])
                r = None
                for i in range(2):
                    r = v.tensor_copy(out=WB[grp * 2 + i][:, 0:ncols],
                                      in_=W2[i][:, 0:ncols])
                return r
            sched.op("vector", wc)
            sched.bar()

    stage_weight(wgk_d, LATD, "wgk")

    def gk(t):
        r = None
        for mc in range(4):
            for dc in range(DC):
                r = t.matmul(psA[:, mc * NLAT:(mc + 1) * NLAT],
                             WB[dc][:, mc * P:(mc + 1) * P], pooledT[dc][:],
                             start=(dc == 0), stop=(dc == DC - 1))
        return r
    sched.op("tensor", gk)
    sched.bar()

    def gke(v):
        r = None
        for mc in range(4):
            r = v.tensor_copy(out=GKT[mc][:], in_=psA[:, mc * NLAT:(mc + 1) * NLAT])
        return r
    sched.op("vector", gke)
    sched.bar()

    stage_weight(wgv_d, LATD, "wgv")

    def gvm(t):
        r = None
        for dc in range(DC):
            r = t.matmul(psA[0:NLAT, 0:LATD], pooledT[dc][:], WB[dc][:, 0:LATD],
                         start=(dc == 0), stop=(dc == DC - 1))
        return r
    sched.op("tensor", gvm)
    sched.bar()

    def gve(v):
        return v.tensor_copy(out=GV[0:NLAT, :], in_=psA[0:NLAT, 0:LATD])
    sched.op("vector", gve)
    sched.bar()

    gvd_mark = STD.marker()

    def gvd(s, pv=gvd_mark):
        STD.start(s, GV[64:128, :], GV[0:NLAT, :])
        STD.fill(pv)
    sched.op("sync", gvd)
    sched.bar()

    def wabs_wide(wten, widx):
        for grp in range(4):
            for half in range(2):
                cnt = LD.marker()

                def wl3(s, grp=grp, half=half, cnt=cnt, wten=wten):
                    for i in range(2):
                        for seg in range(2):
                            LD.start(s, (W2 if seg == 0 else U2)[i][:, 0:1024],
                                     wten.rearrange("(c p) o -> c p o", p=P)[grp * 2 + i][:, (half * 2 + seg) * 1024:(half * 2 + seg + 1) * 1024])
                    LD.fill(cnt)
                sched.op("sync", wl3)
                sched.bar()

                def wa3(sc, grp=grp, half=half, cnt=cnt, widx=widx):
                    sc.wait_ge(ld_sem, cnt[0])
                    r = None
                    for i in range(2):
                        k = grp * 2 + i
                        r = sc.activation(out=W2[i][:, 0:1024], in_=W2[i][:, 0:1024],
                                          func=AF.Abs,
                                          accum_out=wacc[:, (k * 4 + half * 2) * 8:(k * 4 + half * 2) * 8 + 1])
                        r = sc.activation(out=U2[i][:, 0:1024], in_=U2[i][:, 0:1024],
                                          func=AF.Abs,
                                          accum_out=wacc[:, (k * 4 + half * 2 + 1) * 8:(k * 4 + half * 2 + 1) * 8 + 1])
                    return r
                sched.op("scalar", wa3)
                sched.bar()

        def wred(v, widx=widx):
            VS.edge(v, v.tensor_reduce(
                out=mark[:, 8:9],
                in_=wacc[:].rearrange("p (k e) -> p k e", e=8)[:, :, 0:1].squeeze(2),
                axis=AX.X, op=ALU.add))
            return v.tensor_copy(out=mkr[:, 0:1], in_=mkr[:, 0:1])
        sched.op("vector", wred)
        sched.bar()

        def wtr(t, widx=widx):
            return t.transpose(psA[0:1, 0:P], mark[:, 8:9], idf[:])
        sched.op("tensor", wtr)
        sched.bar()

        def wfin(v, widx=widx):
            VS.edge(v, v.tensor_reduce(out=scw[:, widx * 8:widx * 8 + 1],
                                       in_=psA[0:1, 0:P], axis=AX.X, op=ALU.add))
            VS.edge(v, v.tensor_scalar(out=scw[:, widx * 8:widx * 8 + 1],
                                       in0=scw[:, widx * 8:widx * 8 + 1],
                                       scalar1=1.0 / (D * INNER), scalar2=1e-5,
                                       op0=ALU.mult, op1=ALU.max))
            VS.edge(v, v.reciprocal(out=scw[:, widx * 8 + 1:widx * 8 + 2],
                                    in_=scw[:, widx * 8:widx * 8 + 1]))
            return v.tensor_copy(out=mkr[:, 0:1], in_=mkr[:, 0:1])
        sched.op("vector", wfin)
        sched.bar()

    wabs_wide(wgT_d, 0)
    wabs_wide(wuT_d, 1)

    for grp in range(16):
        cnt = LD.marker()

        def wl4(s, grp=grp, cnt=cnt):
            for i in range(2):
                LD.start(s, W2[i][:, 0:1024],
                         wdT_d.rearrange("(c p) o -> c p o", p=P)[grp * 2 + i])
            LD.fill(cnt)
        sched.op("sync", wl4)
        sched.bar()

        def wa4(sc, grp=grp, cnt=cnt):
            sc.wait_ge(ld_sem, cnt[0])
            r = None
            for i in range(2):
                k = grp * 2 + i
                r = sc.activation(out=W2[i][:, 0:1024], in_=W2[i][:, 0:1024],
                                  func=AF.Abs, accum_out=wacc[:, k * 8:k * 8 + 1])
            return r
        sched.op("scalar", wa4)
        sched.bar()

    def wred4(v):
        VS.edge(v, v.tensor_reduce(
            out=mark[:, 8:9],
            in_=wacc[:].rearrange("p (k e) -> p k e", e=8)[:, :, 0:1].squeeze(2),
            axis=AX.X, op=ALU.add))
        return v.tensor_copy(out=mkr[:, 0:1], in_=mkr[:, 0:1])
    sched.op("vector", wred4)
    sched.bar()

    def wtr4(t):
        return t.transpose(psA[0:1, 0:P], mark[:, 8:9], idf[:])
    sched.op("tensor", wtr4)
    sched.bar()

    def wfin4(v):
        VS.edge(v, v.tensor_reduce(out=scw[:, 16:17], in_=psA[0:1, 0:P],
                                   axis=AX.X, op=ALU.add))
        VS.edge(v, v.tensor_scalar(out=scw[:, 16:17], in0=scw[:, 16:17],
                                   scalar1=1.0 / (D * INNER), scalar2=1e-5,
                                   op0=ALU.mult, op1=ALU.max))
        VS.edge(v, v.reciprocal(out=scw[:, 17:18], in_=scw[:, 16:17]))
        return v.tensor_copy(out=mkr[:, 0:1], in_=mkr[:, 0:1])
    sched.op("vector", wfin4)
    sched.bar()

    cnt_swst = STD.marker()

    def swst(s):
        STD.start(s, sc_d2[:, 0:1], scw[:, 1:2])
        STD.start(s, sc_d2[:, 1:2], scw[:, 9:10])
        STD.start(s, sc_d2[:, 2:3], scw[:, 17:18])
        STD.start(s, sc_d2[:, 3:4], scw[:, 0:1])
        STD.start(s, sc_d2[:, 4:5], scw[:, 8:9])
        STD.start(s, sc_d2[:, 5:6], scw[:, 16:17])
        STD.fill(cnt_swst)
    sched.op("sync", swst)
    sched.bar()
    cnt_swld = LD.marker()

    def swld(s):
        s.wait_ge(st_sem, cnt_swst[0])
        for j in range(6):
            LD.start(s, scwb[:, j:j + 1], sc_d2[:, j:j + 1].squeeze(0).partition_broadcast(P))
        LD.fill(cnt_swld)
    sched.op("sync", swld)
    sched.bar()

    def cgu0(v):
        v.wait_ge(ld_sem, cnt_swld[0])
        r = None
        for tt in range(NT):
            r = VS.edge(v, v.tensor_tensor(out=CGU[:, tt * 8:tt * 8 + 1],
                                           in0=RS[:, tt * 8:tt * 8 + 1],
                                           in1=scwb[:, 3:4], op=ALU.mult))
            r = VS.edge(v, v.tensor_tensor(out=CGU[:, tt * 8 + 1:tt * 8 + 2],
                                           in0=RS[:, tt * 8:tt * 8 + 1],
                                           in1=scwb[:, 4:5], op=ALU.mult))
        return v.tensor_copy(out=mkr[:, 0:1], in_=mkr[:, 0:1])
    sched.op("vector", cgu0)
    sched.bar()

    c2_mark = STD.marker()
    for oc in range(8):
        chunks = [(wsel, wten, grp) for wsel, wten in ((0, wgT_d), (1, wuT_d))
                  for grp in range(4)]
        ccnt = {}
        for c in range(len(chunks) + 1):
            if c < len(chunks):
                wsel, wten, grp = chunks[c]
                ccnt[c] = LD.marker()

                def c2l(s, grp=grp, oc=oc, wten=wten, cnt=ccnt[c], pc=c % 2):
                    for i in range(2):
                        LD.start(s, W2[i][:, pc * 512:pc * 512 + 512],
                                 wten.rearrange("(c p) o -> c p o", p=P)[grp * 2 + i][:, oc * 512:(oc + 1) * 512])
                    LD.fill(cnt)
                sched.op("sync", c2l)

            if c > 0:
                wsel, wten, grp = chunks[c - 1]

                def c2q(v, grp=grp, wsel=wsel, cnt=ccnt[c - 1], pv=c2_mark,
                        pc=(c - 1) % 2):
                    v.wait_ge(ld_sem, cnt[0])
                    v.wait_ge(st_sem, pv[0])
                    r = None
                    for i in range(2):
                        v.tensor_scalar(out=U2[i][:, 0:512],
                                        in0=W2[i][:, pc * 512:pc * 512 + 512],
                                        scalar1=scwb[:, wsel:wsel + 1], scalar2=1.49,
                                        op0=ALU.mult, op1=ALU.min)
                        v.tensor_scalar(out=V2[i][:].bitcast(I32)[:, 0:512],
                                        in0=U2[i][:, 0:512],
                                        scalar1=-1.49, scalar2=None, op0=ALU.max)
                        v.tensor_copy(out=U2[i][:, 0:512], in_=V2[i][:].bitcast(I32)[:, 0:512])
                        r = v.tensor_copy(out=WB[grp * 2 + i][:, wsel * 512:(wsel + 1) * 512],
                                          in_=U2[i][:, 0:512])
                    return r
                sched.op("vector", c2q)
            sched.bar()

        def c2dst(tt):
            j = tt % 4
            if j < 2:
                return U2[j][:, 0:512]
            return _v2s[:, (j - 2) * 512:(j - 1) * 512]

        for s in range(NT + 3):
            if s < NT:
                def c2m(t, tt=s, oc=oc):
                    r = None
                    b = (tt % 4) * 512
                    for dc in range(DC):
                        r = t.matmul(psA[:, b:b + 512],
                                     KQT[dc][:, tt * P:(tt + 1) * P],
                                     WB[dc][:, 0:512],
                                     start=(dc == 0), stop=(dc == DC - 1))
                    for dc in range(DC):
                        r = t.matmul(psB[:, b:b + 512],
                                     KQT[dc][:, tt * P:(tt + 1) * P],
                                     WB[dc][:, 512:1024],
                                     start=(dc == 0), stop=(dc == DC - 1))
                    return r
                sched.op("tensor", c2m)

            if 0 <= s - 1 < NT:
                def c2s(sc, tt=s - 1, pv=c2_mark):
                    sc.wait_ge(st_sem, pv[0])
                    b = (tt % 4) * 512
                    return sc.activation(out=c2dst(tt), in_=psA[:, b:b + 512],
                                         func=AF.Silu, scale=CGU[:, tt * 8:tt * 8 + 1])
                sched.op("scalar", c2s)

            if 0 <= s - 2 < NT:
                def c2h(v, tt=s - 2, oc=oc):
                    b = (tt % 4) * 512
                    dst = c2dst(tt)
                    v.tensor_scalar(out=W2[tt % 2][:, 0:512], in0=psA[:, b:b + 512] if False else psB[:, b:b + 512],
                                    scalar1=CGU[:, tt * 8 + 1:tt * 8 + 2], scalar2=None,
                                    op0=ALU.mult)
                    v.tensor_tensor(out=dst, in0=dst,
                                    in1=W2[tt % 2][:, 0:512], op=ALU.mult)
                    VS.edge(v, v.tensor_reduce(out=mark[:, 16:17], in_=dst,
                                               axis=AX.X, op=ALU.max,
                                               apply_absolute_value=True))
                    if oc == 0:
                        VS.edge(v, v.tensor_copy(out=RH[:, tt * 8:tt * 8 + 1],
                                                 in_=mark[:, 16:17]))
                    else:
                        VS.edge(v, v.tensor_tensor(out=RH[:, tt * 8:tt * 8 + 1],
                                                   in0=RH[:, tt * 8:tt * 8 + 1],
                                                   in1=mark[:, 16:17], op=ALU.max))
                    return v.tensor_copy(out=mkr[:, 0:1], in_=mkr[:, 0:1])
                sched.op("vector", c2h)

            if 0 <= s - 3 < NT:
                c2_mark = STD.marker()

                def c2st(st_, tt=s - 3, oc=oc, pv=c2_mark):
                    STD.start(st_, h_t[tt][:, oc * 512:(oc + 1) * 512], c2dst(tt))
                    STD.fill(pv)
                sched.op("sync", c2st)
            sched.bar()

    def c2f(v):
        r = None
        for tt in range(NT):
            VS.edge(v, v.tensor_scalar(out=RH[:, tt * 8:tt * 8 + 1],
                                       in0=RH[:, tt * 8:tt * 8 + 1],
                                       scalar1=1e-5, scalar2=1.0 / 127.0,
                                       op0=ALU.max, op1=ALU.mult))
            VS.edge(v, v.reciprocal(out=RHQ[:, tt * 8:tt * 8 + 1],
                                    in_=RH[:, tt * 8:tt * 8 + 1]))
            r = VS.edge(v, v.tensor_tensor(out=RH[:, tt * 8 + 1:tt * 8 + 2],
                                           in0=RH[:, tt * 8:tt * 8 + 1],
                                           in1=scwb[:, 5:6], op=ALU.mult))
        return v.tensor_copy(out=mkr[:, 0:1], in_=mkr[:, 0:1])
    sched.op("vector", c2f)
    sched.bar()

    c3_mark = STD.marker()
    for grp in range(16):
        cnt = LD.marker()

        def c3l(s, grp=grp, cnt=cnt):
            for i in range(2):
                LD.start(s, W2[i][:, 0:1024],
                         wdT_d.rearrange("(c p) o -> c p o", p=P)[grp * 2 + i])
            LD.fill(cnt)
        sched.op("sync", c3l)
        sched.bar()

        def c3q(v, grp=grp, cnt=cnt, pv=c3_mark):
            v.wait_ge(ld_sem, cnt[0])
            v.wait_ge(st_sem, pv[0])
            r = None
            for i in range(2):
                v.tensor_scalar(out=U2[i][:], in0=W2[i][:],
                                scalar1=scwb[:, 2:3], scalar2=1.49,
                                op0=ALU.mult, op1=ALU.min)
                v.tensor_scalar(out=V2[i][:].bitcast(I32), in0=U2[i][:],
                                scalar1=-1.49, scalar2=None, op0=ALU.max)
                v.tensor_copy(out=U2[i][:], in_=V2[i][:].bitcast(I32))
                r = v.tensor_copy(out=XB2[i][:], in_=U2[i][:])
            return r
        sched.op("vector", c3q)
        sched.bar()

        c3_mark = STD.marker()

        def c3s(s, grp=grp, pv=c3_mark):
            for i in range(2):
                STD.start(s, wdq_sp.rearrange("(c p) o -> c p o", p=P)[grp * 2 + i],
                          XB2[i][:])
            STD.fill(pv)
        sched.op("sync", c3s)
        sched.bar()

    cnt_wdq_done = STD.marker()

    def c4m0(s):
        STD.fill(cnt_wdq_done)
    sched.op("sync", c4m0)
    sched.bar()

    c4_mark = STD.marker()
    for s2 in range(8):
        tset = [2 * s2, 2 * s2 + 1]
        cnt = LD.marker()

        def c4l(s, tset=tset, cnt=cnt, s2=s2, pv=c4_mark):
            if s2 == 0:
                s.wait_ge(st_sem, cnt_wdq_done[0])
            s.wait_ge(st_sem, pv[0])
            for i, tt in enumerate(tset):
                LD.start(s, W2[i][:], h_t[tt][:, 0:1024])
                LD.start(s, U2[i][:], h_t[tt][:, 1024:2048])
            LD.fill(cnt)
        sched.op("sync", c4l)
        sched.bar()
        cnt2 = LD.marker()
        for half in range(2):
            def c4l2(s, tset=tset, cnt2=cnt2):
                for i, tt in enumerate(tset):
                    LD.start(s, W2[i][:], h_t[tt][:, 2048:3072])
                    LD.start(s, U2[i][:], h_t[tt][:, 3072:4096])
                LD.fill(cnt2)
            def c4q(v, tset=tset, half=half, cnt=cnt, cnt2=cnt2):
                v.wait_ge(ld_sem, (cnt if half == 0 else cnt2)[0])
                r = None
                for i, tt in enumerate(tset):
                    for seg, buf in ((0, W2[i]), (1, U2[i])):
                        v.tensor_scalar(out=V2[i][:], in0=buf[:],
                                        scalar1=RHQ[:, tt * 8:tt * 8 + 1], scalar2=127.0,
                                        op0=ALU.mult, op1=ALU.min)
                        v.tensor_scalar(out=buf[:].bitcast(I32), in0=V2[i][:],
                                        scalar1=-128.0, scalar2=None, op0=ALU.max)
                        v.tensor_copy(out=V2[i][:], in_=buf[:].bitcast(I32))
                        r = v.tensor_copy(out=(XB2 if seg == 0 else HB2)[i][:],
                                          in_=V2[i][:])
                return r
            sched.op("vector", c4q)
            sched.bar()

            def c4t(t, tset=tset, half=half):
                r = None
                for i in range(2):
                    for seg in range(2):
                        srct = (XB2 if seg == 0 else HB2)[i][:]
                        for dc in range(DC):
                            r = t.transpose(
                                psAb[:, ((i * 2 + seg) * DC + dc) * P:((i * 2 + seg) * DC + dc + 1) * P],
                                srct[:, dc * P:(dc + 1) * P], idb[:])
                return r
            sched.op("tensor", c4t)
            if half == 0:
                sched.op("sync", c4l2)
            sched.bar()

            def c4e(v, tset=tset, half=half):
                r = None
                for i in range(2):
                    for seg in range(2):
                        for dc in range(DC):
                            k = (half * 2 + seg) * DC + dc
                            r = v.tensor_copy(
                                out=QT[k // 8][:, (k % 8) * 256 + i * P:(k % 8) * 256 + (i + 1) * P],
                                in_=psAb[:, ((i * 2 + seg) * DC + dc) * P:((i * 2 + seg) * DC + dc + 1) * P])
                return r
            sched.op("vector", c4e)
            sched.bar()
        ck = {0: LD.marker()}

        def c4w0(s, cnt3=ck[0]):
            for c in range(4):
                LD.start(s, WB[c][:], wdq_sp.rearrange("(c p) o -> c p o", p=P)[c])
            LD.fill(cnt3)
        sched.op("sync", c4w0)
        sched.bar()

        for kp in range(16):
            if kp + 2 < 16:
                ck[kp + 2] = LD.marker()

                def c4wl(s, kp=kp, cnt3=ck[kp + 2]):
                    for c in (2 * kp + 4, 2 * kp + 5):
                        LD.start(s, WB[c % 8][:],
                                 wdq_sp.rearrange("(c p) o -> c p o", p=P)[c])
                    LD.fill(cnt3)
                sched.op("sync", c4wl)
            if kp == 12:
                cnt4 = LD.marker()

                def c4xl(s, tset=tset, cnt4=cnt4):
                    for i, tt in enumerate(tset):
                        LD.start(s, W2[i][:], x_t[tt])
                    LD.fill(cnt4)
                sched.op("sync", c4xl)

            def c4mm(t, tset=tset, kp=kp, cnt3=ck[0 if kp < 2 else kp]):
                t.wait_ge(ld_sem, cnt3[0])
                r = None
                for k in (2 * kp, 2 * kp + 1):
                    for i, tt in enumerate(tset):
                        for half_o in range(2):
                            r = t.matmul(
                                psB[:, (i * 2 + half_o) * 512:(i * 2 + half_o + 1) * 512],
                                QT[k // 8][:, (k % 8) * 256 + i * P:(k % 8) * 256 + (i + 1) * P],
                                WB[k % 8][:, half_o * 512:(half_o + 1) * 512],
                                start=(k == 0), stop=(k == 31))
                return r
            sched.op("tensor", c4mm)
            sched.bar()

        def c4f(v, tset=tset, cnt4=cnt4):
            v.wait_ge(ld_sem, cnt4[0])
            r = None
            for i, tt in enumerate(tset):
                r = v.scalar_tensor_tensor(out=U2[i][:], in0=psB[:, i * 1024:(i + 1) * 1024],
                                           scalar=RH[:, tt * 8 + 1:tt * 8 + 2],
                                           in1=W2[i][:], op0=ALU.mult, op1=ALU.add)
            return r
        sched.op("vector", c4f)
        sched.bar()

        c4_mark = STD.marker()

        def c4st(s, tset=tset, pv=c4_mark, s2=s2):
            for i, tt in enumerate(tset):
                STD.start(s, x2a_t[tt], U2[i][:])
            if debug and s2 == 7:
                STD.start(s, dbg_hqT, QT[0][:])
                STD.start(s, dbg_rh, RH[:])
            STD.fill(pv)
        sched.op("sync", c4st)
        sched.bar()

    def qkv(dst_feature, dst, wten):
        stage_weight(wten, D, "w")
        for oc in range(DC):
            def mm1(t, oc=oc, dst=dst, dst_feature=dst_feature):
                r = None
                if dst_feature:
                    for tq in range(4):
                        for dc in range(DC):
                            r = t.matmul(psA[:, tq * 512:(tq + 1) * 512],
                                         WB[dc][:, oc * P:(oc + 1) * P],
                                         xnT[dc][:, tq * 512:(tq + 1) * 512],
                                         start=(dc == 0), stop=(dc == DC - 1))
                else:
                    for half in range(2):
                        for dc in range(DC):
                            r = t.matmul(psA[:, half * 512:(half + 1) * 512],
                                         xnT[dc][:, (oc * 2) * P:(oc * 2 + 1) * P],
                                         WB[dc][:, half * 512:(half + 1) * 512],
                                         start=(dc == 0), stop=(dc == DC - 1))
                        for dc in range(DC):
                            r = t.matmul(psA[:, 1024 + half * 512:1024 + (half + 1) * 512],
                                         xnT[dc][:, (oc * 2 + 1) * P:(oc * 2 + 2) * P],
                                         WB[dc][:, half * 512:(half + 1) * 512],
                                         start=(dc == 0), stop=(dc == DC - 1))
                return r
            sched.op("tensor", mm1)
            sched.bar()

            def ev1(v, oc=oc, dst=dst, dst_feature=dst_feature):
                r = None
                if dst_feature:
                    r = v.tensor_copy(out=dst[oc][:], in_=psA[:, 0:T])
                else:
                    r = v.tensor_copy(out=vtile(oc * 2), in_=psA[:, 0:D])
                    r = v.tensor_copy(out=vtile(oc * 2 + 1), in_=psA[:, D:2 * D])
                return r
            sched.op("vector", ev1)
            sched.bar()

    qkv(True, QT, wq_d)
    qkv(True, KQT, wk_d)
    qkv(False, None, wv_d)

    stage_weight(wgq_d, LATD, "wgq")
    for mc in range(4):
        def gqm(t, mc=mc):
            r = None
            for tq in range(4):
                for dc in range(DC):
                    r = t.matmul(psA[:, tq * 512:(tq + 1) * 512],
                                 WB[dc][:, mc * P:(mc + 1) * P],
                                 xnT[dc][:, tq * 512:(tq + 1) * 512],
                                 start=(dc == 0), stop=(dc == DC - 1))
            return r
        sched.op("tensor", gqm)
        sched.bar()

        def gqe(v, mc=mc):
            return v.tensor_copy(out=GQT[mc][:], in_=psA[:, 0:T])
        sched.op("vector", gqe)
        sched.bar()

    cnt_bd = LD.marker()

    def bdl(s):
        for dc in range(DC):
            LD.start(s, W2[0][:, dc * HEADS:(dc + 1) * HEADS],
                     bdgl_d.rearrange("(c p) h -> c p h", p=P)[dc])
            LD.start(s, W2[1][:, dc * HEADS:(dc + 1) * HEADS],
                     bdgr_d.rearrange("(c p) h -> c p h", p=P)[dc])
        LD.fill(cnt_bd)
    sched.op("sync", bdl)
    sched.bar()

    def bdc(v):
        v.wait_ge(ld_sem, cnt_bd[0])
        r = v.tensor_copy(out=XB2[0][:, 0:DC * HEADS], in_=W2[0][:, 0:DC * HEADS])
        r = v.tensor_copy(out=XB2[1][:, 0:DC * HEADS], in_=W2[1][:, 0:DC * HEADS])
        return r
    sched.op("vector", bdc)
    sched.bar()

    for sel in range(2):
        def glm(t, sel=sel):
            r = None
            for half in range(4):
                for oc in range(DC):
                    r = t.matmul(psA[0:HEADS, half * 512:(half + 1) * 512],
                                 XB2[sel][:, oc * HEADS:(oc + 1) * HEADS],
                                 QT[oc][:, half * 512:(half + 1) * 512],
                                 start=(oc == 0), stop=(oc == DC - 1))
            return r
        sched.op("tensor", glm)
        sched.bar()

        for tb8 in range(16):
            def gls(sc, sel=sel, tb8=tb8):
                r = None
                for k in range(1):
                    tt = tb8 + k
                    r = sc.activation(out=glsb[:, k * P:(k + 1) * P],
                                      in_=psA[0:HEADS, tt * P:(tt + 1) * P],
                                      func=AF.Sigmoid)
                return r
            sched.op("scalar", gls)
            sched.bar()

            def glt_t(t, sel=sel, tb8=tb8):
                r = None
                for k in range(1):
                    r = t.transpose(psB[:, 1024 + k * HEADS:1024 + (k + 1) * HEADS],
                                    glsb[0:HEADS, k * P:(k + 1) * P],
                                    idf[0:HEADS, 0:HEADS])
                return r
            sched.op("tensor", glt_t)
            sched.bar()

            def glt_e(v, sel=sel, tb8=tb8):
                r = None
                for k in range(1):
                    tt = tb8 + k
                    r = v.tensor_copy(out=glt[tt][:, sel * HEADS:(sel + 1) * HEADS],
                                      in_=psB[:, 1024 + k * HEADS:1024 + (k + 1) * HEADS])
                return r
            sched.op("vector", glt_e)
            sched.bar()

    lps_mark = {0: STD.marker(), 1: STD.marker()}
    NLH = LHEADS
    for tt in range(NT):
        cntl = LD.marker()

        def g0(s, tt=tt, cntl=cntl):
            LD.start(s, lmsk2[0][:], lmask_t[tt])
            LD.fill(cntl)
        sched.op("sync", g0)
        sched.bar()

        for k in range(NLH + 5):
            def gp_t(t, tt=tt, k=k):
                r = None
                lh = k
                if lh < NLH:
                    base = (lh % 2) * 64
                    r = t.matmul(psA[:, (lh % 3) * 512:(lh % 3) * 512 + NLAT],
                                 GQT[lh // 2][base:base + 64, tt * P:(tt + 1) * P],
                                 GKT[lh // 2][base:base + 64, 0:NLAT],
                                 start=True, stop=True)
                lh = k - 4
                if 0 <= lh < NLH:
                    r = t.transpose(psBb[0:NLAT, 2048 + (lh % 2) * 1024:2048 + (lh % 2) * 1024 + P],
                                    pS[:, (lh % 3) * NLAT:(lh % 3 + 1) * NLAT],
                                    idb[:])
                return r
            sched.op("tensor", gp_t)

            wmk = lps_mark[tt % 2]

            def gp_v(v, tt=tt, k=k, cntl=cntl, wmk=wmk):
                r = None
                lh = k - 1
                if 0 <= lh < NLH:
                    v.wait_ge(ld_sem, cntl[0])
                    v.scalar_tensor_tensor(out=sS[:, (lh % 2) * NLAT:(lh % 2) * NLAT + NLAT],
                                           in0=psA[:, (lh % 3) * 512:(lh % 3) * 512 + NLAT],
                                           scalar=0.125, in1=lmsk2[0][:],
                                           op0=ALU.mult, op1=ALU.add)
                    VS.edge(v, v.tensor_reduce(out=mark[:, 16 + (lh % 2) * 16:17 + (lh % 2) * 16],
                                               in_=sS[:, (lh % 2) * NLAT:(lh % 2) * NLAT + NLAT],
                                               axis=AX.X, op=ALU.max))
                    VS.edge(v, v.tensor_scalar(out=mark[:, 24 + (lh % 2) * 16:25 + (lh % 2) * 16],
                                               in0=mark[:, 16 + (lh % 2) * 16:17 + (lh % 2) * 16],
                                               scalar1=-1.0, scalar2=None, op0=ALU.mult))
                lh = k - 3
                if 0 <= lh < NLH:
                    VS.edge(v, v.reciprocal(out=mark[:, 8:9], in_=rc[:, lh * 8:lh * 8 + 1]))
                    VS.edge(v, v.tensor_scalar(out=pS[:, (lh % 3) * NLAT:(lh % 3 + 1) * NLAT],
                                               in0=pS[:, (lh % 3) * NLAT:(lh % 3 + 1) * NLAT],
                                               scalar1=mark[:, 8:9], scalar2=None,
                                               op0=ALU.mult))
                lh = k - 5
                if 0 <= lh < NLH:
                    if lh % 2 == 0:
                        r = v.tensor_copy(out=LPT[lh // 2][0:NLAT, tt * P:(tt + 1) * P],
                                          in_=psBb[0:NLAT, 2048 + (lh % 2) * 1024:2048 + (lh % 2) * 1024 + P])
                    else:
                        v.wait_ge(st_sem, wmk[0])
                        r = v.tensor_copy(out=LPS[0:NLAT, (tt % 2) * P:(tt % 2 + 1) * P],
                                          in_=psBb[0:NLAT, 2048 + (lh % 2) * 1024:2048 + (lh % 2) * 1024 + P])
                if r is None:
                    r = v.tensor_copy(out=mkr[:, 0:1], in_=mkr[:, 0:1])
                return r
            sched.op("vector", gp_v)

            def gp_s(sc, tt=tt, k=k):
                lh = k - 2
                if 0 <= lh < NLH:
                    return sc.activation(out=pS[:, (lh % 3) * NLAT:(lh % 3 + 1) * NLAT],
                                         in_=sS[:, (lh % 2) * NLAT:(lh % 2) * NLAT + NLAT],
                                         func=AF.Exp,
                                         bias=mark[:, 24 + (lh % 2) * 16:25 + (lh % 2) * 16],
                                         accum_out=rc[:, lh * 8:lh * 8 + 1])
                return None
            sched.op("scalar", gp_s)

            lh6 = k - 6
            if 0 <= lh6 < NLH and lh6 % 2 == 1:
                lps_mark[tt % 2] = STD.marker()

                def g7s(s, tt=tt, lh=lh6, pv=lps_mark[tt % 2]):
                    STD.start(s, LPT[lh // 2][64:128, tt * P:(tt + 1) * P],
                              LPS[0:NLAT, (tt % 2) * P:(tt % 2 + 1) * P])
                    STD.fill(pv)
                sched.op("sync", g7s)
            sched.bar()

        lps_mark[tt % 2] = STD.marker()

        def g7z(s, tt=tt, pv=lps_mark[tt % 2]):
            STD.start(s, LPT[3][64:128, tt * P:(tt + 1) * P],
                      LPS[0:NLAT, (tt % 2) * P:(tt % 2 + 1) * P])
            STD.fill(pv)
        sched.op("sync", g7z)
        sched.bar()

    cnt_lp = STD.marker()

    def h0(s):
        STD.fill(cnt_lp)
    sched.op("sync", h0)
    sched.bar()

    h3_mark = STD.marker()
    for lh in range(LHEADS):
        def h1(t, lh=lh):
            r = None
            base = (lh % 2) * 64
            for tq in range(4):
                r = t.matmul(psA[0:64, tq * 512:(tq + 1) * 512],
                             GV[base:base + 64, lh * 64:(lh + 1) * 64],
                             LPT[lh // 2][base:base + 64, tq * 512:(tq + 1) * 512],
                             start=True, stop=True)
            return r
        if lh == 0:
            def h1w(t, lh=lh, pvg=gvd_mark):
                t.wait_ge(st_sem, pvg[0])
                t.wait_ge(st_sem, cnt_lp[0])
                return h1(t)
            sched.op("tensor", h1w)
        else:
            sched.op("tensor", h1)
        sched.bar()

        if lh % 2 == 0:
            def h2(v, lh=lh):
                r = None
                for tq in range(4):
                    r = v.tensor_copy(out=GQT[lh // 2][0:64, tq * 512:(tq + 1) * 512],
                                      in_=psA[0:64, tq * 512:(tq + 1) * 512])
                return r
            sched.op("vector", h2)
            sched.bar()
        else:
            for tq in range(4):
                def h2b(v, lh=lh, tq=tq, pv=h3_mark):
                    v.wait_ge(st_sem, pv[0])
                    return v.tensor_copy(out=LPS[0:64, (tq % 2) * 256:(tq % 2) * 256 + 256],
                                         in_=psA[0:64, tq * 512:tq * 512 + 256])
                sched.op("vector", h2b)
                sched.bar()
                h3_mark = STD.marker()

                def h3(s, lh=lh, tq=tq, pv=h3_mark):
                    STD.start(s, GQT[lh // 2][64:128, tq * 512:tq * 512 + 256],
                              LPS[0:64, (tq % 2) * 256:(tq % 2) * 256 + 256])
                    STD.fill(pv)
                sched.op("sync", h3)
                sched.bar()
                def h2d(v, lh=lh, tq=tq, pv=h3_mark):
                    v.wait_ge(st_sem, pv[0])
                    return v.tensor_copy(out=LPS[0:64, (tq % 2) * 256:(tq % 2) * 256 + 256],
                                         in_=psA[0:64, tq * 512 + 256:tq * 512 + 512])
                sched.op("vector", h2d)
                sched.bar()
                h3_mark = STD.marker()

                def h3b(s, lh=lh, tq=tq, pv=h3_mark):
                    STD.start(s, GQT[lh // 2][64:128, tq * 512 + 256:tq * 512 + 512],
                              LPS[0:64, (tq % 2) * 256:(tq % 2) * 256 + 256])
                    STD.fill(pv)
                sched.op("sync", h3b)
                sched.bar()

    cnt_rem = STD.marker()

    def h4(s):
        STD.fill(cnt_rem)
    sched.op("sync", h4)
    sched.bar()

    for grp in range(2):
        cnt = LD.marker()

        def rol(s, grp=grp, cnt=cnt):
            s.wait_ge(st_sem, cnt_rem[0])
            for i in range(2):
                LD.start(s, W2[i][:], rout_d.rearrange("(c p) o -> c p o", p=P)[grp * 2 + i])
            LD.fill(cnt)
        sched.op("sync", rol)
        sched.bar()

        def roc(v, grp=grp, cnt=cnt):
            v.wait_ge(ld_sem, cnt[0])
            r = None
            for i in range(2):
                r = v.tensor_copy(out=LPT[grp * 2 + i][:, 0:D], in_=W2[i][:])
            return r
        sched.op("vector", roc)
        sched.bar()

    stage_weight(wout_d, D, "wout")

    xnT5f = xnT[5][:].bitcast(F32)

    def sSv(h):
        return sS[:, 0:512] if h % 2 == 0 else xnT5f[:, 0:512]

    def pS2(h):
        return pS[:, 0:512] if h % 2 == 0 else xnT[5][:, 1024:1536]

    j16_mark = STD.marker()
    for qb in range(8):
        key0 = 0 if qb == 0 else (qb - 1) * 256
        kc0 = key0 // P

        for qc in range(2):
            tt = qb * 2 + qc
            cntm = LD.marker()

            def j0(s, qb=qb, qc=qc, cntm=cntm):
                v0 = (0 if qb == 0 else 2) + qc
                LD.start(s, msk[0][:], masks_d[v0])
                LD.fill(cntm)
            sched.op("sync", j0)
            sched.bar()

            def j1(t, tt=tt, qc=qc):
                r = None
                for kc in range(4):
                    for half in range(2):
                        r = t.matmul(psB[:, half * 512:(half + 1) * 512],
                                     GQT[kc][:, tt * P:(tt + 1) * P],
                                     LPT[kc][:, half * 512:(half + 1) * 512],
                                     start=(kc == 0), stop=(kc == 3))
                return r
            sched.op("tensor", j1)
            sched.bar()

            def j2(v, tt=tt, qc=qc):
                return v.tensor_copy(out=rem2v(qc), in_=psB[:, 0:D])
            sched.op("vector", j2)
            sched.bar()

            for k in range(HEADS + 6):
                def jp_t(t, tt=tt, k=k, kc0=kc0):
                    r = None
                    h = k
                    if h < HEADS:
                        hb = (h % 2) * 64
                        r = t.matmul(psA[:, (h % 3) * 512:(h % 3 + 1) * 512],
                                     QT[h // 2][hb:hb + 64, tt * P:(tt + 1) * P],
                                     KQT[h // 2][hb:hb + 64, kc0 * P:kc0 * P + 512],
                                     start=True, stop=True)
                    h = k - 3
                    if 0 <= h < HEADS:
                        for kc in range(4):
                            r = t.transpose(
                                psBb[:, 2048 + (h % 2) * 1024 + kc * P:2048 + (h % 2) * 1024 + (kc + 1) * P],
                                pS2(h)[:, kc * P:(kc + 1) * P], idb[:])
                    h = k - 5
                    if 0 <= h < HEADS:
                        for kc in range(4):
                            r = t.matmul(psB[:, (h % 2) * 512:(h % 2) * 512 + DH],
                                         PTv(h)[:, kc * P:(kc + 1) * P],
                                         vtile(kc0 + kc)[:, h * DH:(h + 1) * DH],
                                         start=(kc == 0), stop=(kc == 3))
                    return r
                sched.op("tensor", jp_t)

                def jp_v(v, tt=tt, k=k, qc=qc, cntm=cntm):
                    r = None
                    h = k - 1
                    if 0 <= h < HEADS:
                        v.wait_ge(ld_sem, cntm[0])
                        v.scalar_tensor_tensor(out=sSv(h),
                                               in0=psA[:, (h % 3) * 512:(h % 3 + 1) * 512],
                                               scalar=0.125, in1=msk[0][:],
                                               op0=ALU.mult, op1=ALU.add)
                        VS.edge(v, v.tensor_reduce(
                            out=mark[:, 16 + (h % 2) * 16:17 + (h % 2) * 16],
                            in_=sSv(h), axis=AX.X, op=ALU.max))
                        VS.edge(v, v.tensor_scalar(
                            out=mark[:, 24 + (h % 2) * 16:25 + (h % 2) * 16],
                            in0=mark[:, 16 + (h % 2) * 16:17 + (h % 2) * 16],
                            scalar1=-1.0, scalar2=None, op0=ALU.mult))
                    h = k - 3
                    if 0 <= h < HEADS:
                        VS.edge(v, v.reciprocal(out=rc[:, h * 8:h * 8 + 1],
                                                in_=rc[:, h * 8 + 1:h * 8 + 2]))
                    h = k - 6
                    if 0 <= h < HEADS:
                        v.tensor_scalar(out=xnT5f[:, 768:768 + DH],
                                        in0=psB[:, (h % 2) * 512:(h % 2) * 512 + DH],
                                        scalar1=rc[:, h * 8:h * 8 + 1],
                                        scalar2=glt[tt][:, h:h + 1],
                                        op0=ALU.mult, op1=ALU.mult)
                        v.tensor_scalar(out=xnT5f[:, 832:832 + DH],
                                        in0=rem2v(qc)[:, h * DH:(h + 1) * DH],
                                        scalar1=glt[tt][:, HEADS + h:HEADS + h + 1],
                                        scalar2=None, op0=ALU.mult)
                        r = v.tensor_tensor(out=combv(qc)[:, h * DH:(h + 1) * DH],
                                            in0=xnT5f[:, 768:768 + DH],
                                            in1=xnT5f[:, 832:832 + DH], op=ALU.add)
                    if r is None:
                        r = v.tensor_copy(out=mkr[:, 0:1], in_=mkr[:, 0:1])
                    return r
                sched.op("vector", jp_v)

                def jp_s(sc, k=k):
                    r = None
                    h = k - 2
                    if 0 <= h < HEADS:
                        r = sc.activation(out=pS2(h), in_=sSv(h), func=AF.Exp,
                                          bias=mark[:, 24 + (h % 2) * 16:25 + (h % 2) * 16],
                                          accum_out=rc[:, h * 8 + 1:h * 8 + 2])
                    h = k - 4
                    if 0 <= h < HEADS:
                        r = sc.activation(out=PTv(h),
                                          in_=psBb[:, 2048 + (h % 2) * 1024:2048 + (h % 2) * 1024 + 512],
                                          func=AF.Copy, scale=1.0)
                    return r
                sched.op("scalar", jp_s)
                sched.bar()

            def j11(t, qc=qc):
                r = None
                for dc in range(DC):
                    r = t.transpose(psAb[:, 3 * 1024 + dc * P:3 * 1024 + (dc + 1) * P],
                                    combv(qc)[:, dc * P:(dc + 1) * P], idb[:])
                return r
            sched.op("tensor", j11)
            sched.bar()

            def j12(v, qc=qc):
                r = None
                for dc in range(DC):
                    r = v.tensor_copy(out=cTv(qc, dc),
                                      in_=psAb[:, 3 * 1024 + dc * P:3 * 1024 + (dc + 1) * P])
                return r
            sched.op("vector", j12)
            sched.bar()

            def j13(t, qc=qc):
                r = None
                for half in range(2):
                    for dc in range(DC):
                        r = t.matmul(psB[:, D + half * 512:D + (half + 1) * 512],
                                     cTv(qc, dc), WB[dc][:, half * 512:(half + 1) * 512],
                                     start=(dc == 0), stop=(dc == DC - 1))
                return r
            sched.op("tensor", j13)
            sched.bar()

            cntx = LD.marker()

            def j14(s, tt=tt, cntx=cntx):
                LD.start(s, W2[0][:], x2a_t[tt])
                LD.fill(cntx)
            sched.op("sync", j14)
            sched.bar()

            def j15(v, tt=tt, cntx=cntx, pv=j16_mark):
                v.wait_ge(ld_sem, cntx[0])
                v.wait_ge(st_sem, pv[0])
                return v.tensor_tensor(out=U2[0][:], in0=psB[:, D:2 * D],
                                       in1=W2[0][:], op=ALU.add)
            sched.op("vector", j15)
            sched.bar()

            j16_mark = STD.marker()

            def j16(s, tt=tt, pv=j16_mark):
                STD.start(s, out_t[tt], U2[0][:])
                STD.fill(pv)
            sched.op("sync", j16)
            sched.bar()

    def fin(g):
        g.wait_ge(st_sem, STD.count)
    sched.op("gpsimd", fin)

    streams = sched.build(ph)
    with nc.Block() as block:
        block.sync(streams["sync"])
        block.vector(streams["vector"])
        block.scalar(streams["scalar"])
        block.tensor(streams["tensor"])
        block.gpsimd(streams["gpsimd"])
    ctx.close()
    return nc


def host_prep_l1(inputs, x1, n_cores=8):
    f32 = np.float32
    gl_w = np.asarray(inputs["L1_attn_gate_local_w"], f32)
    gr_w = np.asarray(inputs["L1_attn_gate_remote_w"], f32)
    bd_gl = np.zeros((D, HEADS), f32)
    bd_gr = np.zeros((D, HEADS), f32)
    for h in range(HEADS):
        bd_gl[h * DH:(h + 1) * DH, h] = gl_w[h]
        bd_gr[h * DH:(h + 1) * DH, h] = gr_w[h]
    poolm = np.zeros((T, NLAT), f32)
    for l in range(NLAT):
        poolm[l * 32:(l + 1) * 32, l] = 1.0 / 32.0
    NEG = np.float32(-1e30)
    masks = np.full((4, P, 512), NEG, f32)
    for qc in range(2):
        for i in range(P):
            iq = qc * P + i
            lo = iq + 1
            hi = iq + 256
            masks[2 + qc, i, lo:hi + 1] = 0.0
            masks[qc, i, 0:iq + 1] = 0.0
    lmask = np.full((T, NLAT), NEG, f32)
    for t_ in range(T):
        lmask[t_, 0:t_ // 32 + 1] = 0.0
    shared = {
        "wq": np.ascontiguousarray(np.asarray(inputs["L1_attn_wq"], f32)),
        "wk": np.ascontiguousarray(np.asarray(inputs["L1_attn_wk"], f32)),
        "wv": np.ascontiguousarray(np.asarray(inputs["L1_attn_wv"], f32)),
        "wgq": np.ascontiguousarray(np.asarray(inputs["L1_attn_wgq"], f32)),
        "wgk": np.ascontiguousarray(np.asarray(inputs["L1_attn_wgk"], f32)),
        "wgv": np.ascontiguousarray(np.asarray(inputs["L1_attn_wgv"], f32)),
        "r_out_w": np.ascontiguousarray(np.asarray(inputs["L1_attn_r_out_w"], f32)),
        "w_out": np.ascontiguousarray(np.asarray(inputs["L1_attn_w_out"], f32)),
        "bd_glw": bd_gl,
        "bd_grw": bd_gr,
        "w_gT": np.ascontiguousarray(np.asarray(inputs["L1_ffn_w_gate"], f32).T),
        "w_uT": np.ascontiguousarray(np.asarray(inputs["L1_ffn_w_up"], f32).T),
        "w_dT": np.ascontiguousarray(np.asarray(inputs["L1_ffn_w_down"], f32).T),
        "ident_f": np.eye(P, dtype=f32),
        "ident_b": np.eye(P, dtype=BF16NP),
        "poolm": poolm.astype(BF16NP),
        "masks": masks,
        "lmask": lmask,
    }
    maps = []
    for c in range(n_cores):
        m = dict(shared)
        xc = np.asarray(x1[c], f32)
        if xc.shape == (T, D):
            m["x"] = np.ascontiguousarray(xc)
        maps.append(m)
    return maps


def run_l1(inputs, x1, n_cores=8, debug=False):
    nc = bass.Bass("TRN2", target_bir_lowering=False,
                   detect_race_conditions=False,
                   dynamic_dma_scratch_size=2048)
    build_l1(nc, debug=debug)
    maps = host_prep_l1(inputs, x1, n_cores)
    res = run_bass_kernel_spmd(nc, maps, core_ids=list(range(n_cores)))
    if debug:
        return res
    return np.stack([np.asarray(r["out"]) for r in res.results], 0)


def run_fused(inputs, n_cores=8):
    gmax_row = run_pmax(inputs, n_cores)
    nc = bass.Bass("TRN2", target_bir_lowering=False,
                   detect_race_conditions=False,
                   dynamic_dma_scratch_size=2048)
    shared = build_l0(nc, n_cores=n_cores, store_out=False, fused=True)
    build_l1(nc, shared=shared)
    maps0 = host_prep_l0(inputs, n_cores)
    maps1 = host_prep_l1(inputs, np.zeros((n_cores, 1, 1), np.float32), n_cores)
    maps = []
    for c in range(n_cores):
        m = dict(maps0[c])
        m1 = dict(maps1[c])
        for k in ("x", "ident_f", "ident_b"):
            m1.pop(k, None)
        m.update(m1)
        m["gmax_row"] = gmax_row
        maps.append(m)
    res = run_bass_kernel_spmd(nc, maps, core_ids=list(range(n_cores)))
    return np.stack([np.asarray(r["out"]) for r in res.results], 0)


def kernel(**inputs):
    out = run_fused(inputs, n_cores=8)
    return out.astype(np.float32)


# version 09
# speedup vs baseline: 1.3926x; 1.0270x over previous
import contextlib
import numpy as np
import ml_dtypes
import concourse.bass as bass
import concourse.mybir as mybir
from concourse.bass_utils import run_bass_kernel_spmd

F32 = mybir.dt.float32
BF16 = mybir.dt.bfloat16
I32 = mybir.dt.int32
AF = mybir.ActivationFunctionType
ALU = mybir.AluOpType
AX = mybir.AxisListType
BF16NP = ml_dtypes.bfloat16

P = 128
T = 2048
D = 1024
NT = T // P
DC = D // P
NST = 16
EPS = 1e-5

ENGINES = ("tensor", "vector", "scalar", "gpsimd", "sync")


class Sched:

    def __init__(self, nc):
        self.nc = nc
        self.items = []
        self.cur = 0

    def op(self, engine, fn):
        assert engine in ENGINES
        self.items.append((self.cur, engine, fn))

    def bar(self):
        self.cur += 1

    def build(self, ph):
        nph = self.cur + 1
        per = {e: {} for e in ENGINES}
        for p, e, f in self.items:
            per[e].setdefault(p, []).append(f)

        def mk(ename):
            def body(eng):
                for p in range(nph):
                    if p:
                        eng.wait_ge(ph, len(ENGINES) * p)
                    last = None
                    for f in per[ename].get(p, []):
                        r = f(eng)
                        if r is not None:
                            last = r
                    if last is not None and ename != "sync":
                        last.then_inc(ph, 1)
                    else:
                        eng.nop().then_inc(ph, 1)
            return body
        return {e: mk(e) for e in ENGINES}


class VSync:

    def __init__(self, sem):
        self.sem = sem
        self.n = 0

    def edge(self, eng, inst):
        inst.then_inc(self.sem, 1)
        self.n += 1
        eng.wait_ge(self.sem, self.n)
        return inst


class Dma:

    def __init__(self, sem):
        self.sem = sem
        self.count = 0

    def start(self, eng, out, in_, **kw):
        eng.dma_start(out=out, in_=in_, **kw).then_inc(self.sem, 16)
        self.count += 16
        return self.count

    def marker(self):
        return [0]

    def fill(self, m):
        m[0] = self.count




def ln_stats_v(v, src, ST, VS):
    VS.edge(v, v.bn_stats(out=ST[:, 56:62], in_=src[:, 0:512]))
    VS.edge(v, v.bn_stats(out=ST[:, 64:70], in_=src[:, 512:1024]))
    VS.edge(v, v.bn_aggr(out=ST[:, 0:2], in_=ST[:, 56:62]))
    VS.edge(v, v.bn_aggr(out=ST[:, 8:10], in_=ST[:, 64:70]))
    VS.edge(v, v.tensor_tensor(out=ST[:, 16:17], in0=ST[:, 0:1], in1=ST[:, 8:9],
                               op=ALU.subtract))
    VS.edge(v, v.tensor_tensor(out=ST[:, 16:17], in0=ST[:, 16:17], in1=ST[:, 16:17],
                               op=ALU.mult))
    VS.edge(v, v.tensor_tensor(out=ST[:, 24:25], in0=ST[:, 0:1], in1=ST[:, 8:9],
                               op=ALU.add))
    VS.edge(v, v.tensor_scalar(out=ST[:, 24:25], in0=ST[:, 24:25], scalar1=0.5,
                               scalar2=None, op0=ALU.mult))
    VS.edge(v, v.tensor_tensor(out=ST[:, 32:33], in0=ST[:, 1:2], in1=ST[:, 9:10],
                               op=ALU.add))
    VS.edge(v, v.tensor_scalar(out=ST[:, 16:17], in0=ST[:, 16:17], scalar1=0.25,
                               scalar2=None, op0=ALU.mult))
    VS.edge(v, v.scalar_tensor_tensor(out=ST[:, 32:33], in0=ST[:, 32:33], scalar=0.5,
                                      in1=ST[:, 16:17], op0=ALU.mult, op1=ALU.add))
    return VS.edge(v, v.tensor_scalar(out=ST[:, 40:41], in0=ST[:, 32:33], scalar1=EPS,
                                      scalar2=None, op0=ALU.add))


def ln_sqrt_s(sc, ST):
    return sc.activation(out=ST[:, 40:41], in_=ST[:, 40:41], func=AF.Sqrt)


def ln_coefs_v(v, ST, VS):
    VS.edge(v, v.reciprocal(out=ST[:, 40:41], in_=ST[:, 40:41]))
    VS.edge(v, v.tensor_tensor(out=ST[:, 48:49], in0=ST[:, 24:25], in1=ST[:, 40:41],
                               op=ALU.mult))
    return VS.edge(v, v.tensor_scalar(out=ST[:, 48:49], in0=ST[:, 48:49], scalar1=-1.0,
                                      scalar2=None, op0=ALU.mult))




def build_l0(nc, n_cores=8, store_out=True, debug=False, fused=False):

    def din(name, shape, dt=F32):
        return nc.dram_tensor(name, shape, dt, kind="ExternalInput").ap()

    def dint(name, shape, dt=F32):
        return nc.dram_tensor(name, shape, dt, kind="Internal").ap()

    x = din("x", [T, D])
    x1 = nc.dram_tensor("x1", [T, D], F32,
                        kind="ExternalOutput" if store_out else "Internal").ap()
    alog = din("alog_r", [P, P])
    bwT = din("bwT", [D, NST])
    cwT = din("cwT", [D, NST])
    d_row = din("d_row", [1, D])
    gwT = din("gate_wT", [D, D])
    gb_row = din("gate_b_row", [1, D])
    lng_row = din("ln_g_row", [1, D])
    lnb_row = din("ln_b_row", [1, D])
    l0wT = din("l0w_T", [D, D])
    g2_row = din("g2_row", [1, D])
    b2_row = din("b2_row", [1, D])
    ident_f = din("ident_f", [P, P])
    ident_b = din("ident_b", [P, P], BF16)
    ones_row = din("ones_row", [1, P])

    dsc = (lambda n, s, dt=F32: nc.dram_tensor(n, s, dt, kind="ExternalOutput").ap()) if debug else dint
    if debug:
        dbg_st = nc.dram_tensor("dbg_st", [P, 72], F32, kind="ExternalOutput").ap()
        dbg_xn = nc.dram_tensor("dbg_xn", [P, D], F32, kind="ExternalOutput").ap()
        dbg_xnT = nc.dram_tensor("dbg_xnT", [P, T], BF16, kind="ExternalOutput").ap()
        dbg_st1a = nc.dram_tensor("dbg_st1a", [P, 72], F32, kind="ExternalOutput").ap()
        dbg_st1b = nc.dram_tensor("dbg_st1b", [P, 72], F32, kind="ExternalOutput").ap()
        dbg_y = nc.dram_tensor("dbg_y", [P, T], BF16, kind="ExternalOutput").ap()
        dbg_sg = nc.dram_tensor("dbg_sg", [P, D], F32, kind="ExternalOutput").ap()
    xn_sp = dsc("xn_sp", [T, D])
    xn2_sp = dsc("xn2_sp", [T, D])
    x1a_sp = dsc("x1a_sp", [T, D])
    bxT_d = dsc("bxT_d", [NST, T], BF16)
    cxT_d = dsc("cxT_d", [NST, T], BF16)
    sc_d = dint("sc_d", [1, 4])
    gmax_row = din("gmax_row", [1, 8])

    x_t = x.rearrange("(n p) d -> n p d", p=P)
    x1_t = x1.rearrange("(n p) d -> n p d", p=P)
    xn_t = xn_sp.rearrange("(n p) d -> n p d", p=P)
    xn2_t = xn2_sp.rearrange("(n p) d -> n p d", p=P)
    x1a_t = x1a_sp.rearrange("(n p) d -> n p d", p=P)

    ctx = contextlib.ExitStack()
    sb = lambda name, shape, dt=F32: ctx.enter_context(nc.sbuf_tensor(name, shape, dt))

    xnT = [sb(f"xnT{c}", [P, T], BF16) for c in range(DC)]
    y_sb = [sb(f"ysb{c}", [P, T], BF16) for c in range(DC)]
    wqT = [sb(f"wqT{c}", [P, D], BF16) for c in range(DC)]
    gwTb = [sb(f"gwTb{c}", [P, D], BF16) for c in range(DC)]
    idf = sb("idf", [P, P])
    idb = sb("idb", [P, P], BF16)
    ones_r = sb("ones_r", [1, P])
    aexp = sb("aexp", [P, P])
    aexb = sb("aexb", [P, P], BF16)
    dm1b = sb("dm1b", [P, D])
    gbb = sb("gbb", [1, D])
    lngb = sb("lngb", [P, D])
    lnbb = sb("lnbb", [P, D])
    g2b = sb("g2b", [P, D])
    b2b = sb("b2b", [P, D])
    NW = 2
    W4 = [sb(f"w4_{i}", [P, D]) for i in range(NW)]
    XN4 = [sb(f"xn4_{i}", [P, D]) for i in range(NW)]
    XB4 = [sb(f"xb4_{i}", [P, D], BF16) for i in range(NW)]
    U4 = [sb(f"u4_{i}", [P, D]) for i in range(NW)]
    V4 = [sb(f"v4_{i}", [P, D]) for i in range(NW)]
    SG4 = [sb(f"sg4_{i}", [P, D]) for i in range(NW)]
    YT2 = [sb(f"yt2_{i}", [P, D]) for i in range(NW)]
    ST4 = [sb(f"st4_{i}", [P, 72]) for i in range(NW)]
    mark = sb("mark", [P, 8])
    mkr = mark
    qT_t = [sb(f"qTt{c}", [P, NW * P], BF16) for c in range(DC)]
    amax = sb("amax", [P, 16])
    wsum = sb("wsum", [P, 64])
    wmsk = sb("wmsk", [P, 16])
    wcnt = sb("wcnt", [P, 64])
    scal = sb("scal", [1, P])
    scol = sb("scol", [P, 4])
    bw_sb = sb("bw_sb", [P, P])
    bwb = sb("bwb", [P, P], BF16)
    cwb = sb("cwb", [P, P], BF16)
    TBW = 512
    bc2 = [sb(f"bc{i}", [P, TBW], BF16) for i in range(2)]
    cc2 = [sb(f"cc{i}", [P, TBW], BF16) for i in range(2)]
    gsc = [sb("gsc0", [P, TBW], BF16)] * 2
    hsc = [sb("hsc0", [P, TBW], BF16)] * 2
    psc = [sb("psc0", [P, DC * TBW], BF16)] * 2
    carry = sb("carry", [P, P])
    bx_sb = sb("bx_sb", [NST, T], BF16)
    cx_sb = sb("cx_sb", [NST, T], BF16)

    psA = ctx.enter_context(nc.psum_tensor("psA", [P, T], F32))
    psB = ctx.enter_context(nc.psum_tensor("psB", [P, T], F32))
    psAb = psA[:].bitcast(BF16)

    sched = Sched(nc)
    ld_sem = ctx.enter_context(nc.semaphore("LdL0"))
    st_sem = ctx.enter_context(nc.semaphore("StL0"))
    ph = ctx.enter_context(nc.semaphore("PhL0"))
    vs_sem = ctx.enter_context(nc.semaphore("VsL0"))
    LD = Dma(ld_sem)
    STD = Dma(st_sem)
    VS = VSync(vs_sem)

    sched.op("vector", lambda v: v.memset(mark[:, 0:1], 0.0))

    cw = LD.marker()

    def ph_consts(s):
        LD.start(s, idf[:], ident_f)
        LD.start(s, idb[:], ident_b)
        LD.start(s, ones_r[:], ones_row)
        LD.start(s, aexp[:], alog)
        LD.start(s, dm1b[:], d_row.squeeze(0).partition_broadcast(P))
        LD.start(s, gbb[:], gb_row)
        LD.start(s, lngb[:], lng_row.squeeze(0).partition_broadcast(P))
        LD.start(s, lnbb[:], lnb_row.squeeze(0).partition_broadcast(P))
        LD.start(s, g2b[:], g2_row.squeeze(0).partition_broadcast(P))
        LD.start(s, b2b[:], b2_row.squeeze(0).partition_broadcast(P))
        LD.start(s, bw_sb[:].rearrange("p (c n) -> p c n", n=NST), bwT.rearrange("(c p) n -> p c n", p=P))
        LD.fill(cw)
    sched.op("sync", ph_consts)
    sched.bar()

    def ph_c2v(v):
        v.wait_ge(ld_sem, cw[0])
        v.tensor_scalar(out=dm1b[:], in0=dm1b[:], scalar1=1.0, scalar2=None,
                        op0=ALU.subtract)
        v.tensor_copy(out=bwb[:], in_=bw_sb[:])
        return v.memset(carry[:], 0.0)
    sched.op("vector", ph_c2v)

    def ph_c2s(sc):
        sc.wait_ge(ld_sem, cw[0])
        return sc.activation(out=aexp[:], in_=aexp[:], func=AF.Exp)
    sched.op("scalar", ph_c2s)
    sched.bar()

    cw2 = LD.marker()

    def ph_c2l(s):
        LD.start(s, bw_sb[:].rearrange("p (c n) -> p c n", n=NST), cwT.rearrange("(c p) n -> p c n", p=P))
        LD.fill(cw2)
    sched.op("sync", ph_c2l)
    sched.bar()

    def ph_c3v(v):
        v.wait_ge(ld_sem, cw2[0])
        v.tensor_copy(out=cwb[:], in_=bw_sb[:])
        return v.tensor_copy(out=aexb[:], in_=aexp[:])
    sched.op("vector", ph_c3v)
    sched.bar()

    gwT_t = gwT.rearrange("(c p) e -> c p e", p=P)
    for grp in range(4):
        cnt = LD.marker()

        def ph_gwl(s, grp=grp, cnt=cnt):
            for i in range(2):
                LD.start(s, W4[i][:], gwT_t[grp * 2 + i])
            LD.fill(cnt)
        sched.op("sync", ph_gwl)
        sched.bar()

        def ph_gwc(v, grp=grp, cnt=cnt):
            v.wait_ge(ld_sem, cnt[0])
            r = None
            for i in range(2):
                r = v.tensor_copy(out=gwTb[grp * 2 + i][:], in_=W4[i][:])
            return r
        sched.op("vector", ph_gwc)
        sched.bar()

    prev_store = STD.marker()
    for g in range(8):
        tiles = [2 * g + i for i in range(2)]
        cnt = LD.marker()

        def a0(s, tiles=tiles, pv=prev_store, cnt=cnt):
            s.wait_ge(st_sem, pv[0])
            for i, tt in enumerate(tiles):
                LD.start(s, W4[i][:], x_t[tt])
            LD.fill(cnt)
        sched.op("sync", a0)
        sched.bar()

        def a1(v, cnt=cnt, g=g):
            v.wait_ge(ld_sem, cnt[0])
            if debug and g == 0:
                v.tensor_copy(out=SG4[0][:], in_=W4[0][:])
            for i in range(2):
                ln_stats_v(v, W4[i][:], ST4[i], VS)
            return v.tensor_copy(out=mkr[:, 0:1], in_=mkr[:, 0:1])
        sched.op("vector", a1)
        sched.bar()

        def a2(sc):
            r = None
            for i in range(2):
                r = ln_sqrt_s(sc, ST4[i])
            return r
        sched.op("scalar", a2)
        if debug and g == 0:
            def a2d(s):
                STD.start(s, dbg_st1a, ST4[1][:])
            sched.op("sync", a2d)
        sched.bar()

        def a4(v):
            r = None
            for i in range(2):
                ln_coefs_v(v, ST4[i], VS)
                r = v.tensor_scalar(out=XN4[i][:], in0=W4[i][:],
                                    scalar1=ST4[i][:, 40:41],
                                    scalar2=ST4[i][:, 48:49],
                                    op0=ALU.mult, op1=ALU.add)
            return r
        sched.op("vector", a4)

        if debug and g == 0:
            def a4snap(v):
                return v.tensor_copy(out=SG4[1][:, 0:72], in_=ST4[0][:])
            sched.op("vector", a4snap)
            sched.bar()
            def a4d(s):
                STD.start(s, dbg_st, SG4[1][:, 0:72])
                STD.start(s, dbg_xn, SG4[0][:])
            sched.op("sync", a4d)
        sched.bar()

        def a5(v):
            for i in range(2):
                v.tensor_copy(out=XB4[i][:], in_=XN4[i][:])
                ln_stats_v(v, XN4[i][:], ST4[i], VS)
            return v.tensor_copy(out=mkr[:, 0:1], in_=mkr[:, 0:1])
        sched.op("vector", a5)

        def a5s(s, tiles=tiles):
            for i, tt in enumerate(tiles):
                STD.start(s, xn_t[tt], XN4[i][:])
        sched.op("sync", a5s)
        sched.bar()

        def a6(sc):
            r = None
            for i in range(2):
                r = ln_sqrt_s(sc, ST4[i])
            return r
        sched.op("scalar", a6)

        def a6t(t, g=g):
            r = None
            for dc in range(DC):
                for i in range(2):
                    r = t.transpose(psAb[:, dc * 256 + i * P: dc * 256 + (i + 1) * P],
                                    XB4[i][:, dc * P:(dc + 1) * P], idb[:])
            return r
        sched.op("tensor", a6t)
        sched.bar()

        def a7(v, g=g):
            r = None
            for i in range(2):
                ln_coefs_v(v, ST4[i], VS)
            for dc in range(DC):
                r = v.tensor_copy(out=xnT[dc][:, g * 256:(g + 1) * 256],
                                  in_=psAb[:, dc * 256:(dc + 1) * 256])
            return r
        sched.op("vector", a7)
        sched.bar()

        def a8(v):
            r = None
            for i in range(2):
                r = v.tensor_scalar(out=U4[i][:], in0=XN4[i][:],
                                    scalar1=ST4[i][:, 40:41],
                                    scalar2=ST4[i][:, 48:49],
                                    op0=ALU.mult, op1=ALU.add)
            return r
        sched.op("vector", a8)
        sched.bar()

        def a9(v, g=g):
            r = None
            for i in range(2):
                v.tensor_tensor(out=U4[i][:], in0=U4[i][:], in1=g2b[:], op=ALU.mult)
                r = v.tensor_tensor(out=U4[i][:], in0=U4[i][:], in1=b2b[:], op=ALU.add)
            return r
        sched.op("vector", a9)
        sched.bar()

        prev_store = STD.marker()

        def a9s(s, tiles=tiles, pv=prev_store):
            for i, tt in enumerate(tiles):
                STD.start(s, xn2_t[tt], U4[i][:])
            STD.fill(pv)
        sched.op("sync", a9s)
        sched.bar()

    cnt_gmax = LD.marker()

    def b4(s):
        LD.start(s, scal[:, 0:8], gmax_row)
        LD.fill(cnt_gmax)
    sched.op("sync", b4)
    sched.bar()

    def b5(v):
        v.wait_ge(ld_sem, cnt_gmax[0])
        VS.edge(v, v.tensor_reduce(out=scal[:, 0:1], in_=scal[:, 0:8], axis=AX.X,
                                   op=ALU.max))
        VS.edge(v, v.tensor_scalar(out=scal[:, 0:1], in0=scal[:, 0:1], scalar1=1e-8,
                                   scalar2=1.0 / 127.0, op0=ALU.max, op1=ALU.mult))
        VS.edge(v, v.reciprocal(out=scal[:, 1:2], in_=scal[:, 0:1]))
        return v.tensor_copy(out=scal[:, 16:17], in_=scal[:, 1:2])
    sched.op("vector", b5)
    sched.bar()

    l0wT_t = l0wT.rearrange("(c p) o -> c p o", p=P)
    for grp in range(4):
        cnt = LD.marker()

        def c0(s, grp=grp, cnt=cnt):
            for i in range(2):
                LD.start(s, W4[i][:], l0wT_t[grp * 2 + i])
            LD.fill(cnt)
        sched.op("sync", c0)
        sched.bar()

        def c1(sc, grp=grp, cnt=cnt):
            sc.wait_ge(ld_sem, cnt[0])
            r = None
            for i in range(2):
                r = sc.activation(out=U4[i][:], in_=W4[i][:], func=AF.Abs,
                                  accum_out=wsum[:, (grp * 2 + i) * 8: (grp * 2 + i) * 8 + 1])
            return r
        sched.op("scalar", c1)
        sched.bar()

    def c2(v):
        return v.tensor_reduce(out=wmsk[:, 0:1],
                               in_=wsum[:].rearrange("p (k e) -> p k e", e=8)[:, :, 0:1].squeeze(2),
                               axis=AX.X, op=ALU.add)
    sched.op("vector", c2)
    sched.bar()

    def c3(t):
        return t.transpose(psA[0:1, 0:P], wmsk[:, 0:1], idf[:])
    sched.op("tensor", c3)
    sched.bar()

    def c4(v):
        VS.edge(v, v.tensor_reduce(out=scal[:, 2:3], in_=psA[0:1, 0:P], axis=AX.X,
                                   op=ALU.add))
        VS.edge(v, v.tensor_scalar(out=scal[:, 2:3], in0=scal[:, 2:3],
                                   scalar1=0.7 / (D * D), scalar2=None, op0=ALU.mult))
        return v.tensor_copy(out=scal[:, 16:17], in_=scal[:, 2:3])
    sched.op("vector", c4)
    sched.bar()

    cnt_c5 = STD.marker()

    def c5(s):
        STD.start(s, sc_d[:, 0:1], scal[:, 2:3])
        STD.fill(cnt_c5)
    sched.op("sync", c5)
    sched.bar()
    cnt_delta = LD.marker()

    def c6(s):
        s.wait_ge(st_sem, cnt_c5[0])
        LD.start(s, scol[:, 2:3], sc_d[:, 0:1].squeeze(0).partition_broadcast(P))
        LD.fill(cnt_delta)
    sched.op("sync", c6)
    sched.bar()

    for grp in range(4):
        cnt = LD.marker()

        def c7(s, grp=grp, cnt=cnt):
            for i in range(2):
                LD.start(s, W4[i][:], l0wT_t[grp * 2 + i])
            LD.fill(cnt)
        sched.op("sync", c7)
        sched.bar()

        def c8(sc, cnt=cnt):
            sc.wait_ge(ld_sem, cnt[0])
            sc.wait_ge(ld_sem, cnt_delta[0])
            r = None
            for i in range(2):
                sc.activation(out=U4[i][:], in_=W4[i][:], func=AF.Abs)
                r = sc.activation(out=V4[i][:], in_=W4[i][:], func=AF.Sign)
            return r
        sched.op("scalar", c8)
        sched.bar()

        def c9(v, grp=grp):
            r = None
            for i in range(2):
                k = grp * 2 + i
                v.tensor_scalar(out=SG4[i][:], in0=U4[i][:], scalar1=scol[:, 2:3],
                                scalar2=None, op0=ALU.is_gt)
                r = v.scalar_tensor_tensor(out=W4[i][:], in0=U4[i][:], scalar=1.0,
                                           in1=SG4[i][:], op0=ALU.mult,
                                           op1=ALU.mult, accum_out=wsum[:, k * 8:k * 8 + 1])
            return r
        sched.op("vector", c9)
        sched.bar()

        def c10(sc, grp=grp):
            r = None
            for i in range(2):
                k = grp * 2 + i
                r = sc.activation(out=SG4[i][:], in_=SG4[i][:], func=AF.Copy,
                                  accum_out=wcnt[:, k * 8:k * 8 + 1])
            return r
        sched.op("scalar", c10)
        sched.bar()

        def c11(v, grp=grp):
            r = None
            for i in range(2):
                k = grp * 2 + i
                r = v.tensor_tensor(out=wqT[k][:], in0=V4[i][:], in1=SG4[i][:],
                                    op=ALU.mult)
            return r
        sched.op("vector", c11)
        sched.bar()

    def c12(v):
        VS.edge(v, v.tensor_reduce(out=wmsk[:, 0:1],
                                   in_=wsum[:].rearrange("p (k e) -> p k e", e=8)[:, :, 0:1].squeeze(2),
                                   axis=AX.X, op=ALU.add))
        VS.edge(v, v.tensor_reduce(out=wmsk[:, 8:9],
                                   in_=wcnt[:].rearrange("p (k e) -> p k e", e=8)[:, :, 0:1].squeeze(2),
                                   axis=AX.X, op=ALU.add))
        return v.tensor_copy(out=mkr[:, 0:1], in_=mkr[:, 0:1])
    sched.op("vector", c12)
    sched.bar()

    def c13(t):
        t.transpose(psA[0:1, 0:P], wmsk[:, 0:1], idf[:])
        return t.transpose(psB[0:1, 0:P], wmsk[:, 8:9], idf[:])
    sched.op("tensor", c13)
    sched.bar()

    def c14(v):
        VS.edge(v, v.tensor_reduce(out=scal[:, 2:3], in_=psA[0:1, 0:P], axis=AX.X,
                                   op=ALU.add))
        VS.edge(v, v.tensor_reduce(out=scal[:, 3:4], in_=psB[0:1, 0:P], axis=AX.X,
                                   op=ALU.add))
        VS.edge(v, v.tensor_scalar(out=scal[:, 3:4], in0=scal[:, 3:4], scalar1=1.0,
                                   scalar2=None, op0=ALU.max))
        VS.edge(v, v.reciprocal(out=scal[:, 3:4], in_=scal[:, 3:4]))
        VS.edge(v, v.tensor_tensor(out=scal[:, 3:4], in0=scal[:, 3:4],
                                   in1=scal[:, 2:3], op=ALU.mult))
        VS.edge(v, v.tensor_tensor(out=scal[:, 3:4], in0=scal[:, 3:4],
                                   in1=scal[:, 0:1], op=ALU.mult))
        return v.tensor_copy(out=scal[:, 16:17], in_=scal[:, 3:4])
    sched.op("vector", c14)
    sched.bar()

    cnt_c15 = STD.marker()

    def c15(s):
        STD.start(s, sc_d[:, 1:2], scal[:, 1:2])
        STD.start(s, sc_d[:, 2:3], scal[:, 3:4])
        STD.fill(cnt_c15)
    sched.op("sync", c15)
    sched.bar()
    cnt_scales = LD.marker()

    def c16(s):
        s.wait_ge(st_sem, cnt_c15[0])
        LD.start(s, scol[:, 0:1], sc_d[:, 1:2].squeeze(0).partition_broadcast(P))
        LD.start(s, scol[:, 1:2], sc_d[:, 2:3].squeeze(0).partition_broadcast(P))
        LD.fill(cnt_scales)
    sched.op("sync", c16)
    sched.bar()

    for g in range(8):
        tiles = [2 * g + i for i in range(2)]
        cnt = LD.marker()

        def d0(s, tiles=tiles, pv=prev_store, cnt=cnt):
            s.wait_ge(st_sem, pv[0])
            for i, tt in enumerate(tiles):
                LD.start(s, U4[i][:], xn2_t[tt])
                LD.start(s, W4[i][:], x_t[tt])
            LD.fill(cnt)
        sched.op("sync", d0)
        sched.bar()

        def d1(v, cnt=cnt):
            v.wait_ge(ld_sem, cnt[0])
            v.wait_ge(ld_sem, cnt_scales[0])
            r = None
            for i in range(2):
                v.tensor_scalar(out=U4[i][:], in0=U4[i][:], scalar1=scol[:, 0:1],
                                scalar2=127.0, op0=ALU.mult, op1=ALU.min)
                v.tensor_scalar(out=V4[i][:].bitcast(I32), in0=U4[i][:],
                                scalar1=-127.0, scalar2=None, op0=ALU.max)
                v.tensor_copy(out=U4[i][:], in_=V4[i][:].bitcast(I32))
                r = v.tensor_copy(out=XB4[i][:], in_=U4[i][:])
            return r
        sched.op("vector", d1)
        sched.bar()

        def d2(t):
            r = None
            for i in range(2):
                for dc in range(DC):
                    r = t.transpose(psAb[:, (i * DC + dc) * P:(i * DC + dc + 1) * P],
                                    XB4[i][:, dc * P:(dc + 1) * P], idb[:])
            return r
        sched.op("tensor", d2)
        sched.bar()

        def d3(v):
            r = None
            for i in range(2):
                for dc in range(DC):
                    r = v.tensor_copy(out=qT_t[dc][:, i * P:(i + 1) * P],
                                      in_=psAb[:, (i * DC + dc) * P:(i * DC + dc + 1) * P])
            return r
        sched.op("vector", d3)
        sched.bar()

        def d4(t):
            r = None
            for i in range(2):
                for half_o in range(2):
                    for dc in range(DC):
                        r = t.matmul(psB[:, (i * 2 + half_o) * 512:(i * 2 + half_o + 1) * 512],
                                     qT_t[dc][:, i * P:(i + 1) * P],
                                     wqT[dc][:, half_o * 512:(half_o + 1) * 512],
                                     start=(dc == 0), stop=(dc == DC - 1))
            return r
        sched.op("tensor", d4)
        sched.bar()

        def d5(v):
            r = None
            for i in range(2):
                r = v.scalar_tensor_tensor(
                    out=V4[i][:], in0=psB[:, i * 1024:(i + 1) * 1024],
                    scalar=scol[:, 1:2], in1=W4[i][:],
                    op0=ALU.mult, op1=ALU.add)
            return r
        sched.op("vector", d5)
        sched.bar()

        prev_store = STD.marker()

        def d5s(s, tiles=tiles, pv=prev_store):
            for i, tt in enumerate(tiles):
                STD.start(s, x1a_t[tt], V4[i][:])
            STD.fill(pv)
        sched.op("sync", d5s)
        sched.bar()

    def e1(t):
        r = None
        for four in range(4):
            for dc in range(DC):
                r = t.matmul(psA[0:NST, four * 512:(four + 1) * 512],
                             bwb[:, dc * NST:(dc + 1) * NST],
                             xnT[dc][:, four * 512:(four + 1) * 512],
                             start=(dc == 0), stop=(dc == DC - 1))
        for four in range(4):
            for dc in range(DC):
                r = t.matmul(psB[0:NST, four * 512:(four + 1) * 512],
                             cwb[:, dc * NST:(dc + 1) * NST],
                             xnT[dc][:, four * 512:(four + 1) * 512],
                             start=(dc == 0), stop=(dc == DC - 1))
        return r
    sched.op("tensor", e1)
    sched.bar()

    def e2(v):
        v.tensor_copy(out=bx_sb[:], in_=psA[0:NST, 0:T])
        return v.tensor_copy(out=cx_sb[:], in_=psB[0:NST, 0:T])
    sched.op("vector", e2)
    sched.bar()

    cnt_bxcx = STD.marker()

    def e3(s):
        STD.start(s, bxT_d, bx_sb[:])
        STD.start(s, cxT_d, cx_sb[:])
        STD.fill(cnt_bxcx)
    sched.op("sync", e3)
    sched.bar()

    NTB = T // TBW
    NS = NTB * NST
    fcnt = {0: LD.marker()}

    def fpre(s):
        s.wait_ge(st_sem, cnt_bxcx[0])
        LD.start(s, bc2[0][:], bxT_d[0:1, 0:TBW].squeeze(0).partition_broadcast(P))
        LD.start(s, cc2[0][:], cxT_d[0:1, 0:TBW].squeeze(0).partition_broadcast(P))
        LD.fill(fcnt[0])
    sched.op("sync", fpre)
    sched.bar()

    def f2body(t, n, buf):
        r = None
        for dc in range(DC):
            ps = psA if dc < 4 else psB
            r = t.matmul(ps[:, (dc % 4) * 512:(dc % 4 + 1) * 512],
                         idb[:], psc[buf][:, dc * TBW:(dc + 1) * TBW],
                         start=(n == 0), stop=(n == NST - 1))
        return r

    for st in range(NS):
        tb, n = st // NST, st % NST
        buf = st % 2

        if st + 1 < NS:
            tb1, n1 = (st + 1) // NST, (st + 1) % NST
            fcnt[st + 1] = LD.marker()

            def fld(s, tb1=tb1, n1=n1, b1=(st + 1) % 2, cnt=fcnt[st + 1]):
                LD.start(s, bc2[b1][:],
                         bxT_d[n1:n1 + 1, tb1 * TBW:(tb1 + 1) * TBW].squeeze(0).partition_broadcast(P))
                LD.start(s, cc2[b1][:],
                         cxT_d[n1:n1 + 1, tb1 * TBW:(tb1 + 1) * TBW].squeeze(0).partition_broadcast(P))
                LD.fill(cnt)
            sched.op("sync", fld)

        def f1(v, tb=tb, n=n, buf=buf, cnt=fcnt[st]):
            if n == 0 and tb > 0:
                for dc in range(DC):
                    ps = psA if dc < 4 else psB
                    v.tensor_copy(out=y_sb[dc][:, (tb - 1) * TBW:tb * TBW],
                                  in_=ps[:, (dc % 4) * 512:(dc % 4 + 1) * 512])
            v.wait_ge(ld_sem, cnt[0])
            r = None
            for dc in range(DC):
                col = n * DC + dc
                v.tensor_tensor(out=gsc[buf][:],
                                in0=xnT[dc][:, tb * TBW:(tb + 1) * TBW],
                                in1=bc2[buf][:], op=ALU.mult)
                VS.edge(v, v.tensor_tensor_scan(
                    out=hsc[buf][:],
                    data0=aexb[:, dc * NST + n: dc * NST + n + 1].broadcast_to([P, TBW]),
                    data1=gsc[buf][:],
                    initial=carry[:, col:col + 1],
                    op0=ALU.mult, op1=ALU.add))
                VS.edge(v, v.tensor_copy(out=carry[:, col:col + 1],
                                         in_=hsc[buf][:, TBW - 1:TBW]))
                r = v.tensor_tensor(out=psc[buf][:, dc * TBW:(dc + 1) * TBW],
                                    in0=hsc[buf][:],
                                    in1=cc2[buf][:], op=ALU.mult)
            return r
        sched.op("vector", f1)

        if n > 0:
            def f2(t, n=n - 1, buf=(st - 1) % 2):
                return f2body(t, n, buf)
            sched.op("tensor", f2)
        sched.bar()

        if n == NST - 1:
            def f2e(t, n=n, buf=buf):
                return f2body(t, n, buf)
            sched.op("tensor", f2e)
            sched.bar()

    def f3z(v):
        r = None
        for dc in range(DC):
            ps = psA if dc < 4 else psB
            r = v.tensor_copy(out=y_sb[dc][:, (NTB - 1) * TBW:NTB * TBW],
                              in_=ps[:, (dc % 4) * 512:(dc % 4 + 1) * 512])
        return r
    sched.op("vector", f3z)
    sched.bar()

    for pair in range(8):
        tiles = [2 * pair, 2 * pair + 1]
        cnt = LD.marker()

        def g0(s, tiles=tiles, pv=prev_store, cnt=cnt):
            s.wait_ge(st_sem, pv[0])
            for i, tt in enumerate(tiles):
                LD.start(s, W4[i][:], xn_t[tt])
                LD.start(s, U4[i][:], x1a_t[tt])
            LD.fill(cnt)
        sched.op("sync", g0)
        sched.bar()

        def g1(t, tiles=tiles):
            r = None
            for i, tt in enumerate(tiles):
                for half_o in range(2):
                    for dc in range(DC):
                        t.matmul(psB[:, i * 1024 + half_o * 512: i * 1024 + (half_o + 1) * 512],
                                 xnT[dc][:, tt * P:(tt + 1) * P],
                                 gwTb[dc][:, half_o * 512:(half_o + 1) * 512],
                                 start=(dc == 0), stop=False)
                    r = t.matmul(psB[:, i * 1024 + half_o * 512: i * 1024 + (half_o + 1) * 512],
                                 ones_r[0:1, :],
                                 gbb[0:1, half_o * 512:(half_o + 1) * 512],
                                 start=False, stop=True)
                for dc in range(DC):
                    r = t.transpose(psAb[:, i * 1024 + dc * P: i * 1024 + (dc + 1) * P],
                                    y_sb[dc][:, tt * P:(tt + 1) * P], idb[:])
            return r
        sched.op("tensor", g1)
        sched.bar()

        def g2(sc):
            r = None
            for i in range(2):
                r = sc.activation(out=SG4[i][:], in_=psB[:, i * 1024:(i + 1) * 1024],
                                  func=AF.Sigmoid)
            return r
        sched.op("scalar", g2)
        sched.bar()

        def g3(v, cnt=cnt):
            v.wait_ge(ld_sem, cnt[0])
            r = None
            for i in range(2):
                v.tensor_copy(out=YT2[i][:], in_=psAb[:, i * 1024:(i + 1) * 1024])
                v.tensor_tensor(out=V4[i][:], in0=W4[i][:], in1=dm1b[:], op=ALU.mult)
                v.tensor_tensor(out=V4[i][:], in0=V4[i][:],
                                in1=YT2[i][:], op=ALU.add)
                v.tensor_tensor(out=V4[i][:], in0=V4[i][:], in1=SG4[i][:], op=ALU.mult)
                v.tensor_tensor(out=V4[i][:], in0=V4[i][:], in1=W4[i][:], op=ALU.add)
                ln_stats_v(v, V4[i][:], ST4[i], VS)
            return v.tensor_copy(out=mkr[:, 0:1], in_=mkr[:, 0:1])
        sched.op("vector", g3)
        sched.bar()

        def g4(sc):
            r = None
            for i in range(2):
                r = ln_sqrt_s(sc, ST4[i])
            return r
        sched.op("scalar", g4)
        sched.bar()

        def g6(v):
            r = None
            for i in range(2):
                ln_coefs_v(v, ST4[i], VS)
                r = v.tensor_scalar(out=V4[i][:], in0=V4[i][:],
                                    scalar1=ST4[i][:, 40:41],
                                    scalar2=ST4[i][:, 48:49],
                                    op0=ALU.mult, op1=ALU.add)
            return r
        sched.op("vector", g6)
        sched.bar()

        def g7(v):
            r = None
            for i in range(2):
                v.tensor_tensor(out=V4[i][:], in0=V4[i][:], in1=lngb[:], op=ALU.mult)
                v.tensor_tensor(out=V4[i][:], in0=V4[i][:], in1=lnbb[:], op=ALU.add)
                r = v.tensor_tensor(out=V4[i][:], in0=V4[i][:], in1=U4[i][:], op=ALU.add)
            return r
        sched.op("vector", g7)
        sched.bar()

        prev_store = STD.marker()

        def g8(s, tiles=tiles, pv=prev_store):
            for i, tt in enumerate(tiles):
                STD.start(s, x1_t[tt], V4[i][:])
            STD.fill(pv)
        sched.op("sync", g8)
        sched.bar()

    if debug:
        def dumpT(s):
            STD.start(s, dbg_xnT, xnT[0][:])
            STD.start(s, dbg_y, y_sb[0][:])
            STD.start(s, dbg_sg, SG4[0][:])
        sched.op("sync", dumpT)
        sched.bar()

    def fin(g):
        g.wait_ge(st_sem, STD.count)
    sched.op("gpsimd", fin)

    streams = sched.build(ph)
    with nc.Block() as block:
        block.sync(streams["sync"])
        block.vector(streams["vector"])
        block.scalar(streams["scalar"])
        block.tensor(streams["tensor"])
        block.gpsimd(streams["gpsimd"])
    ctx.close()
    if fused:
        return {"x1": x1, "ident_f": ident_f, "ident_b": ident_b}
    return nc


def build_pmax(nc):
    x = nc.dram_tensor("x", [T, D], F32, kind="ExternalInput").ap()
    g2_row = nc.dram_tensor("g2_row", [1, D], F32, kind="ExternalInput").ap()
    b2_row = nc.dram_tensor("b2_row", [1, D], F32, kind="ExternalInput").ap()
    ident_f = nc.dram_tensor("ident_f", [P, P], F32, kind="ExternalInput").ap()
    pmax = nc.dram_tensor("pmax", [1, 1], F32, kind="ExternalOutput").ap()
    x_t = x.rearrange("(n p) d -> n p d", p=P)

    ctx = contextlib.ExitStack()
    sb = lambda name, shape, dt=F32: ctx.enter_context(nc.sbuf_tensor(name, shape, dt))
    W4 = [[sb(f"pw{pg}_{i}", [P, D]) for i in range(2)] for pg in range(2)]
    U4 = [[sb(f"pu{pg}_{i}", [P, D]) for i in range(2)] for pg in range(2)]
    ST4 = [[sb(f"pst{pg}_{i}", [P, 72]) for i in range(2)] for pg in range(2)]
    mark = sb("pmark", [P, 8])
    mkr = mark
    g2b = sb("pg2b", [P, D])
    b2b = sb("pb2b", [P, D])
    idf = sb("pidf", [P, P])
    amax = sb("pamax", [P, 16])
    scal = sb("pscal", [1, P])
    psA = ctx.enter_context(nc.psum_tensor("ppsA", [P, 512], F32))

    sched = Sched(nc)
    ld_sem = ctx.enter_context(nc.semaphore("LdP"))
    st_sem = ctx.enter_context(nc.semaphore("StP"))
    ph = ctx.enter_context(nc.semaphore("PhP"))
    vs_sem = ctx.enter_context(nc.semaphore("VsP"))
    LD = Dma(ld_sem)
    STD = Dma(st_sem)
    VS = VSync(vs_sem)

    sched.op("vector", lambda v: v.memset(mark[:, 0:1], 0.0))

    pcw = LD.marker()

    def pc(s):
        LD.start(s, g2b[:], g2_row.squeeze(0).partition_broadcast(P))
        LD.start(s, b2b[:], b2_row.squeeze(0).partition_broadcast(P))
        LD.start(s, idf[:], ident_f)
        LD.fill(pcw)
    sched.op("sync", pc)
    sched.bar()

    gcnt = {0: LD.marker()}

    def ppre(s):
        for i in range(2):
            LD.start(s, W4[0][i][:], x_t[i])
        LD.fill(gcnt[0])
    sched.op("sync", ppre)
    sched.bar()

    for g in range(9):
        pg = g % 2
        qg = (g - 1) % 2

        if g < 8:
            if g + 1 < 8:
                gcnt[g + 1] = LD.marker()

                def p0(s, g=g, cnt=gcnt[g + 1]):
                    for i, tt in enumerate([2 * (g + 1), 2 * (g + 1) + 1]):
                        LD.start(s, W4[(g + 1) % 2][i][:], x_t[tt])
                    LD.fill(cnt)
                sched.op("sync", p0)

            def p1(v, pg=pg, cnt=gcnt[g]):
                v.wait_ge(ld_sem, cnt[0])
                v.wait_ge(ld_sem, pcw[0])
                for i in range(2):
                    ln_stats_v(v, W4[pg][i][:], ST4[pg][i], VS)
                return v.tensor_copy(out=mkr[:, 0:1], in_=mkr[:, 0:1])
            sched.op("vector", p1)

        if g > 0:
            def p4(sc, qg=qg):
                r = None
                for i in range(2):
                    r = ln_sqrt_s(sc, ST4[qg][i])
                return r
            sched.op("scalar", p4)
        sched.bar()

        if g > 0:
            def p5(v, g=g - 1, qg=qg):
                for i in range(2):
                    ln_coefs_v(v, ST4[qg][i], VS)
                    v.tensor_scalar(out=U4[qg][i][:], in0=U4[qg][i][:],
                                    scalar1=ST4[qg][i][:, 40:41], scalar2=ST4[qg][i][:, 48:49],
                                    op0=ALU.mult, op1=ALU.add)
                    v.tensor_tensor(out=U4[qg][i][:], in0=U4[qg][i][:], in1=g2b[:], op=ALU.mult)
                    v.tensor_tensor(out=U4[qg][i][:], in0=U4[qg][i][:], in1=b2b[:], op=ALU.add)
                    VS.edge(v, v.tensor_reduce(out=amax[:, 8:9], in_=U4[qg][i][:], axis=AX.X,
                                               op=ALU.max, apply_absolute_value=True))
                    if g == 0 and i == 0:
                        VS.edge(v, v.tensor_copy(out=amax[:, 0:1], in_=amax[:, 8:9]))
                    else:
                        VS.edge(v, v.tensor_tensor(out=amax[:, 0:1], in0=amax[:, 0:1],
                                                   in1=amax[:, 8:9], op=ALU.max))
                return v.tensor_copy(out=mkr[:, 0:1], in_=mkr[:, 0:1])
            sched.op("vector", p5)

        if g < 8:
            def p2(sc, pg=pg):
                r = None
                for i in range(2):
                    r = ln_sqrt_s(sc, ST4[pg][i])
                return r
            sched.op("scalar", p2)
        sched.bar()

        if g < 8:
            def p3(v, pg=pg):
                for i in range(2):
                    ln_coefs_v(v, ST4[pg][i], VS)
                    v.tensor_scalar(out=U4[pg][i][:], in0=W4[pg][i][:],
                                    scalar1=ST4[pg][i][:, 40:41], scalar2=ST4[pg][i][:, 48:49],
                                    op0=ALU.mult, op1=ALU.add)
                    ln_stats_v(v, U4[pg][i][:], ST4[pg][i], VS)
                return v.tensor_copy(out=mkr[:, 0:1], in_=mkr[:, 0:1])
            sched.op("vector", p3)
            sched.bar()

    def pf0(t):
        return t.transpose(psA[0:1, 0:P], amax[:, 0:1], idf[:])
    sched.op("tensor", pf0)
    sched.bar()

    def pf1(v):
        return v.tensor_reduce(out=scal[:, 0:1], in_=psA[0:1, 0:P], axis=AX.X,
                               op=ALU.max)
    sched.op("vector", pf1)
    sched.bar()

    def pf2(s):
        STD.start(s, pmax, scal[:, 0:1])
    sched.op("sync", pf2)
    sched.bar()

    def pfin(g):
        g.wait_ge(st_sem, STD.count)
    sched.op("gpsimd", pfin)

    streams = sched.build(ph)
    with nc.Block() as block:
        block.sync(streams["sync"])
        block.vector(streams["vector"])
        block.scalar(streams["scalar"])
        block.tensor(streams["tensor"])
        block.gpsimd(streams["gpsimd"])
    ctx.close()
    return nc


def host_prep_l0(inputs, n_cores=8):
    f32 = np.float32
    x = np.ascontiguousarray(np.asarray(inputs["L0_attn_x"], f32))
    A_log = np.asarray(inputs["L0_attn_A_log"], f32)
    B_w = np.asarray(inputs["L0_attn_B_w"], f32)
    C_w = np.asarray(inputs["L0_attn_C_w"], f32)
    Dv = np.asarray(inputs["L0_attn_D"], f32)
    gate_w = np.asarray(inputs["L0_attn_gate_w"], f32)
    gate_b = np.asarray(inputs["L0_attn_gate_b"], f32)
    ln_g = np.asarray(inputs["L0_attn_ln_gamma"], f32)
    ln_b = np.asarray(inputs["L0_attn_ln_beta"], f32)
    w0 = np.asarray(inputs["L0_ffn_weight"], f32)
    g2 = np.asarray(inputs["L0_ffn_gamma"], f32)
    b2 = np.asarray(inputs["L0_ffn_beta"], f32)

    alog_r = np.ascontiguousarray(
        A_log.reshape(DC, P, NST).transpose(1, 0, 2).reshape(P, P))
    shared = {
        "alog_r": alog_r,
        "bwT": np.ascontiguousarray(B_w.T),
        "cwT": np.ascontiguousarray(C_w.T),
        "d_row": np.ascontiguousarray(Dv[None, :]),
        "gate_wT": np.ascontiguousarray(gate_w.T),
        "gate_b_row": np.ascontiguousarray(gate_b[None, :]),
        "ln_g_row": np.ascontiguousarray(ln_g[None, :]),
        "ln_b_row": np.ascontiguousarray(ln_b[None, :]),
        "l0w_T": np.ascontiguousarray(w0.T),
        "g2_row": np.ascontiguousarray(g2[None, :]),
        "b2_row": np.ascontiguousarray(b2[None, :]),
        "ident_f": np.eye(P, dtype=f32),
        "ident_b": np.eye(P, dtype=BF16NP),
        "ones_row": np.ones((1, P), dtype=f32),
    }
    maps = []
    for c in range(n_cores):
        m = dict(shared)
        m["x"] = np.ascontiguousarray(x[c])
        maps.append(m)
    return maps


def run_pmax(inputs, n_cores=8):
    f32 = np.float32
    x = np.asarray(inputs["L0_attn_x"], f32)
    nc = bass.Bass("TRN2", target_bir_lowering=False,
                   detect_race_conditions=False)
    build_pmax(nc)
    shared = {
        "g2_row": np.ascontiguousarray(np.asarray(inputs["L0_ffn_gamma"], f32)[None, :]),
        "b2_row": np.ascontiguousarray(np.asarray(inputs["L0_ffn_beta"], f32)[None, :]),
        "ident_f": np.eye(P, dtype=f32),
    }
    maps = [dict(shared, x=np.ascontiguousarray(x[c])) for c in range(n_cores)]
    res = run_bass_kernel_spmd(nc, maps, core_ids=list(range(n_cores)))
    pm = np.array([[float(np.asarray(r["pmax"]).ravel()[0]) for r in res.results]],
                  np.float32)
    return pm


def run_l0(inputs, n_cores=8, debug=False):
    gmax_row = run_pmax(inputs, n_cores)
    nc = bass.Bass("TRN2", target_bir_lowering=False,
                   detect_race_conditions=False)
    build_l0(nc, n_cores=n_cores, debug=debug)
    maps = host_prep_l0(inputs, n_cores)
    for m in maps:
        m["gmax_row"] = gmax_row
    res = run_bass_kernel_spmd(nc, maps, core_ids=list(range(n_cores)))
    if debug:
        return res
    return np.stack([np.asarray(r["x1"]) for r in res.results], 0)


def kernel(**inputs):
    x1 = run_l0(inputs, n_cores=8)
    return x1.astype(np.float32)


HEADS = 16
DH = 64
LATD = 512
LHEADS = 8
NLAT = 64
WIN = 256
INNER = 4096
IC = INNER // P


def build_l1(nc, debug=False, shared=None):

    def din(name, shape, dt=F32):
        return nc.dram_tensor(name, shape, dt, kind="ExternalInput").ap()

    dsc = ((lambda n, s, dt=F32: nc.dram_tensor(n, s, dt, kind="ExternalOutput").ap())
           if debug else
           (lambda n, s, dt=F32: nc.dram_tensor(n, s, dt, kind="Internal").ap()))

    if shared is None:
        x = din("x", [T, D])
    else:
        x = shared["x1"]
    out = nc.dram_tensor("out", [T, D], F32, kind="ExternalOutput").ap()
    wq_d = din("wq", [D, D])
    wk_d = din("wk", [D, D])
    wv_d = din("wv", [D, D])
    wgq_d = din("wgq", [D, LATD])
    wgk_d = din("wgk", [D, LATD])
    wgv_d = din("wgv", [D, LATD])
    rout_d = din("r_out_w", [LATD, D])
    wout_d = din("w_out", [D, D])
    bdgl_d = din("bd_glw", [D, HEADS])
    bdgr_d = din("bd_grw", [D, HEADS])
    wgT_d = din("w_gT", [D, INNER])
    wuT_d = din("w_uT", [D, INNER])
    wdT_d = din("w_dT", [INNER, D])
    if shared is None:
        ident_f = din("ident_f", [P, P])
        ident_b = din("ident_b", [P, P], BF16)
    else:
        ident_f = shared["ident_f"]
        ident_b = shared["ident_b"]
    poolm_d = din("poolm", [T, NLAT], BF16)
    masks_d = din("masks", [4, P, 512])
    lmask_d = din("lmask", [T, NLAT])

    h_sp = dsc("h_sp", [T, INNER])
    x2a_sp = dsc("x2a_sp", [T, D])
    wdq_sp = dsc("wdq_sp", [INNER, D], BF16)
    if debug:
        dbg_hqT = nc.dram_tensor("dbg_hqT", [P, T], BF16, kind="ExternalOutput").ap()
        dbg_rh = nc.dram_tensor("dbg_rh", [P, NT * 8], F32, kind="ExternalOutput").ap()

    x_t = x.rearrange("(n p) d -> n p d", p=P)
    out_t = out.rearrange("(n p) d -> n p d", p=P)
    h_t = h_sp.rearrange("(n p) d -> n p d", p=P)
    x2a_t = x2a_sp.rearrange("(n p) d -> n p d", p=P)
    lmask_t = lmask_d.rearrange("(n p) l -> n p l", p=P)
    poolm_t = poolm_d.rearrange("(n p) l -> n p l", p=P)

    ctx = contextlib.ExitStack()
    sb = lambda name, shape, dt=F32: ctx.enter_context(nc.sbuf_tensor(name, shape, dt))

    xnT = [sb(f"l1xnT{c}", [P, T], BF16) for c in range(DC)]
    VTK = [sb(f"l1v{c}", [P, T], BF16) for c in range(DC)]
    KQT = [sb(f"l1kq{c}", [P, T], BF16) for c in range(DC)]
    QT = [sb(f"l1qt{c}", [P, T], BF16) for c in range(DC)]
    GQT = [sb(f"l1gqt{c}", [P, T], BF16) for c in range(4)]
    LPT = [sb(f"l1lpt{c}", [P, T], BF16) for c in range(4)]
    WB = [sb(f"l1wb{c}", [P, D], BF16) for c in range(DC)]
    W2 = [sb(f"l1w2_{i}", [P, D]) for i in range(2)]
    U2 = [sb(f"l1u2_{i}", [P, D]) for i in range(2)]
    _v2s = sb("l1v2s", [P, D])
    V2 = [_v2s, _v2s]
    XB2 = [sb(f"l1xb{i}", [P, D], BF16) for i in range(2)]
    HB2 = [sb(f"l1hb{i}", [P, D], BF16) for i in range(2)]
    ST2 = [sb(f"l1st{i}", [P, 72]) for i in range(2)]
    mark = sb("l1mark", [P, 48])
    mkr = sb("l1mkr", [P, 8])
    idf = sb("l1idf", [P, P])
    idb = sb("l1idb", [P, P], BF16)
    RS = sb("l1rs", [P, NT * 8])
    RSQ = sb("l1rsq", [P, NT * 8])
    RH = sb("l1rh", [P, NT * 8])
    RHQ = sb("l1rhq", [P, NT * 8])
    CGU = sb("l1cgu", [P, NT * 8])
    scw = sb("l1scw", [1, 32])
    scwb = sb("l1scwb", [P, 8])
    sc_d2 = dsc("l1sc_d", [1, 8])
    wacc = sb("l1wacc", [P, IC * 8])
    pooledT = [sb(f"l1plT{c}", [P, NLAT], BF16) for c in range(DC)]
    GKT = [sb(f"l1gkt{c}", [P, NLAT], BF16) for c in range(4)]
    GV = sb("l1gv", [P, LATD], BF16)

    LPS = sb("l1lps", [NLAT, 512], BF16)
    glt = [sb(f"l1glt{i}", [P, 2 * HEADS]) for i in range(NT)]
    rc = sb("l1rc", [P, HEADS * 8])
    glr = sb("l1glr", [P, HEADS])
    _msk = sb("l1msk", [P, 512])
    msk = [_msk, _msk]
    _lm = sb("l1lmsk", [P, NLAT])
    lmsk2 = [_lm, _lm]
    pmT2 = [sb(f"l1pmT{i}", [P, NLAT], BF16) for i in range(2)]
    sS = sb("l1sS", [P, 512])
    pS = sb("l1pS", [P, 512], BF16)

    glsb = sb("l1glsb", [HEADS, P])

    def PTv(h):
        return xnT[h // 4][:, (h % 4) * 512:(h % 4) * 512 + 512]

    def rem2v(qc):
        return xnT[6][:, qc * D:(qc + 1) * D]

    def combv(qc):
        return xnT[7][:, qc * D:(qc + 1) * D]

    def cTv(qc, dc):
        return xnT[4][:, (qc * DC + dc) * P:(qc * DC + dc + 1) * P]

    def vtile(tt):
        return VTK[tt // 2][:, (tt % 2) * D:(tt % 2 + 1) * D]

    def hqTv(k, set4):
        return QT[k // 8][:, (k % 8) * 256:(k % 8) * 256 + 256]

    psA = ctx.enter_context(nc.psum_tensor("l1psA", [P, T], F32))
    psB = ctx.enter_context(nc.psum_tensor("l1psB", [P, T], F32))
    psAb = psA[:].bitcast(BF16)
    psBb = psB[:].bitcast(BF16)

    sched = Sched(nc)
    ld_sem = ctx.enter_context(nc.semaphore("LdL1"))
    st_sem = ctx.enter_context(nc.semaphore("StL1"))
    ph = ctx.enter_context(nc.semaphore("PhL1"))
    vs_sem = ctx.enter_context(nc.semaphore("VsL1"))
    LD = Dma(ld_sem)
    STD = Dma(st_sem)
    VS = VSync(vs_sem)

    sched.op("vector", lambda v: v.memset(mkr[:, 0:1], 0.0))

    cw = LD.marker()

    def l1c0(s):
        LD.start(s, idf[:], ident_f)
        LD.start(s, idb[:], ident_b)
        LD.fill(cw)
    sched.op("sync", l1c0)
    sched.bar()

    for g in range(8):
        tiles = [2 * g, 2 * g + 1]
        cnt = LD.marker()

        def a0(s, tiles=tiles, cnt=cnt, g=g):
            for i, tt in enumerate(tiles):
                LD.start(s, W2[i][:], x_t[tt])
                LD.start(s, pmT2[i][:], poolm_t[tt])
            LD.fill(cnt)
        sched.op("sync", a0)
        sched.bar()

        def a1(v, cnt=cnt):
            v.wait_ge(ld_sem, cnt[0])
            for i in range(2):
                ln_stats_v(v, W2[i][:], ST2[i], VS)
            return v.tensor_copy(out=mkr[:, 0:1], in_=mkr[:, 0:1])
        sched.op("vector", a1)
        sched.bar()

        def a2(sc):
            r = None
            for i in range(2):
                r = ln_sqrt_s(sc, ST2[i])
            return r
        sched.op("scalar", a2)
        sched.bar()

        def a4(v, tiles=tiles):
            r = None
            for i, tt in enumerate(tiles):
                ln_coefs_v(v, ST2[i], VS)
                v.tensor_scalar(out=U2[i][:], in0=W2[i][:],
                                scalar1=ST2[i][:, 40:41], scalar2=ST2[i][:, 48:49],
                                op0=ALU.mult, op1=ALU.add)
                v.tensor_copy(out=XB2[i][:], in_=U2[i][:])
                VS.edge(v, v.tensor_reduce(out=RS[:, tt * 8:tt * 8 + 1], in_=U2[i][:],
                                           axis=AX.X, op=ALU.max,
                                           apply_absolute_value=True))
                VS.edge(v, v.tensor_scalar(out=RS[:, tt * 8:tt * 8 + 1],
                                           in0=RS[:, tt * 8:tt * 8 + 1],
                                           scalar1=1e-5, scalar2=1.0 / 127.0,
                                           op0=ALU.max, op1=ALU.mult))
                VS.edge(v, v.reciprocal(out=RSQ[:, tt * 8:tt * 8 + 1],
                                        in_=RS[:, tt * 8:tt * 8 + 1]))
            return v.tensor_copy(out=mkr[:, 0:1], in_=mkr[:, 0:1])
        sched.op("vector", a4)
        sched.bar()

        def a5(v, tiles=tiles):
            r = None
            for i, tt in enumerate(tiles):
                v.tensor_scalar(out=V2[i][:], in0=U2[i][:],
                                scalar1=RSQ[:, tt * 8:tt * 8 + 1], scalar2=127.0,
                                op0=ALU.mult, op1=ALU.min)
                v.tensor_scalar(out=U2[i][:].bitcast(I32), in0=V2[i][:],
                                scalar1=-128.0, scalar2=None, op0=ALU.max)
                v.tensor_copy(out=V2[i][:], in_=U2[i][:].bitcast(I32))
                r = v.tensor_copy(out=W2[i][:].bitcast(BF16)[:, 0:D], in_=V2[i][:])
            return r
        sched.op("vector", a5)

        def a5p(t, tiles=tiles):
            r = None
            for i, tt in enumerate(tiles):
                for half in range(2):
                    r = t.matmul(psB[0:NLAT, half * 512:(half + 1) * 512],
                                 pmT2[i][:], XB2[i][:, half * 512:(half + 1) * 512],
                                 start=(tt == 0), stop=(tt == T // P - 1))
            return r
        sched.op("tensor", a5p)
        sched.bar()

        def a8t(t):
            r = None
            for i in range(2):
                for dc in range(DC):
                    r = t.transpose(psAb[:, (i * DC + dc) * P:(i * DC + dc + 1) * P],
                                    XB2[i][:, dc * P:(dc + 1) * P], idb[:])
            return r
        sched.op("tensor", a8t)
        sched.bar()

        def a9e(v, tiles=tiles):
            r = None
            for i, tt in enumerate(tiles):
                for dc in range(DC):
                    r = v.tensor_copy(out=xnT[dc][:, tt * P:(tt + 1) * P],
                                      in_=psAb[:, (i * DC + dc) * P:(i * DC + dc + 1) * P])
            return r
        sched.op("vector", a9e)
        sched.bar()

        def a10t(t):
            r = None
            for i in range(2):
                for dc in range(DC):
                    r = t.transpose(psAb[:, (i * DC + dc) * P:(i * DC + dc + 1) * P],
                                    W2[i][:].bitcast(BF16)[:, dc * P:(dc + 1) * P], idb[:])
            return r
        sched.op("tensor", a10t)
        sched.bar()

        def a11e(v, tiles=tiles):
            r = None
            for i, tt in enumerate(tiles):
                for dc in range(DC):
                    r = v.tensor_copy(out=KQT[dc][:, tt * P:(tt + 1) * P],
                                      in_=psAb[:, (i * DC + dc) * P:(i * DC + dc + 1) * P])
            return r
        sched.op("vector", a11e)
        sched.bar()

    def b0(v):
        return v.tensor_copy(out=W2[0][:].bitcast(BF16)[0:NLAT, 0:D],
                             in_=psB[0:NLAT, 0:D])
    sched.op("vector", b0)
    sched.bar()

    def b1(t):
        r = None
        for dc in range(DC):
            r = t.transpose(psAb[:, dc * NLAT:(dc + 1) * NLAT],
                            W2[0][:].bitcast(BF16)[0:NLAT, dc * P:(dc + 1) * P],
                            idb[0:NLAT, 0:NLAT])
        return r
    sched.op("tensor", b1)
    sched.bar()

    def b2(v):
        r = None
        for dc in range(DC):
            r = v.tensor_copy(out=pooledT[dc][:], in_=psAb[:, dc * NLAT:(dc + 1) * NLAT])
        return r
    sched.op("vector", b2)
    sched.bar()

    def stage_weight(wten, ncols, tag):
        for grp in range(4):
            cnt = LD.marker()

            def wl(s, grp=grp, cnt=cnt, wten=wten, ncols=ncols):
                for i in range(2):
                    LD.start(s, W2[i][:, 0:ncols],
                             wten.rearrange("(c p) o -> c p o", p=P)[grp * 2 + i])
                LD.fill(cnt)
            sched.op("sync", wl)
            sched.bar()

            def wc(v, grp=grp, cnt=cnt, ncols=ncols):
                v.wait_ge(ld_sem, cnt[0])
                r = None
                for i in range(2):
                    r = v.tensor_copy(out=WB[grp * 2 + i][:, 0:ncols],
                                      in_=W2[i][:, 0:ncols])
                return r
            sched.op("vector", wc)
            sched.bar()

    stage_weight(wgk_d, LATD, "wgk")

    def gk(t):
        r = None
        for mc in range(4):
            for dc in range(DC):
                r = t.matmul(psA[:, mc * NLAT:(mc + 1) * NLAT],
                             WB[dc][:, mc * P:(mc + 1) * P], pooledT[dc][:],
                             start=(dc == 0), stop=(dc == DC - 1))
        return r
    sched.op("tensor", gk)
    sched.bar()

    def gke(v):
        r = None
        for mc in range(4):
            r = v.tensor_copy(out=GKT[mc][:], in_=psA[:, mc * NLAT:(mc + 1) * NLAT])
        return r
    sched.op("vector", gke)
    sched.bar()

    stage_weight(wgv_d, LATD, "wgv")

    def gvm(t):
        r = None
        for dc in range(DC):
            r = t.matmul(psA[0:NLAT, 0:LATD], pooledT[dc][:], WB[dc][:, 0:LATD],
                         start=(dc == 0), stop=(dc == DC - 1))
        return r
    sched.op("tensor", gvm)
    sched.bar()

    def gve(v):
        return v.tensor_copy(out=GV[0:NLAT, :], in_=psA[0:NLAT, 0:LATD])
    sched.op("vector", gve)
    sched.bar()

    gvd_mark = STD.marker()

    def gvd(s, pv=gvd_mark):
        STD.start(s, GV[64:128, :], GV[0:NLAT, :])
        STD.fill(pv)
    sched.op("sync", gvd)
    sched.bar()

    def wabs_wide(wten, widx):
        for grp in range(4):
            for half in range(2):
                cnt = LD.marker()

                def wl3(s, grp=grp, half=half, cnt=cnt, wten=wten):
                    for i in range(2):
                        for seg in range(2):
                            LD.start(s, (W2 if seg == 0 else U2)[i][:, 0:1024],
                                     wten.rearrange("(c p) o -> c p o", p=P)[grp * 2 + i][:, (half * 2 + seg) * 1024:(half * 2 + seg + 1) * 1024])
                    LD.fill(cnt)
                sched.op("sync", wl3)
                sched.bar()

                def wa3(sc, grp=grp, half=half, cnt=cnt, widx=widx):
                    sc.wait_ge(ld_sem, cnt[0])
                    r = None
                    for i in range(2):
                        k = grp * 2 + i
                        r = sc.activation(out=W2[i][:, 0:1024], in_=W2[i][:, 0:1024],
                                          func=AF.Abs,
                                          accum_out=wacc[:, (k * 4 + half * 2) * 8:(k * 4 + half * 2) * 8 + 1])
                        r = sc.activation(out=U2[i][:, 0:1024], in_=U2[i][:, 0:1024],
                                          func=AF.Abs,
                                          accum_out=wacc[:, (k * 4 + half * 2 + 1) * 8:(k * 4 + half * 2 + 1) * 8 + 1])
                    return r
                sched.op("scalar", wa3)
                sched.bar()

        def wred(v, widx=widx):
            VS.edge(v, v.tensor_reduce(
                out=mark[:, 8:9],
                in_=wacc[:].rearrange("p (k e) -> p k e", e=8)[:, :, 0:1].squeeze(2),
                axis=AX.X, op=ALU.add))
            return v.tensor_copy(out=mkr[:, 0:1], in_=mkr[:, 0:1])
        sched.op("vector", wred)
        sched.bar()

        def wtr(t, widx=widx):
            return t.transpose(psA[0:1, 0:P], mark[:, 8:9], idf[:])
        sched.op("tensor", wtr)
        sched.bar()

        def wfin(v, widx=widx):
            VS.edge(v, v.tensor_reduce(out=scw[:, widx * 8:widx * 8 + 1],
                                       in_=psA[0:1, 0:P], axis=AX.X, op=ALU.add))
            VS.edge(v, v.tensor_scalar(out=scw[:, widx * 8:widx * 8 + 1],
                                       in0=scw[:, widx * 8:widx * 8 + 1],
                                       scalar1=1.0 / (D * INNER), scalar2=1e-5,
                                       op0=ALU.mult, op1=ALU.max))
            VS.edge(v, v.reciprocal(out=scw[:, widx * 8 + 1:widx * 8 + 2],
                                    in_=scw[:, widx * 8:widx * 8 + 1]))
            return v.tensor_copy(out=mkr[:, 0:1], in_=mkr[:, 0:1])
        sched.op("vector", wfin)
        sched.bar()

    wabs_wide(wgT_d, 0)
    wabs_wide(wuT_d, 1)

    for grp in range(16):
        cnt = LD.marker()

        def wl4(s, grp=grp, cnt=cnt):
            for i in range(2):
                LD.start(s, W2[i][:, 0:1024],
                         wdT_d.rearrange("(c p) o -> c p o", p=P)[grp * 2 + i])
            LD.fill(cnt)
        sched.op("sync", wl4)
        sched.bar()

        def wa4(sc, grp=grp, cnt=cnt):
            sc.wait_ge(ld_sem, cnt[0])
            r = None
            for i in range(2):
                k = grp * 2 + i
                r = sc.activation(out=W2[i][:, 0:1024], in_=W2[i][:, 0:1024],
                                  func=AF.Abs, accum_out=wacc[:, k * 8:k * 8 + 1])
            return r
        sched.op("scalar", wa4)
        sched.bar()

    def wred4(v):
        VS.edge(v, v.tensor_reduce(
            out=mark[:, 8:9],
            in_=wacc[:].rearrange("p (k e) -> p k e", e=8)[:, :, 0:1].squeeze(2),
            axis=AX.X, op=ALU.add))
        return v.tensor_copy(out=mkr[:, 0:1], in_=mkr[:, 0:1])
    sched.op("vector", wred4)
    sched.bar()

    def wtr4(t):
        return t.transpose(psA[0:1, 0:P], mark[:, 8:9], idf[:])
    sched.op("tensor", wtr4)
    sched.bar()

    def wfin4(v):
        VS.edge(v, v.tensor_reduce(out=scw[:, 16:17], in_=psA[0:1, 0:P],
                                   axis=AX.X, op=ALU.add))
        VS.edge(v, v.tensor_scalar(out=scw[:, 16:17], in0=scw[:, 16:17],
                                   scalar1=1.0 / (D * INNER), scalar2=1e-5,
                                   op0=ALU.mult, op1=ALU.max))
        VS.edge(v, v.reciprocal(out=scw[:, 17:18], in_=scw[:, 16:17]))
        return v.tensor_copy(out=mkr[:, 0:1], in_=mkr[:, 0:1])
    sched.op("vector", wfin4)
    sched.bar()

    cnt_swst = STD.marker()

    def swst(s):
        STD.start(s, sc_d2[:, 0:1], scw[:, 1:2])
        STD.start(s, sc_d2[:, 1:2], scw[:, 9:10])
        STD.start(s, sc_d2[:, 2:3], scw[:, 17:18])
        STD.start(s, sc_d2[:, 3:4], scw[:, 0:1])
        STD.start(s, sc_d2[:, 4:5], scw[:, 8:9])
        STD.start(s, sc_d2[:, 5:6], scw[:, 16:17])
        STD.fill(cnt_swst)
    sched.op("sync", swst)
    sched.bar()
    cnt_swld = LD.marker()

    def swld(s):
        s.wait_ge(st_sem, cnt_swst[0])
        for j in range(6):
            LD.start(s, scwb[:, j:j + 1], sc_d2[:, j:j + 1].squeeze(0).partition_broadcast(P))
        LD.fill(cnt_swld)
    sched.op("sync", swld)
    sched.bar()

    def cgu0(v):
        v.wait_ge(ld_sem, cnt_swld[0])
        r = None
        for tt in range(NT):
            r = VS.edge(v, v.tensor_tensor(out=CGU[:, tt * 8:tt * 8 + 1],
                                           in0=RS[:, tt * 8:tt * 8 + 1],
                                           in1=scwb[:, 3:4], op=ALU.mult))
            r = VS.edge(v, v.tensor_tensor(out=CGU[:, tt * 8 + 1:tt * 8 + 2],
                                           in0=RS[:, tt * 8:tt * 8 + 1],
                                           in1=scwb[:, 4:5], op=ALU.mult))
        return v.tensor_copy(out=mkr[:, 0:1], in_=mkr[:, 0:1])
    sched.op("vector", cgu0)
    sched.bar()

    c2_mark = STD.marker()
    for oc in range(8):
        chunks = [(wsel, wten, grp) for wsel, wten in ((0, wgT_d), (1, wuT_d))
                  for grp in range(4)]
        ccnt = {}
        for c in range(len(chunks) + 1):
            if c < len(chunks):
                wsel, wten, grp = chunks[c]
                ccnt[c] = LD.marker()

                def c2l(s, grp=grp, oc=oc, wten=wten, cnt=ccnt[c], pc=c % 2):
                    for i in range(2):
                        LD.start(s, W2[i][:, pc * 512:pc * 512 + 512],
                                 wten.rearrange("(c p) o -> c p o", p=P)[grp * 2 + i][:, oc * 512:(oc + 1) * 512])
                    LD.fill(cnt)
                sched.op("sync", c2l)

            if c > 0:
                wsel, wten, grp = chunks[c - 1]

                def c2q(v, grp=grp, wsel=wsel, cnt=ccnt[c - 1], pv=c2_mark,
                        pc=(c - 1) % 2):
                    v.wait_ge(ld_sem, cnt[0])
                    v.wait_ge(st_sem, pv[0])
                    r = None
                    for i in range(2):
                        v.tensor_scalar(out=U2[i][:, 0:512],
                                        in0=W2[i][:, pc * 512:pc * 512 + 512],
                                        scalar1=scwb[:, wsel:wsel + 1], scalar2=1.49,
                                        op0=ALU.mult, op1=ALU.min)
                        v.tensor_scalar(out=V2[i][:].bitcast(I32)[:, 0:512],
                                        in0=U2[i][:, 0:512],
                                        scalar1=-1.49, scalar2=None, op0=ALU.max)
                        v.tensor_copy(out=U2[i][:, 0:512], in_=V2[i][:].bitcast(I32)[:, 0:512])
                        r = v.tensor_copy(out=WB[grp * 2 + i][:, wsel * 512:(wsel + 1) * 512],
                                          in_=U2[i][:, 0:512])
                    return r
                sched.op("vector", c2q)
            sched.bar()

        def c2dst(tt):
            j = tt % 4
            if j < 2:
                return U2[j][:, 0:512]
            return _v2s[:, (j - 2) * 512:(j - 1) * 512]

        for s in range(NT + 3):
            if s < NT:
                def c2m(t, tt=s, oc=oc):
                    r = None
                    b = (tt % 4) * 512
                    for dc in range(DC):
                        r = t.matmul(psA[:, b:b + 512],
                                     KQT[dc][:, tt * P:(tt + 1) * P],
                                     WB[dc][:, 0:512],
                                     start=(dc == 0), stop=(dc == DC - 1))
                    for dc in range(DC):
                        r = t.matmul(psB[:, b:b + 512],
                                     KQT[dc][:, tt * P:(tt + 1) * P],
                                     WB[dc][:, 512:1024],
                                     start=(dc == 0), stop=(dc == DC - 1))
                    return r
                sched.op("tensor", c2m)

            if 0 <= s - 1 < NT:
                def c2s(sc, tt=s - 1, pv=c2_mark):
                    sc.wait_ge(st_sem, pv[0])
                    b = (tt % 4) * 512
                    return sc.activation(out=c2dst(tt), in_=psA[:, b:b + 512],
                                         func=AF.Silu, scale=CGU[:, tt * 8:tt * 8 + 1])
                sched.op("scalar", c2s)

            if 0 <= s - 2 < NT:
                def c2h(v, tt=s - 2, oc=oc):
                    b = (tt % 4) * 512
                    dst = c2dst(tt)
                    v.tensor_scalar(out=W2[tt % 2][:, 0:512], in0=psA[:, b:b + 512] if False else psB[:, b:b + 512],
                                    scalar1=CGU[:, tt * 8 + 1:tt * 8 + 2], scalar2=None,
                                    op0=ALU.mult)
                    v.tensor_tensor(out=dst, in0=dst,
                                    in1=W2[tt % 2][:, 0:512], op=ALU.mult)
                    VS.edge(v, v.tensor_reduce(out=mark[:, 16:17], in_=dst,
                                               axis=AX.X, op=ALU.max,
                                               apply_absolute_value=True))
                    if oc == 0:
                        VS.edge(v, v.tensor_copy(out=RH[:, tt * 8:tt * 8 + 1],
                                                 in_=mark[:, 16:17]))
                    else:
                        VS.edge(v, v.tensor_tensor(out=RH[:, tt * 8:tt * 8 + 1],
                                                   in0=RH[:, tt * 8:tt * 8 + 1],
                                                   in1=mark[:, 16:17], op=ALU.max))
                    return v.tensor_copy(out=mkr[:, 0:1], in_=mkr[:, 0:1])
                sched.op("vector", c2h)

            if 0 <= s - 3 < NT:
                c2_mark = STD.marker()

                def c2st(st_, tt=s - 3, oc=oc, pv=c2_mark):
                    STD.start(st_, h_t[tt][:, oc * 512:(oc + 1) * 512], c2dst(tt))
                    STD.fill(pv)
                sched.op("sync", c2st)
            sched.bar()

    def c2f(v):
        r = None
        for tt in range(NT):
            VS.edge(v, v.tensor_scalar(out=RH[:, tt * 8:tt * 8 + 1],
                                       in0=RH[:, tt * 8:tt * 8 + 1],
                                       scalar1=1e-5, scalar2=1.0 / 127.0,
                                       op0=ALU.max, op1=ALU.mult))
            VS.edge(v, v.reciprocal(out=RHQ[:, tt * 8:tt * 8 + 1],
                                    in_=RH[:, tt * 8:tt * 8 + 1]))
            r = VS.edge(v, v.tensor_tensor(out=RH[:, tt * 8 + 1:tt * 8 + 2],
                                           in0=RH[:, tt * 8:tt * 8 + 1],
                                           in1=scwb[:, 5:6], op=ALU.mult))
        return v.tensor_copy(out=mkr[:, 0:1], in_=mkr[:, 0:1])
    sched.op("vector", c2f)
    sched.bar()

    c3_mark = STD.marker()
    for grp in range(16):
        cnt = LD.marker()

        def c3l(s, grp=grp, cnt=cnt):
            for i in range(2):
                LD.start(s, W2[i][:, 0:1024],
                         wdT_d.rearrange("(c p) o -> c p o", p=P)[grp * 2 + i])
            LD.fill(cnt)
        sched.op("sync", c3l)
        sched.bar()

        def c3q(v, grp=grp, cnt=cnt, pv=c3_mark):
            v.wait_ge(ld_sem, cnt[0])
            v.wait_ge(st_sem, pv[0])
            r = None
            for i in range(2):
                v.tensor_scalar(out=U2[i][:], in0=W2[i][:],
                                scalar1=scwb[:, 2:3], scalar2=1.49,
                                op0=ALU.mult, op1=ALU.min)
                v.tensor_scalar(out=V2[i][:].bitcast(I32), in0=U2[i][:],
                                scalar1=-1.49, scalar2=None, op0=ALU.max)
                v.tensor_copy(out=U2[i][:], in_=V2[i][:].bitcast(I32))
                r = v.tensor_copy(out=XB2[i][:], in_=U2[i][:])
            return r
        sched.op("vector", c3q)
        sched.bar()

        c3_mark = STD.marker()

        def c3s(s, grp=grp, pv=c3_mark):
            for i in range(2):
                STD.start(s, wdq_sp.rearrange("(c p) o -> c p o", p=P)[grp * 2 + i],
                          XB2[i][:])
            STD.fill(pv)
        sched.op("sync", c3s)
        sched.bar()

    cnt_wdq_done = STD.marker()

    def c4m0(s):
        STD.fill(cnt_wdq_done)
    sched.op("sync", c4m0)
    sched.bar()

    c4_mark = STD.marker()
    for s2 in range(8):
        tset = [2 * s2, 2 * s2 + 1]
        cnt = LD.marker()

        def c4l(s, tset=tset, cnt=cnt, s2=s2, pv=c4_mark):
            if s2 == 0:
                s.wait_ge(st_sem, cnt_wdq_done[0])
            s.wait_ge(st_sem, pv[0])
            for i, tt in enumerate(tset):
                LD.start(s, W2[i][:], h_t[tt][:, 0:1024])
                LD.start(s, U2[i][:], h_t[tt][:, 1024:2048])
            LD.fill(cnt)
        sched.op("sync", c4l)
        sched.bar()
        cnt2 = LD.marker()
        for half in range(2):
            def c4l2(s, tset=tset, cnt2=cnt2):
                for i, tt in enumerate(tset):
                    LD.start(s, W2[i][:], h_t[tt][:, 2048:3072])
                    LD.start(s, U2[i][:], h_t[tt][:, 3072:4096])
                LD.fill(cnt2)
            def c4q(v, tset=tset, half=half, cnt=cnt, cnt2=cnt2):
                v.wait_ge(ld_sem, (cnt if half == 0 else cnt2)[0])
                r = None
                for i, tt in enumerate(tset):
                    for seg, buf in ((0, W2[i]), (1, U2[i])):
                        v.tensor_scalar(out=V2[i][:], in0=buf[:],
                                        scalar1=RHQ[:, tt * 8:tt * 8 + 1], scalar2=127.0,
                                        op0=ALU.mult, op1=ALU.min)
                        v.tensor_scalar(out=buf[:].bitcast(I32), in0=V2[i][:],
                                        scalar1=-128.0, scalar2=None, op0=ALU.max)
                        v.tensor_copy(out=V2[i][:], in_=buf[:].bitcast(I32))
                        r = v.tensor_copy(out=(XB2 if seg == 0 else HB2)[i][:],
                                          in_=V2[i][:])
                return r
            sched.op("vector", c4q)
            sched.bar()

            def c4t(t, tset=tset, half=half):
                r = None
                for i in range(2):
                    for seg in range(2):
                        srct = (XB2 if seg == 0 else HB2)[i][:]
                        for dc in range(DC):
                            r = t.transpose(
                                psAb[:, ((i * 2 + seg) * DC + dc) * P:((i * 2 + seg) * DC + dc + 1) * P],
                                srct[:, dc * P:(dc + 1) * P], idb[:])
                return r
            sched.op("tensor", c4t)
            if half == 0:
                sched.op("sync", c4l2)
            sched.bar()

            def c4e(v, tset=tset, half=half):
                r = None
                for i in range(2):
                    for seg in range(2):
                        for dc in range(DC):
                            k = (half * 2 + seg) * DC + dc
                            r = v.tensor_copy(
                                out=QT[k // 8][:, (k % 8) * 256 + i * P:(k % 8) * 256 + (i + 1) * P],
                                in_=psAb[:, ((i * 2 + seg) * DC + dc) * P:((i * 2 + seg) * DC + dc + 1) * P])
                return r
            sched.op("vector", c4e)
            sched.bar()
        ck = {0: LD.marker()}

        def c4w0(s, cnt3=ck[0]):
            for c in range(4):
                LD.start(s, WB[c][:], wdq_sp.rearrange("(c p) o -> c p o", p=P)[c])
            LD.fill(cnt3)
        sched.op("sync", c4w0)
        sched.bar()

        for kp in range(16):
            if kp + 2 < 16:
                ck[kp + 2] = LD.marker()

                def c4wl(s, kp=kp, cnt3=ck[kp + 2]):
                    for c in (2 * kp + 4, 2 * kp + 5):
                        LD.start(s, WB[c % 8][:],
                                 wdq_sp.rearrange("(c p) o -> c p o", p=P)[c])
                    LD.fill(cnt3)
                sched.op("sync", c4wl)
            if kp == 12:
                cnt4 = LD.marker()

                def c4xl(s, tset=tset, cnt4=cnt4):
                    for i, tt in enumerate(tset):
                        LD.start(s, W2[i][:], x_t[tt])
                    LD.fill(cnt4)
                sched.op("sync", c4xl)

            def c4mm(t, tset=tset, kp=kp, cnt3=ck[0 if kp < 2 else kp]):
                t.wait_ge(ld_sem, cnt3[0])
                r = None
                for k in (2 * kp, 2 * kp + 1):
                    for i, tt in enumerate(tset):
                        for half_o in range(2):
                            r = t.matmul(
                                psB[:, (i * 2 + half_o) * 512:(i * 2 + half_o + 1) * 512],
                                QT[k // 8][:, (k % 8) * 256 + i * P:(k % 8) * 256 + (i + 1) * P],
                                WB[k % 8][:, half_o * 512:(half_o + 1) * 512],
                                start=(k == 0), stop=(k == 31))
                return r
            sched.op("tensor", c4mm)
            sched.bar()

        def c4f(v, tset=tset, cnt4=cnt4):
            v.wait_ge(ld_sem, cnt4[0])
            r = None
            for i, tt in enumerate(tset):
                r = v.scalar_tensor_tensor(out=U2[i][:], in0=psB[:, i * 1024:(i + 1) * 1024],
                                           scalar=RH[:, tt * 8 + 1:tt * 8 + 2],
                                           in1=W2[i][:], op0=ALU.mult, op1=ALU.add)
            return r
        sched.op("vector", c4f)
        sched.bar()

        c4_mark = STD.marker()

        def c4st(s, tset=tset, pv=c4_mark, s2=s2):
            for i, tt in enumerate(tset):
                STD.start(s, x2a_t[tt], U2[i][:])
            if debug and s2 == 7:
                STD.start(s, dbg_hqT, QT[0][:])
                STD.start(s, dbg_rh, RH[:])
            STD.fill(pv)
        sched.op("sync", c4st)
        sched.bar()

    def qkv(dst_feature, dst, wten):
        stage_weight(wten, D, "w")
        for oc in range(DC):
            def mm1(t, oc=oc, dst=dst, dst_feature=dst_feature):
                r = None
                if dst_feature:
                    for tq in range(4):
                        for dc in range(DC):
                            r = t.matmul(psA[:, tq * 512:(tq + 1) * 512],
                                         WB[dc][:, oc * P:(oc + 1) * P],
                                         xnT[dc][:, tq * 512:(tq + 1) * 512],
                                         start=(dc == 0), stop=(dc == DC - 1))
                else:
                    for half in range(2):
                        for dc in range(DC):
                            r = t.matmul(psA[:, half * 512:(half + 1) * 512],
                                         xnT[dc][:, (oc * 2) * P:(oc * 2 + 1) * P],
                                         WB[dc][:, half * 512:(half + 1) * 512],
                                         start=(dc == 0), stop=(dc == DC - 1))
                        for dc in range(DC):
                            r = t.matmul(psA[:, 1024 + half * 512:1024 + (half + 1) * 512],
                                         xnT[dc][:, (oc * 2 + 1) * P:(oc * 2 + 2) * P],
                                         WB[dc][:, half * 512:(half + 1) * 512],
                                         start=(dc == 0), stop=(dc == DC - 1))
                return r
            sched.op("tensor", mm1)
            sched.bar()

            def ev1(v, oc=oc, dst=dst, dst_feature=dst_feature):
                r = None
                if dst_feature:
                    r = v.tensor_copy(out=dst[oc][:], in_=psA[:, 0:T])
                else:
                    r = v.tensor_copy(out=vtile(oc * 2), in_=psA[:, 0:D])
                    r = v.tensor_copy(out=vtile(oc * 2 + 1), in_=psA[:, D:2 * D])
                return r
            sched.op("vector", ev1)
            sched.bar()

    qkv(True, QT, wq_d)
    qkv(True, KQT, wk_d)
    qkv(False, None, wv_d)

    stage_weight(wgq_d, LATD, "wgq")
    for mc in range(4):
        def gqm(t, mc=mc):
            r = None
            for tq in range(4):
                for dc in range(DC):
                    r = t.matmul(psA[:, tq * 512:(tq + 1) * 512],
                                 WB[dc][:, mc * P:(mc + 1) * P],
                                 xnT[dc][:, tq * 512:(tq + 1) * 512],
                                 start=(dc == 0), stop=(dc == DC - 1))
            return r
        sched.op("tensor", gqm)
        sched.bar()

        def gqe(v, mc=mc):
            return v.tensor_copy(out=GQT[mc][:], in_=psA[:, 0:T])
        sched.op("vector", gqe)
        sched.bar()

    cnt_bd = LD.marker()

    def bdl(s):
        for dc in range(DC):
            LD.start(s, W2[0][:, dc * HEADS:(dc + 1) * HEADS],
                     bdgl_d.rearrange("(c p) h -> c p h", p=P)[dc])
            LD.start(s, W2[1][:, dc * HEADS:(dc + 1) * HEADS],
                     bdgr_d.rearrange("(c p) h -> c p h", p=P)[dc])
        LD.fill(cnt_bd)
    sched.op("sync", bdl)
    sched.bar()

    def bdc(v):
        v.wait_ge(ld_sem, cnt_bd[0])
        r = v.tensor_copy(out=XB2[0][:, 0:DC * HEADS], in_=W2[0][:, 0:DC * HEADS])
        r = v.tensor_copy(out=XB2[1][:, 0:DC * HEADS], in_=W2[1][:, 0:DC * HEADS])
        return r
    sched.op("vector", bdc)
    sched.bar()

    for sel in range(2):
        def glm(t, sel=sel):
            r = None
            for half in range(4):
                for oc in range(DC):
                    r = t.matmul(psA[0:HEADS, half * 512:(half + 1) * 512],
                                 XB2[sel][:, oc * HEADS:(oc + 1) * HEADS],
                                 QT[oc][:, half * 512:(half + 1) * 512],
                                 start=(oc == 0), stop=(oc == DC - 1))
            return r
        sched.op("tensor", glm)
        sched.bar()

        for tb8 in range(16):
            def gls(sc, sel=sel, tb8=tb8):
                r = None
                for k in range(1):
                    tt = tb8 + k
                    r = sc.activation(out=glsb[:, k * P:(k + 1) * P],
                                      in_=psA[0:HEADS, tt * P:(tt + 1) * P],
                                      func=AF.Sigmoid)
                return r
            sched.op("scalar", gls)
            sched.bar()

            def glt_t(t, sel=sel, tb8=tb8):
                r = None
                for k in range(1):
                    r = t.transpose(psB[:, 1024 + k * HEADS:1024 + (k + 1) * HEADS],
                                    glsb[0:HEADS, k * P:(k + 1) * P],
                                    idf[0:HEADS, 0:HEADS])
                return r
            sched.op("tensor", glt_t)
            sched.bar()

            def glt_e(v, sel=sel, tb8=tb8):
                r = None
                for k in range(1):
                    tt = tb8 + k
                    r = v.tensor_copy(out=glt[tt][:, sel * HEADS:(sel + 1) * HEADS],
                                      in_=psB[:, 1024 + k * HEADS:1024 + (k + 1) * HEADS])
                return r
            sched.op("vector", glt_e)
            sched.bar()

    lps_mark = {0: STD.marker(), 1: STD.marker()}
    NLH = LHEADS
    for tt in range(NT):
        cntl = LD.marker()

        def g0(s, tt=tt, cntl=cntl):
            LD.start(s, lmsk2[0][:], lmask_t[tt])
            LD.fill(cntl)

        for k in range(NLH + 5):
            if k == 0:
                sched.op("sync", g0)
            def gp_t(t, tt=tt, k=k):
                r = None
                lh = k
                if lh < NLH:
                    base = (lh % 2) * 64
                    r = t.matmul(psA[:, (lh % 3) * 512:(lh % 3) * 512 + NLAT],
                                 GQT[lh // 2][base:base + 64, tt * P:(tt + 1) * P],
                                 GKT[lh // 2][base:base + 64, 0:NLAT],
                                 start=True, stop=True)
                lh = k - 4
                if 0 <= lh < NLH:
                    r = t.transpose(psBb[0:NLAT, 2048 + (lh % 2) * 1024:2048 + (lh % 2) * 1024 + P],
                                    pS[:, (lh % 3) * NLAT:(lh % 3 + 1) * NLAT],
                                    idb[:])
                return r
            sched.op("tensor", gp_t)

            wmk = lps_mark[tt % 2]

            def gp_v(v, tt=tt, k=k, cntl=cntl, wmk=wmk):
                r = None
                lh = k - 1
                if 0 <= lh < NLH:
                    v.wait_ge(ld_sem, cntl[0])
                    v.scalar_tensor_tensor(out=sS[:, (lh % 2) * NLAT:(lh % 2) * NLAT + NLAT],
                                           in0=psA[:, (lh % 3) * 512:(lh % 3) * 512 + NLAT],
                                           scalar=0.125, in1=lmsk2[0][:],
                                           op0=ALU.mult, op1=ALU.add)
                    VS.edge(v, v.tensor_reduce(out=mark[:, 16 + (lh % 2) * 16:17 + (lh % 2) * 16],
                                               in_=sS[:, (lh % 2) * NLAT:(lh % 2) * NLAT + NLAT],
                                               axis=AX.X, op=ALU.max))
                    VS.edge(v, v.tensor_scalar(out=mark[:, 24 + (lh % 2) * 16:25 + (lh % 2) * 16],
                                               in0=mark[:, 16 + (lh % 2) * 16:17 + (lh % 2) * 16],
                                               scalar1=-1.0, scalar2=None, op0=ALU.mult))
                lh = k - 3
                if 0 <= lh < NLH:
                    VS.edge(v, v.reciprocal(out=mark[:, 8:9], in_=rc[:, lh * 8:lh * 8 + 1]))
                    VS.edge(v, v.tensor_scalar(out=pS[:, (lh % 3) * NLAT:(lh % 3 + 1) * NLAT],
                                               in0=pS[:, (lh % 3) * NLAT:(lh % 3 + 1) * NLAT],
                                               scalar1=mark[:, 8:9], scalar2=None,
                                               op0=ALU.mult))
                lh = k - 5
                if 0 <= lh < NLH:
                    if lh % 2 == 0:
                        r = v.tensor_copy(out=LPT[lh // 2][0:NLAT, tt * P:(tt + 1) * P],
                                          in_=psBb[0:NLAT, 2048 + (lh % 2) * 1024:2048 + (lh % 2) * 1024 + P])
                    else:
                        v.wait_ge(st_sem, wmk[0])
                        r = v.tensor_copy(out=LPS[0:NLAT, (tt % 2) * P:(tt % 2 + 1) * P],
                                          in_=psBb[0:NLAT, 2048 + (lh % 2) * 1024:2048 + (lh % 2) * 1024 + P])
                if r is None:
                    r = v.tensor_copy(out=mkr[:, 0:1], in_=mkr[:, 0:1])
                return r
            sched.op("vector", gp_v)

            def gp_s(sc, tt=tt, k=k):
                lh = k - 2
                if 0 <= lh < NLH:
                    return sc.activation(out=pS[:, (lh % 3) * NLAT:(lh % 3 + 1) * NLAT],
                                         in_=sS[:, (lh % 2) * NLAT:(lh % 2) * NLAT + NLAT],
                                         func=AF.Exp,
                                         bias=mark[:, 24 + (lh % 2) * 16:25 + (lh % 2) * 16],
                                         accum_out=rc[:, lh * 8:lh * 8 + 1])
                return None
            sched.op("scalar", gp_s)

            lh6 = k - 6
            if 0 <= lh6 < NLH and lh6 % 2 == 1:
                lps_mark[tt % 2] = STD.marker()

                def g7s(s, tt=tt, lh=lh6, pv=lps_mark[tt % 2]):
                    STD.start(s, LPT[lh // 2][64:128, tt * P:(tt + 1) * P],
                              LPS[0:NLAT, (tt % 2) * P:(tt % 2 + 1) * P])
                    STD.fill(pv)
                sched.op("sync", g7s)
            sched.bar()

        lps_mark[tt % 2] = STD.marker()

        def g7z(s, tt=tt, pv=lps_mark[tt % 2]):
            STD.start(s, LPT[3][64:128, tt * P:(tt + 1) * P],
                      LPS[0:NLAT, (tt % 2) * P:(tt % 2 + 1) * P])
            STD.fill(pv)
        sched.op("sync", g7z)
        sched.bar()

    cnt_lp = STD.marker()

    def h0(s):
        STD.fill(cnt_lp)
    sched.op("sync", h0)
    sched.bar()

    h3_mark = STD.marker()
    for lh in range(LHEADS):
        def h1(t, lh=lh):
            r = None
            base = (lh % 2) * 64
            for tq in range(4):
                r = t.matmul(psA[0:64, tq * 512:(tq + 1) * 512],
                             GV[base:base + 64, lh * 64:(lh + 1) * 64],
                             LPT[lh // 2][base:base + 64, tq * 512:(tq + 1) * 512],
                             start=True, stop=True)
            return r
        if lh == 0:
            def h1w(t, lh=lh, pvg=gvd_mark):
                t.wait_ge(st_sem, pvg[0])
                t.wait_ge(st_sem, cnt_lp[0])
                return h1(t)
            sched.op("tensor", h1w)
        else:
            sched.op("tensor", h1)
        sched.bar()

        if lh % 2 == 0:
            def h2(v, lh=lh):
                r = None
                for tq in range(4):
                    r = v.tensor_copy(out=GQT[lh // 2][0:64, tq * 512:(tq + 1) * 512],
                                      in_=psA[0:64, tq * 512:(tq + 1) * 512])
                return r
            sched.op("vector", h2)
            sched.bar()
        else:
            for tq in range(4):
                def h2b(v, lh=lh, tq=tq, pv=h3_mark):
                    v.wait_ge(st_sem, pv[0])
                    return v.tensor_copy(out=LPS[0:64, (tq % 2) * 256:(tq % 2) * 256 + 256],
                                         in_=psA[0:64, tq * 512:tq * 512 + 256])
                sched.op("vector", h2b)
                sched.bar()
                h3_mark = STD.marker()

                def h3(s, lh=lh, tq=tq, pv=h3_mark):
                    STD.start(s, GQT[lh // 2][64:128, tq * 512:tq * 512 + 256],
                              LPS[0:64, (tq % 2) * 256:(tq % 2) * 256 + 256])
                    STD.fill(pv)
                sched.op("sync", h3)
                sched.bar()
                def h2d(v, lh=lh, tq=tq, pv=h3_mark):
                    v.wait_ge(st_sem, pv[0])
                    return v.tensor_copy(out=LPS[0:64, (tq % 2) * 256:(tq % 2) * 256 + 256],
                                         in_=psA[0:64, tq * 512 + 256:tq * 512 + 512])
                sched.op("vector", h2d)
                sched.bar()
                h3_mark = STD.marker()

                def h3b(s, lh=lh, tq=tq, pv=h3_mark):
                    STD.start(s, GQT[lh // 2][64:128, tq * 512 + 256:tq * 512 + 512],
                              LPS[0:64, (tq % 2) * 256:(tq % 2) * 256 + 256])
                    STD.fill(pv)
                sched.op("sync", h3b)
                sched.bar()

    cnt_rem = STD.marker()

    def h4(s):
        STD.fill(cnt_rem)
    sched.op("sync", h4)
    sched.bar()

    for grp in range(2):
        cnt = LD.marker()

        def rol(s, grp=grp, cnt=cnt):
            s.wait_ge(st_sem, cnt_rem[0])
            for i in range(2):
                LD.start(s, W2[i][:], rout_d.rearrange("(c p) o -> c p o", p=P)[grp * 2 + i])
            LD.fill(cnt)
        sched.op("sync", rol)
        sched.bar()

        def roc(v, grp=grp, cnt=cnt):
            v.wait_ge(ld_sem, cnt[0])
            r = None
            for i in range(2):
                r = v.tensor_copy(out=LPT[grp * 2 + i][:, 0:D], in_=W2[i][:])
            return r
        sched.op("vector", roc)
        sched.bar()

    stage_weight(wout_d, D, "wout")

    xnT5f = xnT[5][:].bitcast(F32)

    def sSv(h):
        return sS[:, 0:512] if h % 2 == 0 else xnT5f[:, 0:512]

    def pS2(h):
        return pS[:, 0:512] if h % 2 == 0 else xnT[5][:, 1024:1536]

    j16_mark = STD.marker()
    for qb in range(8):
        key0 = 0 if qb == 0 else (qb - 1) * 256
        kc0 = key0 // P

        for qc in range(2):
            tt = qb * 2 + qc
            cntm = LD.marker()

            def j0(s, qb=qb, qc=qc, cntm=cntm):
                v0 = (0 if qb == 0 else 2) + qc
                LD.start(s, msk[0][:], masks_d[v0])
                LD.fill(cntm)

            for k in range(HEADS + 6):
                if k == 0:
                    sched.op("sync", j0)

                def jp_t(t, tt=tt, k=k, kc0=kc0, qc=qc):
                    r = None
                    if k == 0:
                        for kc in range(4):
                            for half in range(2):
                                r = t.matmul(psB[:, half * 512:(half + 1) * 512],
                                             GQT[kc][:, tt * P:(tt + 1) * P],
                                             LPT[kc][:, half * 512:(half + 1) * 512],
                                             start=(kc == 0), stop=(kc == 3))
                    h = k
                    if h < HEADS:
                        hb = (h % 2) * 64
                        r = t.matmul(psA[:, (h % 3) * 512:(h % 3 + 1) * 512],
                                     QT[h // 2][hb:hb + 64, tt * P:(tt + 1) * P],
                                     KQT[h // 2][hb:hb + 64, kc0 * P:kc0 * P + 512],
                                     start=True, stop=True)
                    h = k - 3
                    if 0 <= h < HEADS:
                        for kc in range(4):
                            r = t.transpose(
                                psBb[:, 2048 + (h % 2) * 1024 + kc * P:2048 + (h % 2) * 1024 + (kc + 1) * P],
                                pS2(h)[:, kc * P:(kc + 1) * P], idb[:])
                    h = k - 5
                    if 0 <= h < HEADS:
                        for kc in range(4):
                            r = t.matmul(psB[:, (h % 2) * 512:(h % 2) * 512 + DH],
                                         PTv(h)[:, kc * P:(kc + 1) * P],
                                         vtile(kc0 + kc)[:, h * DH:(h + 1) * DH],
                                         start=(kc == 0), stop=(kc == 3))
                    return r
                sched.op("tensor", jp_t)

                def jp_v(v, tt=tt, k=k, qc=qc, cntm=cntm):
                    r = None
                    if k == 1:
                        v.tensor_copy(out=rem2v(qc), in_=psB[:, 0:D])
                    h = k - 1
                    if 0 <= h < HEADS:
                        v.wait_ge(ld_sem, cntm[0])
                        v.scalar_tensor_tensor(out=sSv(h),
                                               in0=psA[:, (h % 3) * 512:(h % 3 + 1) * 512],
                                               scalar=0.125, in1=msk[0][:],
                                               op0=ALU.mult, op1=ALU.add)
                        VS.edge(v, v.tensor_reduce(
                            out=mark[:, 16 + (h % 2) * 16:17 + (h % 2) * 16],
                            in_=sSv(h), axis=AX.X, op=ALU.max))
                        VS.edge(v, v.tensor_scalar(
                            out=mark[:, 24 + (h % 2) * 16:25 + (h % 2) * 16],
                            in0=mark[:, 16 + (h % 2) * 16:17 + (h % 2) * 16],
                            scalar1=-1.0, scalar2=None, op0=ALU.mult))
                    h = k - 3
                    if 0 <= h < HEADS:
                        VS.edge(v, v.reciprocal(out=rc[:, h * 8:h * 8 + 1],
                                                in_=rc[:, h * 8 + 1:h * 8 + 2]))
                    h = k - 6
                    if 0 <= h < HEADS:
                        v.tensor_scalar(out=xnT5f[:, 768:768 + DH],
                                        in0=psB[:, (h % 2) * 512:(h % 2) * 512 + DH],
                                        scalar1=rc[:, h * 8:h * 8 + 1],
                                        scalar2=glt[tt][:, h:h + 1],
                                        op0=ALU.mult, op1=ALU.mult)
                        v.tensor_scalar(out=xnT5f[:, 832:832 + DH],
                                        in0=rem2v(qc)[:, h * DH:(h + 1) * DH],
                                        scalar1=glt[tt][:, HEADS + h:HEADS + h + 1],
                                        scalar2=None, op0=ALU.mult)
                        r = v.tensor_tensor(out=combv(qc)[:, h * DH:(h + 1) * DH],
                                            in0=xnT5f[:, 768:768 + DH],
                                            in1=xnT5f[:, 832:832 + DH], op=ALU.add)
                    if r is None:
                        r = v.tensor_copy(out=mkr[:, 0:1], in_=mkr[:, 0:1])
                    return r
                sched.op("vector", jp_v)

                def jp_s(sc, k=k):
                    r = None
                    h = k - 2
                    if 0 <= h < HEADS:
                        r = sc.activation(out=pS2(h), in_=sSv(h), func=AF.Exp,
                                          bias=mark[:, 24 + (h % 2) * 16:25 + (h % 2) * 16],
                                          accum_out=rc[:, h * 8 + 1:h * 8 + 2])
                    h = k - 4
                    if 0 <= h < HEADS:
                        r = sc.activation(out=PTv(h),
                                          in_=psBb[:, 2048 + (h % 2) * 1024:2048 + (h % 2) * 1024 + 512],
                                          func=AF.Copy, scale=1.0)
                    return r
                sched.op("scalar", jp_s)
                sched.bar()

            def j11(t, qc=qc):
                r = None
                for dc in range(DC):
                    r = t.transpose(psAb[:, 3 * 1024 + dc * P:3 * 1024 + (dc + 1) * P],
                                    combv(qc)[:, dc * P:(dc + 1) * P], idb[:])
                return r
            sched.op("tensor", j11)
            sched.bar()

            def j12(v, qc=qc):
                r = None
                for dc in range(DC):
                    r = v.tensor_copy(out=cTv(qc, dc),
                                      in_=psAb[:, 3 * 1024 + dc * P:3 * 1024 + (dc + 1) * P])
                return r
            sched.op("vector", j12)
            sched.bar()

            def j13(t, qc=qc):
                r = None
                for half in range(2):
                    for dc in range(DC):
                        r = t.matmul(psB[:, D + half * 512:D + (half + 1) * 512],
                                     cTv(qc, dc), WB[dc][:, half * 512:(half + 1) * 512],
                                     start=(dc == 0), stop=(dc == DC - 1))
                return r
            sched.op("tensor", j13)
            cntx = LD.marker()

            def j14(s, tt=tt, cntx=cntx):
                LD.start(s, W2[0][:], x2a_t[tt])
                LD.fill(cntx)
            sched.op("sync", j14)
            sched.bar()

            def j15(v, tt=tt, cntx=cntx, pv=j16_mark):
                v.wait_ge(ld_sem, cntx[0])
                v.wait_ge(st_sem, pv[0])
                return v.tensor_tensor(out=U2[0][:], in0=psB[:, D:2 * D],
                                       in1=W2[0][:], op=ALU.add)
            sched.op("vector", j15)
            sched.bar()

            j16_mark = STD.marker()

            def j16(s, tt=tt, pv=j16_mark):
                STD.start(s, out_t[tt], U2[0][:])
                STD.fill(pv)
            sched.op("sync", j16)
            sched.bar()

    def fin(g):
        g.wait_ge(st_sem, STD.count)
    sched.op("gpsimd", fin)

    streams = sched.build(ph)
    with nc.Block() as block:
        block.sync(streams["sync"])
        block.vector(streams["vector"])
        block.scalar(streams["scalar"])
        block.tensor(streams["tensor"])
        block.gpsimd(streams["gpsimd"])
    ctx.close()
    return nc


def host_prep_l1(inputs, x1, n_cores=8):
    f32 = np.float32
    gl_w = np.asarray(inputs["L1_attn_gate_local_w"], f32)
    gr_w = np.asarray(inputs["L1_attn_gate_remote_w"], f32)
    bd_gl = np.zeros((D, HEADS), f32)
    bd_gr = np.zeros((D, HEADS), f32)
    for h in range(HEADS):
        bd_gl[h * DH:(h + 1) * DH, h] = gl_w[h]
        bd_gr[h * DH:(h + 1) * DH, h] = gr_w[h]
    poolm = np.zeros((T, NLAT), f32)
    for l in range(NLAT):
        poolm[l * 32:(l + 1) * 32, l] = 1.0 / 32.0
    NEG = np.float32(-1e30)
    masks = np.full((4, P, 512), NEG, f32)
    for qc in range(2):
        for i in range(P):
            iq = qc * P + i
            lo = iq + 1
            hi = iq + 256
            masks[2 + qc, i, lo:hi + 1] = 0.0
            masks[qc, i, 0:iq + 1] = 0.0
    lmask = np.full((T, NLAT), NEG, f32)
    for t_ in range(T):
        lmask[t_, 0:t_ // 32 + 1] = 0.0
    shared = {
        "wq": np.ascontiguousarray(np.asarray(inputs["L1_attn_wq"], f32)),
        "wk": np.ascontiguousarray(np.asarray(inputs["L1_attn_wk"], f32)),
        "wv": np.ascontiguousarray(np.asarray(inputs["L1_attn_wv"], f32)),
        "wgq": np.ascontiguousarray(np.asarray(inputs["L1_attn_wgq"], f32)),
        "wgk": np.ascontiguousarray(np.asarray(inputs["L1_attn_wgk"], f32)),
        "wgv": np.ascontiguousarray(np.asarray(inputs["L1_attn_wgv"], f32)),
        "r_out_w": np.ascontiguousarray(np.asarray(inputs["L1_attn_r_out_w"], f32)),
        "w_out": np.ascontiguousarray(np.asarray(inputs["L1_attn_w_out"], f32)),
        "bd_glw": bd_gl,
        "bd_grw": bd_gr,
        "w_gT": np.ascontiguousarray(np.asarray(inputs["L1_ffn_w_gate"], f32).T),
        "w_uT": np.ascontiguousarray(np.asarray(inputs["L1_ffn_w_up"], f32).T),
        "w_dT": np.ascontiguousarray(np.asarray(inputs["L1_ffn_w_down"], f32).T),
        "ident_f": np.eye(P, dtype=f32),
        "ident_b": np.eye(P, dtype=BF16NP),
        "poolm": poolm.astype(BF16NP),
        "masks": masks,
        "lmask": lmask,
    }
    maps = []
    for c in range(n_cores):
        m = dict(shared)
        xc = np.asarray(x1[c], f32)
        if xc.shape == (T, D):
            m["x"] = np.ascontiguousarray(xc)
        maps.append(m)
    return maps


def run_l1(inputs, x1, n_cores=8, debug=False):
    nc = bass.Bass("TRN2", target_bir_lowering=False,
                   detect_race_conditions=False,
                   dynamic_dma_scratch_size=2048)
    build_l1(nc, debug=debug)
    maps = host_prep_l1(inputs, x1, n_cores)
    res = run_bass_kernel_spmd(nc, maps, core_ids=list(range(n_cores)))
    if debug:
        return res
    return np.stack([np.asarray(r["out"]) for r in res.results], 0)


def run_fused(inputs, n_cores=8):
    gmax_row = run_pmax(inputs, n_cores)
    nc = bass.Bass("TRN2", target_bir_lowering=False,
                   detect_race_conditions=False,
                   dynamic_dma_scratch_size=2048)
    shared = build_l0(nc, n_cores=n_cores, store_out=False, fused=True)
    build_l1(nc, shared=shared)
    maps0 = host_prep_l0(inputs, n_cores)
    maps1 = host_prep_l1(inputs, np.zeros((n_cores, 1, 1), np.float32), n_cores)
    maps = []
    for c in range(n_cores):
        m = dict(maps0[c])
        m1 = dict(maps1[c])
        for k in ("x", "ident_f", "ident_b"):
            m1.pop(k, None)
        m.update(m1)
        m["gmax_row"] = gmax_row
        maps.append(m)
    res = run_bass_kernel_spmd(nc, maps, core_ids=list(range(n_cores)))
    return np.stack([np.asarray(r["out"]) for r in res.results], 0)


def kernel(**inputs):
    out = run_fused(inputs, n_cores=8)
    return out.astype(np.float32)


# version 10 (final)
# speedup vs baseline: 1.4111x; 1.0133x over previous
import contextlib
import numpy as np
import ml_dtypes
import concourse.bass as bass
import concourse.mybir as mybir
from concourse.bass_utils import run_bass_kernel_spmd

F32 = mybir.dt.float32
BF16 = mybir.dt.bfloat16
I32 = mybir.dt.int32
AF = mybir.ActivationFunctionType
ALU = mybir.AluOpType
AX = mybir.AxisListType
BF16NP = ml_dtypes.bfloat16

P = 128
T = 2048
D = 1024
NT = T // P
DC = D // P
NST = 16
EPS = 1e-5

ENGINES = ("tensor", "vector", "scalar", "gpsimd", "sync")


class Sched:

    def __init__(self, nc):
        self.nc = nc
        self.items = []
        self.cur = 0

    def op(self, engine, fn):
        assert engine in ENGINES
        self.items.append((self.cur, engine, fn))

    def bar(self):
        self.cur += 1

    def build(self, ph):
        nph = self.cur + 1
        per = {e: {} for e in ENGINES}
        for p, e, f in self.items:
            per[e].setdefault(p, []).append(f)

        def mk(ename):
            def body(eng):
                for p in range(nph):
                    if p:
                        eng.wait_ge(ph, len(ENGINES) * p)
                    last = None
                    for f in per[ename].get(p, []):
                        r = f(eng)
                        if r is not None:
                            last = r
                    if last is not None and ename != "sync":
                        last.then_inc(ph, 1)
                    else:
                        eng.nop().then_inc(ph, 1)
            return body
        return {e: mk(e) for e in ENGINES}


class VSync:

    def __init__(self, sem):
        self.sem = sem
        self.n = 0

    def edge(self, eng, inst):
        inst.then_inc(self.sem, 1)
        self.n += 1
        eng.wait_ge(self.sem, self.n)
        return inst


class Dma:

    def __init__(self, sem):
        self.sem = sem
        self.count = 0

    def start(self, eng, out, in_, **kw):
        eng.dma_start(out=out, in_=in_, **kw).then_inc(self.sem, 16)
        self.count += 16
        return self.count

    def marker(self):
        return [0]

    def fill(self, m):
        m[0] = self.count




def ln_stats_v(v, src, ST, VS):
    VS.edge(v, v.bn_stats(out=ST[:, 56:62], in_=src[:, 0:512]))
    VS.edge(v, v.bn_stats(out=ST[:, 64:70], in_=src[:, 512:1024]))
    VS.edge(v, v.bn_aggr(out=ST[:, 0:2], in_=ST[:, 56:62]))
    VS.edge(v, v.bn_aggr(out=ST[:, 8:10], in_=ST[:, 64:70]))
    VS.edge(v, v.tensor_tensor(out=ST[:, 16:17], in0=ST[:, 0:1], in1=ST[:, 8:9],
                               op=ALU.subtract))
    VS.edge(v, v.tensor_tensor(out=ST[:, 16:17], in0=ST[:, 16:17], in1=ST[:, 16:17],
                               op=ALU.mult))
    VS.edge(v, v.tensor_tensor(out=ST[:, 24:25], in0=ST[:, 0:1], in1=ST[:, 8:9],
                               op=ALU.add))
    VS.edge(v, v.tensor_scalar(out=ST[:, 24:25], in0=ST[:, 24:25], scalar1=0.5,
                               scalar2=None, op0=ALU.mult))
    VS.edge(v, v.tensor_tensor(out=ST[:, 32:33], in0=ST[:, 1:2], in1=ST[:, 9:10],
                               op=ALU.add))
    VS.edge(v, v.tensor_scalar(out=ST[:, 16:17], in0=ST[:, 16:17], scalar1=0.25,
                               scalar2=None, op0=ALU.mult))
    VS.edge(v, v.scalar_tensor_tensor(out=ST[:, 32:33], in0=ST[:, 32:33], scalar=0.5,
                                      in1=ST[:, 16:17], op0=ALU.mult, op1=ALU.add))
    return VS.edge(v, v.tensor_scalar(out=ST[:, 40:41], in0=ST[:, 32:33], scalar1=EPS,
                                      scalar2=None, op0=ALU.add))


def ln_sqrt_s(sc, ST):
    return sc.activation(out=ST[:, 40:41], in_=ST[:, 40:41], func=AF.Sqrt)


def ln_coefs_v(v, ST, VS):
    VS.edge(v, v.reciprocal(out=ST[:, 40:41], in_=ST[:, 40:41]))
    VS.edge(v, v.tensor_tensor(out=ST[:, 48:49], in0=ST[:, 24:25], in1=ST[:, 40:41],
                               op=ALU.mult))
    return VS.edge(v, v.tensor_scalar(out=ST[:, 48:49], in0=ST[:, 48:49], scalar1=-1.0,
                                      scalar2=None, op0=ALU.mult))




def build_l0(nc, n_cores=8, store_out=True, debug=False, fused=False):

    def din(name, shape, dt=F32):
        return nc.dram_tensor(name, shape, dt, kind="ExternalInput").ap()

    def dint(name, shape, dt=F32):
        return nc.dram_tensor(name, shape, dt, kind="Internal").ap()

    x = din("x", [T, D])
    x1 = nc.dram_tensor("x1", [T, D], F32,
                        kind="ExternalOutput" if store_out else "Internal").ap()
    alog = din("alog_r", [P, P])
    bwT = din("bwT", [D, NST])
    cwT = din("cwT", [D, NST])
    d_row = din("d_row", [1, D])
    gwT = din("gate_wT", [D, D])
    gb_row = din("gate_b_row", [1, D])
    lng_row = din("ln_g_row", [1, D])
    lnb_row = din("ln_b_row", [1, D])
    l0wT = din("l0w_T", [D, D])
    g2_row = din("g2_row", [1, D])
    b2_row = din("b2_row", [1, D])
    ident_f = din("ident_f", [P, P])
    ident_b = din("ident_b", [P, P], BF16)
    ones_row = din("ones_row", [1, P])

    dsc = (lambda n, s, dt=F32: nc.dram_tensor(n, s, dt, kind="ExternalOutput").ap()) if debug else dint
    if debug:
        dbg_st = nc.dram_tensor("dbg_st", [P, 72], F32, kind="ExternalOutput").ap()
        dbg_xn = nc.dram_tensor("dbg_xn", [P, D], F32, kind="ExternalOutput").ap()
        dbg_xnT = nc.dram_tensor("dbg_xnT", [P, T], BF16, kind="ExternalOutput").ap()
        dbg_st1a = nc.dram_tensor("dbg_st1a", [P, 72], F32, kind="ExternalOutput").ap()
        dbg_st1b = nc.dram_tensor("dbg_st1b", [P, 72], F32, kind="ExternalOutput").ap()
        dbg_y = nc.dram_tensor("dbg_y", [P, T], BF16, kind="ExternalOutput").ap()
        dbg_sg = nc.dram_tensor("dbg_sg", [P, D], F32, kind="ExternalOutput").ap()
    xn_sp = dsc("xn_sp", [T, D])
    xn2_sp = dsc("xn2_sp", [T, D])
    x1a_sp = dsc("x1a_sp", [T, D])
    bxT_d = dsc("bxT_d", [NST, T], BF16)
    cxT_d = dsc("cxT_d", [NST, T], BF16)
    sc_d = dint("sc_d", [1, 4])
    gmax_row = din("gmax_row", [1, 8])

    x_t = x.rearrange("(n p) d -> n p d", p=P)
    x1_t = x1.rearrange("(n p) d -> n p d", p=P)
    xn_t = xn_sp.rearrange("(n p) d -> n p d", p=P)
    xn2_t = xn2_sp.rearrange("(n p) d -> n p d", p=P)
    x1a_t = x1a_sp.rearrange("(n p) d -> n p d", p=P)

    ctx = contextlib.ExitStack()
    sb = lambda name, shape, dt=F32: ctx.enter_context(nc.sbuf_tensor(name, shape, dt))

    xnT = [sb(f"xnT{c}", [P, T], BF16) for c in range(DC)]
    y_sb = [sb(f"ysb{c}", [P, T], BF16) for c in range(DC)]
    wqT = [sb(f"wqT{c}", [P, D], BF16) for c in range(DC)]
    gwTb = [sb(f"gwTb{c}", [P, D], BF16) for c in range(DC)]
    idf = sb("idf", [P, P])
    idb = sb("idb", [P, P], BF16)
    ones_r = sb("ones_r", [1, P])
    aexp = sb("aexp", [P, P])
    aexb = sb("aexb", [P, P], BF16)
    dm1b = sb("dm1b", [P, D])
    gbb = sb("gbb", [1, D])
    lngb = sb("lngb", [P, D])
    lnbb = sb("lnbb", [P, D])
    g2b = sb("g2b", [P, D])
    b2b = sb("b2b", [P, D])
    NW = 2
    W4 = [sb(f"w4_{i}", [P, D]) for i in range(NW)]
    XN4 = [sb(f"xn4_{i}", [P, D]) for i in range(NW)]
    XB4 = [sb(f"xb4_{i}", [P, D], BF16) for i in range(NW)]
    U4 = [sb(f"u4_{i}", [P, D]) for i in range(NW)]
    V4 = [sb(f"v4_{i}", [P, D]) for i in range(NW)]
    SG4 = [sb(f"sg4_{i}", [P, D]) for i in range(NW)]
    YT2 = [sb(f"yt2_{i}", [P, D]) for i in range(NW)]
    ST4 = [sb(f"st4_{i}", [P, 72]) for i in range(NW)]
    mark = sb("mark", [P, 8])
    mkr = mark
    qT_t = [sb(f"qTt{c}", [P, NW * P], BF16) for c in range(DC)]
    amax = sb("amax", [P, 16])
    wsum = sb("wsum", [P, 64])
    wmsk = sb("wmsk", [P, 16])
    wcnt = sb("wcnt", [P, 64])
    scal = sb("scal", [1, P])
    scol = sb("scol", [P, 4])
    bw_sb = sb("bw_sb", [P, P])
    bwb = sb("bwb", [P, P], BF16)
    cwb = sb("cwb", [P, P], BF16)
    TBW = 512
    bc2 = [sb(f"bc{i}", [P, TBW], BF16) for i in range(2)]
    cc2 = [sb(f"cc{i}", [P, TBW], BF16) for i in range(2)]
    gsc = [sb("gsc0", [P, TBW], BF16)] * 2
    hsc = [sb("hsc0", [P, TBW], BF16)] * 2
    psc = [sb("psc0", [P, DC * TBW], BF16)] * 2
    carry = sb("carry", [P, P])
    bx_sb = sb("bx_sb", [NST, T], BF16)
    cx_sb = sb("cx_sb", [NST, T], BF16)

    psA = ctx.enter_context(nc.psum_tensor("psA", [P, T], F32))
    psB = ctx.enter_context(nc.psum_tensor("psB", [P, T], F32))
    psAb = psA[:].bitcast(BF16)

    sched = Sched(nc)
    ld_sem = ctx.enter_context(nc.semaphore("LdL0"))
    st_sem = ctx.enter_context(nc.semaphore("StL0"))
    ph = ctx.enter_context(nc.semaphore("PhL0"))
    vs_sem = ctx.enter_context(nc.semaphore("VsL0"))
    LD = Dma(ld_sem)
    STD = Dma(st_sem)
    VS = VSync(vs_sem)

    sched.op("vector", lambda v: v.memset(mark[:, 0:1], 0.0))

    cw = LD.marker()

    def ph_consts(s):
        LD.start(s, idf[:], ident_f)
        LD.start(s, idb[:], ident_b)
        LD.start(s, ones_r[:], ones_row)
        LD.start(s, aexp[:], alog)
        LD.start(s, dm1b[:], d_row.squeeze(0).partition_broadcast(P))
        LD.start(s, gbb[:], gb_row)
        LD.start(s, lngb[:], lng_row.squeeze(0).partition_broadcast(P))
        LD.start(s, lnbb[:], lnb_row.squeeze(0).partition_broadcast(P))
        LD.start(s, g2b[:], g2_row.squeeze(0).partition_broadcast(P))
        LD.start(s, b2b[:], b2_row.squeeze(0).partition_broadcast(P))
        LD.start(s, bw_sb[:].rearrange("p (c n) -> p c n", n=NST), bwT.rearrange("(c p) n -> p c n", p=P))
        LD.fill(cw)
    sched.op("sync", ph_consts)
    sched.bar()

    def ph_c2v(v):
        v.wait_ge(ld_sem, cw[0])
        v.tensor_scalar(out=dm1b[:], in0=dm1b[:], scalar1=1.0, scalar2=None,
                        op0=ALU.subtract)
        v.tensor_copy(out=bwb[:], in_=bw_sb[:])
        return v.memset(carry[:], 0.0)
    sched.op("vector", ph_c2v)

    def ph_c2s(sc):
        sc.wait_ge(ld_sem, cw[0])
        return sc.activation(out=aexp[:], in_=aexp[:], func=AF.Exp)
    sched.op("scalar", ph_c2s)
    sched.bar()

    cw2 = LD.marker()

    def ph_c2l(s):
        LD.start(s, bw_sb[:].rearrange("p (c n) -> p c n", n=NST), cwT.rearrange("(c p) n -> p c n", p=P))
        LD.fill(cw2)
    sched.op("sync", ph_c2l)
    sched.bar()

    def ph_c3v(v):
        v.wait_ge(ld_sem, cw2[0])
        v.tensor_copy(out=cwb[:], in_=bw_sb[:])
        return v.tensor_copy(out=aexb[:], in_=aexp[:])
    sched.op("vector", ph_c3v)
    sched.bar()

    gwT_t = gwT.rearrange("(c p) e -> c p e", p=P)
    for grp in range(4):
        cnt = LD.marker()

        def ph_gwl(s, grp=grp, cnt=cnt):
            for i in range(2):
                LD.start(s, W4[i][:], gwT_t[grp * 2 + i])
            LD.fill(cnt)
        sched.op("sync", ph_gwl)
        sched.bar()

        def ph_gwc(v, grp=grp, cnt=cnt):
            v.wait_ge(ld_sem, cnt[0])
            r = None
            for i in range(2):
                r = v.tensor_copy(out=gwTb[grp * 2 + i][:], in_=W4[i][:])
            return r
        sched.op("vector", ph_gwc)
        sched.bar()

    prev_store = STD.marker()
    for g in range(8):
        tiles = [2 * g + i for i in range(2)]
        cnt = LD.marker()

        def a0(s, tiles=tiles, pv=prev_store, cnt=cnt):
            s.wait_ge(st_sem, pv[0])
            for i, tt in enumerate(tiles):
                LD.start(s, W4[i][:], x_t[tt])
            LD.fill(cnt)
        sched.op("sync", a0)
        sched.bar()

        def a1(v, cnt=cnt, g=g):
            v.wait_ge(ld_sem, cnt[0])
            if debug and g == 0:
                v.tensor_copy(out=SG4[0][:], in_=W4[0][:])
            for i in range(2):
                ln_stats_v(v, W4[i][:], ST4[i], VS)
            return v.tensor_copy(out=mkr[:, 0:1], in_=mkr[:, 0:1])
        sched.op("vector", a1)
        sched.bar()

        def a2(sc):
            r = None
            for i in range(2):
                r = ln_sqrt_s(sc, ST4[i])
            return r
        sched.op("scalar", a2)
        if debug and g == 0:
            def a2d(s):
                STD.start(s, dbg_st1a, ST4[1][:])
            sched.op("sync", a2d)
        sched.bar()

        def a4(v):
            r = None
            for i in range(2):
                ln_coefs_v(v, ST4[i], VS)
                r = v.tensor_scalar(out=XN4[i][:], in0=W4[i][:],
                                    scalar1=ST4[i][:, 40:41],
                                    scalar2=ST4[i][:, 48:49],
                                    op0=ALU.mult, op1=ALU.add)
            return r
        sched.op("vector", a4)

        if debug and g == 0:
            def a4snap(v):
                return v.tensor_copy(out=SG4[1][:, 0:72], in_=ST4[0][:])
            sched.op("vector", a4snap)
            sched.bar()
            def a4d(s):
                STD.start(s, dbg_st, SG4[1][:, 0:72])
                STD.start(s, dbg_xn, SG4[0][:])
            sched.op("sync", a4d)
        sched.bar()

        def a5(v):
            for i in range(2):
                v.tensor_copy(out=XB4[i][:], in_=XN4[i][:])
                ln_stats_v(v, XN4[i][:], ST4[i], VS)
            return v.tensor_copy(out=mkr[:, 0:1], in_=mkr[:, 0:1])
        sched.op("vector", a5)

        def a5s(s, tiles=tiles):
            for i, tt in enumerate(tiles):
                STD.start(s, xn_t[tt], XN4[i][:])
        sched.op("sync", a5s)
        sched.bar()

        def a6(sc):
            r = None
            for i in range(2):
                r = ln_sqrt_s(sc, ST4[i])
            return r
        sched.op("scalar", a6)

        def a6t(t, g=g):
            r = None
            for dc in range(DC):
                for i in range(2):
                    r = t.transpose(psAb[:, dc * 256 + i * P: dc * 256 + (i + 1) * P],
                                    XB4[i][:, dc * P:(dc + 1) * P], idb[:])
            return r
        sched.op("tensor", a6t)
        sched.bar()

        def a7(v, g=g):
            r = None
            for i in range(2):
                ln_coefs_v(v, ST4[i], VS)
            for dc in range(DC):
                r = v.tensor_copy(out=xnT[dc][:, g * 256:(g + 1) * 256],
                                  in_=psAb[:, dc * 256:(dc + 1) * 256])
            return r
        sched.op("vector", a7)
        sched.bar()

        def a8(v):
            r = None
            for i in range(2):
                r = v.tensor_scalar(out=U4[i][:], in0=XN4[i][:],
                                    scalar1=ST4[i][:, 40:41],
                                    scalar2=ST4[i][:, 48:49],
                                    op0=ALU.mult, op1=ALU.add)
            return r
        sched.op("vector", a8)
        sched.bar()

        def a9(v, g=g):
            r = None
            for i in range(2):
                v.tensor_tensor(out=U4[i][:], in0=U4[i][:], in1=g2b[:], op=ALU.mult)
                r = v.tensor_tensor(out=U4[i][:], in0=U4[i][:], in1=b2b[:], op=ALU.add)
            return r
        sched.op("vector", a9)
        sched.bar()

        prev_store = STD.marker()

        def a9s(s, tiles=tiles, pv=prev_store):
            for i, tt in enumerate(tiles):
                STD.start(s, xn2_t[tt], U4[i][:])
            STD.fill(pv)
        sched.op("sync", a9s)
        sched.bar()

    cnt_gmax = LD.marker()

    def b4(s):
        LD.start(s, scal[:, 0:8], gmax_row)
        LD.fill(cnt_gmax)
    sched.op("sync", b4)
    sched.bar()

    def b5(v):
        v.wait_ge(ld_sem, cnt_gmax[0])
        VS.edge(v, v.tensor_reduce(out=scal[:, 0:1], in_=scal[:, 0:8], axis=AX.X,
                                   op=ALU.max))
        VS.edge(v, v.tensor_scalar(out=scal[:, 0:1], in0=scal[:, 0:1], scalar1=1e-8,
                                   scalar2=1.0 / 127.0, op0=ALU.max, op1=ALU.mult))
        VS.edge(v, v.reciprocal(out=scal[:, 1:2], in_=scal[:, 0:1]))
        return v.tensor_copy(out=scal[:, 16:17], in_=scal[:, 1:2])
    sched.op("vector", b5)
    sched.bar()

    l0wT_t = l0wT.rearrange("(c p) o -> c p o", p=P)
    for grp in range(4):
        cnt = LD.marker()

        def c0(s, grp=grp, cnt=cnt):
            for i in range(2):
                LD.start(s, W4[i][:], l0wT_t[grp * 2 + i])
            LD.fill(cnt)
        sched.op("sync", c0)
        sched.bar()

        def c1(sc, grp=grp, cnt=cnt):
            sc.wait_ge(ld_sem, cnt[0])
            r = None
            for i in range(2):
                r = sc.activation(out=U4[i][:], in_=W4[i][:], func=AF.Abs,
                                  accum_out=wsum[:, (grp * 2 + i) * 8: (grp * 2 + i) * 8 + 1])
            return r
        sched.op("scalar", c1)
        sched.bar()

    def c2(v):
        return v.tensor_reduce(out=wmsk[:, 0:1],
                               in_=wsum[:].rearrange("p (k e) -> p k e", e=8)[:, :, 0:1].squeeze(2),
                               axis=AX.X, op=ALU.add)
    sched.op("vector", c2)
    sched.bar()

    def c3(t):
        return t.transpose(psA[0:1, 0:P], wmsk[:, 0:1], idf[:])
    sched.op("tensor", c3)
    sched.bar()

    def c4(v):
        VS.edge(v, v.tensor_reduce(out=scal[:, 2:3], in_=psA[0:1, 0:P], axis=AX.X,
                                   op=ALU.add))
        VS.edge(v, v.tensor_scalar(out=scal[:, 2:3], in0=scal[:, 2:3],
                                   scalar1=0.7 / (D * D), scalar2=None, op0=ALU.mult))
        return v.tensor_copy(out=scal[:, 16:17], in_=scal[:, 2:3])
    sched.op("vector", c4)
    sched.bar()

    cnt_c5 = STD.marker()

    def c5(s):
        STD.start(s, sc_d[:, 0:1], scal[:, 2:3])
        STD.fill(cnt_c5)
    sched.op("sync", c5)
    sched.bar()
    cnt_delta = LD.marker()

    def c6(s):
        s.wait_ge(st_sem, cnt_c5[0])
        LD.start(s, scol[:, 2:3], sc_d[:, 0:1].squeeze(0).partition_broadcast(P))
        LD.fill(cnt_delta)
    sched.op("sync", c6)
    sched.bar()

    for grp in range(4):
        cnt = LD.marker()

        def c7(s, grp=grp, cnt=cnt):
            for i in range(2):
                LD.start(s, W4[i][:], l0wT_t[grp * 2 + i])
            LD.fill(cnt)
        sched.op("sync", c7)
        sched.bar()

        def c8(sc, cnt=cnt):
            sc.wait_ge(ld_sem, cnt[0])
            sc.wait_ge(ld_sem, cnt_delta[0])
            r = None
            for i in range(2):
                sc.activation(out=U4[i][:], in_=W4[i][:], func=AF.Abs)
                r = sc.activation(out=V4[i][:], in_=W4[i][:], func=AF.Sign)
            return r
        sched.op("scalar", c8)
        sched.bar()

        def c9(v, grp=grp):
            r = None
            for i in range(2):
                k = grp * 2 + i
                v.tensor_scalar(out=SG4[i][:], in0=U4[i][:], scalar1=scol[:, 2:3],
                                scalar2=None, op0=ALU.is_gt)
                r = v.scalar_tensor_tensor(out=W4[i][:], in0=U4[i][:], scalar=1.0,
                                           in1=SG4[i][:], op0=ALU.mult,
                                           op1=ALU.mult, accum_out=wsum[:, k * 8:k * 8 + 1])
            return r
        sched.op("vector", c9)
        sched.bar()

        def c10(sc, grp=grp):
            r = None
            for i in range(2):
                k = grp * 2 + i
                r = sc.activation(out=SG4[i][:], in_=SG4[i][:], func=AF.Copy,
                                  accum_out=wcnt[:, k * 8:k * 8 + 1])
            return r
        sched.op("scalar", c10)
        sched.bar()

        def c11(v, grp=grp):
            r = None
            for i in range(2):
                k = grp * 2 + i
                r = v.tensor_tensor(out=wqT[k][:], in0=V4[i][:], in1=SG4[i][:],
                                    op=ALU.mult)
            return r
        sched.op("vector", c11)
        sched.bar()

    def c12(v):
        VS.edge(v, v.tensor_reduce(out=wmsk[:, 0:1],
                                   in_=wsum[:].rearrange("p (k e) -> p k e", e=8)[:, :, 0:1].squeeze(2),
                                   axis=AX.X, op=ALU.add))
        VS.edge(v, v.tensor_reduce(out=wmsk[:, 8:9],
                                   in_=wcnt[:].rearrange("p (k e) -> p k e", e=8)[:, :, 0:1].squeeze(2),
                                   axis=AX.X, op=ALU.add))
        return v.tensor_copy(out=mkr[:, 0:1], in_=mkr[:, 0:1])
    sched.op("vector", c12)
    sched.bar()

    def c13(t):
        t.transpose(psA[0:1, 0:P], wmsk[:, 0:1], idf[:])
        return t.transpose(psB[0:1, 0:P], wmsk[:, 8:9], idf[:])
    sched.op("tensor", c13)
    sched.bar()

    def c14(v):
        VS.edge(v, v.tensor_reduce(out=scal[:, 2:3], in_=psA[0:1, 0:P], axis=AX.X,
                                   op=ALU.add))
        VS.edge(v, v.tensor_reduce(out=scal[:, 3:4], in_=psB[0:1, 0:P], axis=AX.X,
                                   op=ALU.add))
        VS.edge(v, v.tensor_scalar(out=scal[:, 3:4], in0=scal[:, 3:4], scalar1=1.0,
                                   scalar2=None, op0=ALU.max))
        VS.edge(v, v.reciprocal(out=scal[:, 3:4], in_=scal[:, 3:4]))
        VS.edge(v, v.tensor_tensor(out=scal[:, 3:4], in0=scal[:, 3:4],
                                   in1=scal[:, 2:3], op=ALU.mult))
        VS.edge(v, v.tensor_tensor(out=scal[:, 3:4], in0=scal[:, 3:4],
                                   in1=scal[:, 0:1], op=ALU.mult))
        return v.tensor_copy(out=scal[:, 16:17], in_=scal[:, 3:4])
    sched.op("vector", c14)
    sched.bar()

    cnt_c15 = STD.marker()

    def c15(s):
        STD.start(s, sc_d[:, 1:2], scal[:, 1:2])
        STD.start(s, sc_d[:, 2:3], scal[:, 3:4])
        STD.fill(cnt_c15)
    sched.op("sync", c15)
    sched.bar()
    cnt_scales = LD.marker()

    def c16(s):
        s.wait_ge(st_sem, cnt_c15[0])
        LD.start(s, scol[:, 0:1], sc_d[:, 1:2].squeeze(0).partition_broadcast(P))
        LD.start(s, scol[:, 1:2], sc_d[:, 2:3].squeeze(0).partition_broadcast(P))
        LD.fill(cnt_scales)
    sched.op("sync", c16)
    sched.bar()

    for g in range(8):
        tiles = [2 * g + i for i in range(2)]
        cnt = LD.marker()

        def d0(s, tiles=tiles, pv=prev_store, cnt=cnt):
            s.wait_ge(st_sem, pv[0])
            for i, tt in enumerate(tiles):
                LD.start(s, U4[i][:], xn2_t[tt])
                LD.start(s, W4[i][:], x_t[tt])
            LD.fill(cnt)
        sched.op("sync", d0)
        sched.bar()

        def d1(v, cnt=cnt):
            v.wait_ge(ld_sem, cnt[0])
            v.wait_ge(ld_sem, cnt_scales[0])
            r = None
            for i in range(2):
                v.tensor_scalar(out=U4[i][:], in0=U4[i][:], scalar1=scol[:, 0:1],
                                scalar2=127.0, op0=ALU.mult, op1=ALU.min)
                v.tensor_scalar(out=V4[i][:].bitcast(I32), in0=U4[i][:],
                                scalar1=-127.0, scalar2=None, op0=ALU.max)
                v.tensor_copy(out=U4[i][:], in_=V4[i][:].bitcast(I32))
                r = v.tensor_copy(out=XB4[i][:], in_=U4[i][:])
            return r
        sched.op("vector", d1)
        sched.bar()

        def d2(t):
            r = None
            for i in range(2):
                for dc in range(DC):
                    r = t.transpose(psAb[:, (i * DC + dc) * P:(i * DC + dc + 1) * P],
                                    XB4[i][:, dc * P:(dc + 1) * P], idb[:])
            return r
        sched.op("tensor", d2)
        sched.bar()

        def d3(v):
            r = None
            for i in range(2):
                for dc in range(DC):
                    r = v.tensor_copy(out=qT_t[dc][:, i * P:(i + 1) * P],
                                      in_=psAb[:, (i * DC + dc) * P:(i * DC + dc + 1) * P])
            return r
        sched.op("vector", d3)
        sched.bar()

        def d4(t):
            r = None
            for i in range(2):
                for half_o in range(2):
                    for dc in range(DC):
                        r = t.matmul(psB[:, (i * 2 + half_o) * 512:(i * 2 + half_o + 1) * 512],
                                     qT_t[dc][:, i * P:(i + 1) * P],
                                     wqT[dc][:, half_o * 512:(half_o + 1) * 512],
                                     start=(dc == 0), stop=(dc == DC - 1))
            return r
        sched.op("tensor", d4)
        sched.bar()

        def d5(v):
            r = None
            for i in range(2):
                r = v.scalar_tensor_tensor(
                    out=V4[i][:], in0=psB[:, i * 1024:(i + 1) * 1024],
                    scalar=scol[:, 1:2], in1=W4[i][:],
                    op0=ALU.mult, op1=ALU.add)
            return r
        sched.op("vector", d5)
        sched.bar()

        prev_store = STD.marker()

        def d5s(s, tiles=tiles, pv=prev_store):
            for i, tt in enumerate(tiles):
                STD.start(s, x1a_t[tt], V4[i][:])
            STD.fill(pv)
        sched.op("sync", d5s)
        sched.bar()

    def e1(t):
        r = None
        for four in range(4):
            for dc in range(DC):
                r = t.matmul(psA[0:NST, four * 512:(four + 1) * 512],
                             bwb[:, dc * NST:(dc + 1) * NST],
                             xnT[dc][:, four * 512:(four + 1) * 512],
                             start=(dc == 0), stop=(dc == DC - 1))
        for four in range(4):
            for dc in range(DC):
                r = t.matmul(psB[0:NST, four * 512:(four + 1) * 512],
                             cwb[:, dc * NST:(dc + 1) * NST],
                             xnT[dc][:, four * 512:(four + 1) * 512],
                             start=(dc == 0), stop=(dc == DC - 1))
        return r
    sched.op("tensor", e1)
    sched.bar()

    def e2(v):
        v.tensor_copy(out=bx_sb[:], in_=psA[0:NST, 0:T])
        return v.tensor_copy(out=cx_sb[:], in_=psB[0:NST, 0:T])
    sched.op("vector", e2)
    sched.bar()

    cnt_bxcx = STD.marker()

    def e3(s):
        STD.start(s, bxT_d, bx_sb[:])
        STD.start(s, cxT_d, cx_sb[:])
        STD.fill(cnt_bxcx)
    sched.op("sync", e3)
    sched.bar()

    NTB = T // TBW
    NS = NTB * NST
    fcnt = {0: LD.marker()}

    def fpre(s):
        s.wait_ge(st_sem, cnt_bxcx[0])
        LD.start(s, bc2[0][:], bxT_d[0:1, 0:TBW].squeeze(0).partition_broadcast(P))
        LD.start(s, cc2[0][:], cxT_d[0:1, 0:TBW].squeeze(0).partition_broadcast(P))
        LD.fill(fcnt[0])
    sched.op("sync", fpre)
    sched.bar()

    def f2body(t, n, buf):
        r = None
        for dc in range(DC):
            ps = psA if dc < 4 else psB
            r = t.matmul(ps[:, (dc % 4) * 512:(dc % 4 + 1) * 512],
                         idb[:], psc[buf][:, dc * TBW:(dc + 1) * TBW],
                         start=(n == 0), stop=(n == NST - 1))
        return r

    for st in range(NS):
        tb, n = st // NST, st % NST
        buf = st % 2

        if st + 1 < NS:
            tb1, n1 = (st + 1) // NST, (st + 1) % NST
            fcnt[st + 1] = LD.marker()

            def fld(s, tb1=tb1, n1=n1, b1=(st + 1) % 2, cnt=fcnt[st + 1]):
                LD.start(s, bc2[b1][:],
                         bxT_d[n1:n1 + 1, tb1 * TBW:(tb1 + 1) * TBW].squeeze(0).partition_broadcast(P))
                LD.start(s, cc2[b1][:],
                         cxT_d[n1:n1 + 1, tb1 * TBW:(tb1 + 1) * TBW].squeeze(0).partition_broadcast(P))
                LD.fill(cnt)
            sched.op("sync", fld)

        def f1(v, tb=tb, n=n, buf=buf, cnt=fcnt[st]):
            if n == 0 and tb > 0:
                for dc in range(DC):
                    ps = psA if dc < 4 else psB
                    v.tensor_copy(out=y_sb[dc][:, (tb - 1) * TBW:tb * TBW],
                                  in_=ps[:, (dc % 4) * 512:(dc % 4 + 1) * 512])
            v.wait_ge(ld_sem, cnt[0])
            r = None
            for dc in range(DC):
                col = n * DC + dc
                v.tensor_tensor(out=gsc[buf][:],
                                in0=xnT[dc][:, tb * TBW:(tb + 1) * TBW],
                                in1=bc2[buf][:], op=ALU.mult)
                VS.edge(v, v.tensor_tensor_scan(
                    out=hsc[buf][:],
                    data0=aexb[:, dc * NST + n: dc * NST + n + 1].broadcast_to([P, TBW]),
                    data1=gsc[buf][:],
                    initial=carry[:, col:col + 1],
                    op0=ALU.mult, op1=ALU.add))
                VS.edge(v, v.tensor_copy(out=carry[:, col:col + 1],
                                         in_=hsc[buf][:, TBW - 1:TBW]))
                r = v.tensor_tensor(out=psc[buf][:, dc * TBW:(dc + 1) * TBW],
                                    in0=hsc[buf][:],
                                    in1=cc2[buf][:], op=ALU.mult)
            return r
        sched.op("vector", f1)

        if n > 0:
            def f2(t, n=n - 1, buf=(st - 1) % 2):
                return f2body(t, n, buf)
            sched.op("tensor", f2)
        sched.bar()

        if n == NST - 1:
            def f2e(t, n=n, buf=buf):
                return f2body(t, n, buf)
            sched.op("tensor", f2e)
            sched.bar()

    def f3z(v):
        r = None
        for dc in range(DC):
            ps = psA if dc < 4 else psB
            r = v.tensor_copy(out=y_sb[dc][:, (NTB - 1) * TBW:NTB * TBW],
                              in_=ps[:, (dc % 4) * 512:(dc % 4 + 1) * 512])
        return r
    sched.op("vector", f3z)
    sched.bar()

    for pair in range(8):
        tiles = [2 * pair, 2 * pair + 1]
        cnt = LD.marker()

        def g0(s, tiles=tiles, pv=prev_store, cnt=cnt):
            s.wait_ge(st_sem, pv[0])
            for i, tt in enumerate(tiles):
                LD.start(s, W4[i][:], xn_t[tt])
                LD.start(s, U4[i][:], x1a_t[tt])
            LD.fill(cnt)
        sched.op("sync", g0)
        sched.bar()

        def g1(t, tiles=tiles):
            r = None
            for i, tt in enumerate(tiles):
                for half_o in range(2):
                    for dc in range(DC):
                        t.matmul(psB[:, i * 1024 + half_o * 512: i * 1024 + (half_o + 1) * 512],
                                 xnT[dc][:, tt * P:(tt + 1) * P],
                                 gwTb[dc][:, half_o * 512:(half_o + 1) * 512],
                                 start=(dc == 0), stop=False)
                    r = t.matmul(psB[:, i * 1024 + half_o * 512: i * 1024 + (half_o + 1) * 512],
                                 ones_r[0:1, :],
                                 gbb[0:1, half_o * 512:(half_o + 1) * 512],
                                 start=False, stop=True)
                for dc in range(DC):
                    r = t.transpose(psAb[:, i * 1024 + dc * P: i * 1024 + (dc + 1) * P],
                                    y_sb[dc][:, tt * P:(tt + 1) * P], idb[:])
            return r
        sched.op("tensor", g1)
        sched.bar()

        def g2(sc):
            r = None
            for i in range(2):
                r = sc.activation(out=SG4[i][:], in_=psB[:, i * 1024:(i + 1) * 1024],
                                  func=AF.Sigmoid)
            return r
        sched.op("scalar", g2)
        sched.bar()

        def g3(v, cnt=cnt):
            v.wait_ge(ld_sem, cnt[0])
            r = None
            for i in range(2):
                v.tensor_copy(out=YT2[i][:], in_=psAb[:, i * 1024:(i + 1) * 1024])
                v.tensor_tensor(out=V4[i][:], in0=W4[i][:], in1=dm1b[:], op=ALU.mult)
                v.tensor_tensor(out=V4[i][:], in0=V4[i][:],
                                in1=YT2[i][:], op=ALU.add)
                v.tensor_tensor(out=V4[i][:], in0=V4[i][:], in1=SG4[i][:], op=ALU.mult)
                v.tensor_tensor(out=V4[i][:], in0=V4[i][:], in1=W4[i][:], op=ALU.add)
                ln_stats_v(v, V4[i][:], ST4[i], VS)
            return v.tensor_copy(out=mkr[:, 0:1], in_=mkr[:, 0:1])
        sched.op("vector", g3)
        sched.bar()

        def g4(sc):
            r = None
            for i in range(2):
                r = ln_sqrt_s(sc, ST4[i])
            return r
        sched.op("scalar", g4)
        sched.bar()

        def g6(v):
            r = None
            for i in range(2):
                ln_coefs_v(v, ST4[i], VS)
                r = v.tensor_scalar(out=V4[i][:], in0=V4[i][:],
                                    scalar1=ST4[i][:, 40:41],
                                    scalar2=ST4[i][:, 48:49],
                                    op0=ALU.mult, op1=ALU.add)
            return r
        sched.op("vector", g6)
        sched.bar()

        def g7(v):
            r = None
            for i in range(2):
                v.tensor_tensor(out=V4[i][:], in0=V4[i][:], in1=lngb[:], op=ALU.mult)
                v.tensor_tensor(out=V4[i][:], in0=V4[i][:], in1=lnbb[:], op=ALU.add)
                r = v.tensor_tensor(out=V4[i][:], in0=V4[i][:], in1=U4[i][:], op=ALU.add)
            return r
        sched.op("vector", g7)
        sched.bar()

        prev_store = STD.marker()

        def g8(s, tiles=tiles, pv=prev_store):
            for i, tt in enumerate(tiles):
                STD.start(s, x1_t[tt], V4[i][:])
            STD.fill(pv)
        sched.op("sync", g8)
        sched.bar()

    if debug:
        def dumpT(s):
            STD.start(s, dbg_xnT, xnT[0][:])
            STD.start(s, dbg_y, y_sb[0][:])
            STD.start(s, dbg_sg, SG4[0][:])
        sched.op("sync", dumpT)
        sched.bar()

    def fin(g):
        g.wait_ge(st_sem, STD.count)
    sched.op("gpsimd", fin)

    streams = sched.build(ph)
    with nc.Block() as block:
        block.sync(streams["sync"])
        block.vector(streams["vector"])
        block.scalar(streams["scalar"])
        block.tensor(streams["tensor"])
        block.gpsimd(streams["gpsimd"])
    ctx.close()
    if fused:
        return {"x1": x1, "ident_f": ident_f, "ident_b": ident_b}
    return nc


def build_pmax(nc):
    x = nc.dram_tensor("x", [T, D], F32, kind="ExternalInput").ap()
    g2_row = nc.dram_tensor("g2_row", [1, D], F32, kind="ExternalInput").ap()
    b2_row = nc.dram_tensor("b2_row", [1, D], F32, kind="ExternalInput").ap()
    ident_f = nc.dram_tensor("ident_f", [P, P], F32, kind="ExternalInput").ap()
    pmax = nc.dram_tensor("pmax", [1, 1], F32, kind="ExternalOutput").ap()
    x_t = x.rearrange("(n p) d -> n p d", p=P)

    ctx = contextlib.ExitStack()
    sb = lambda name, shape, dt=F32: ctx.enter_context(nc.sbuf_tensor(name, shape, dt))
    W4 = [[sb(f"pw{pg}_{i}", [P, D]) for i in range(2)] for pg in range(2)]
    U4 = [[sb(f"pu{pg}_{i}", [P, D]) for i in range(2)] for pg in range(2)]
    ST4 = [[sb(f"pst{pg}_{i}", [P, 72]) for i in range(2)] for pg in range(2)]
    mark = sb("pmark", [P, 8])
    mkr = mark
    g2b = sb("pg2b", [P, D])
    b2b = sb("pb2b", [P, D])
    idf = sb("pidf", [P, P])
    amax = sb("pamax", [P, 16])
    scal = sb("pscal", [1, P])
    psA = ctx.enter_context(nc.psum_tensor("ppsA", [P, 512], F32))

    sched = Sched(nc)
    ld_sem = ctx.enter_context(nc.semaphore("LdP"))
    st_sem = ctx.enter_context(nc.semaphore("StP"))
    ph = ctx.enter_context(nc.semaphore("PhP"))
    vs_sem = ctx.enter_context(nc.semaphore("VsP"))
    LD = Dma(ld_sem)
    STD = Dma(st_sem)
    VS = VSync(vs_sem)

    sched.op("vector", lambda v: v.memset(mark[:, 0:1], 0.0))

    pcw = LD.marker()

    def pc(s):
        LD.start(s, g2b[:], g2_row.squeeze(0).partition_broadcast(P))
        LD.start(s, b2b[:], b2_row.squeeze(0).partition_broadcast(P))
        LD.start(s, idf[:], ident_f)
        LD.fill(pcw)
    sched.op("sync", pc)
    sched.bar()

    gcnt = {0: LD.marker()}

    def ppre(s):
        for i in range(2):
            LD.start(s, W4[0][i][:], x_t[i])
        LD.fill(gcnt[0])
    sched.op("sync", ppre)
    sched.bar()

    for g in range(9):
        pg = g % 2
        qg = (g - 1) % 2

        if g < 8:
            if g + 1 < 8:
                gcnt[g + 1] = LD.marker()

                def p0(s, g=g, cnt=gcnt[g + 1]):
                    for i, tt in enumerate([2 * (g + 1), 2 * (g + 1) + 1]):
                        LD.start(s, W4[(g + 1) % 2][i][:], x_t[tt])
                    LD.fill(cnt)
                sched.op("sync", p0)

            def p1(v, pg=pg, cnt=gcnt[g]):
                v.wait_ge(ld_sem, cnt[0])
                v.wait_ge(ld_sem, pcw[0])
                for i in range(2):
                    ln_stats_v(v, W4[pg][i][:], ST4[pg][i], VS)
                return v.tensor_copy(out=mkr[:, 0:1], in_=mkr[:, 0:1])
            sched.op("vector", p1)

        if g > 0:
            def p4(sc, qg=qg):
                r = None
                for i in range(2):
                    r = ln_sqrt_s(sc, ST4[qg][i])
                return r
            sched.op("scalar", p4)
        sched.bar()

        if g > 0:
            def p5(v, g=g - 1, qg=qg):
                for i in range(2):
                    ln_coefs_v(v, ST4[qg][i], VS)
                    v.tensor_scalar(out=U4[qg][i][:], in0=U4[qg][i][:],
                                    scalar1=ST4[qg][i][:, 40:41], scalar2=ST4[qg][i][:, 48:49],
                                    op0=ALU.mult, op1=ALU.add)
                    v.tensor_tensor(out=U4[qg][i][:], in0=U4[qg][i][:], in1=g2b[:], op=ALU.mult)
                    v.tensor_tensor(out=U4[qg][i][:], in0=U4[qg][i][:], in1=b2b[:], op=ALU.add)
                    VS.edge(v, v.tensor_reduce(out=amax[:, 8:9], in_=U4[qg][i][:], axis=AX.X,
                                               op=ALU.max, apply_absolute_value=True))
                    if g == 0 and i == 0:
                        VS.edge(v, v.tensor_copy(out=amax[:, 0:1], in_=amax[:, 8:9]))
                    else:
                        VS.edge(v, v.tensor_tensor(out=amax[:, 0:1], in0=amax[:, 0:1],
                                                   in1=amax[:, 8:9], op=ALU.max))
                return v.tensor_copy(out=mkr[:, 0:1], in_=mkr[:, 0:1])
            sched.op("vector", p5)

        if g < 8:
            def p2(sc, pg=pg):
                r = None
                for i in range(2):
                    r = ln_sqrt_s(sc, ST4[pg][i])
                return r
            sched.op("scalar", p2)
        sched.bar()

        if g < 8:
            def p3(v, pg=pg):
                for i in range(2):
                    ln_coefs_v(v, ST4[pg][i], VS)
                    v.tensor_scalar(out=U4[pg][i][:], in0=W4[pg][i][:],
                                    scalar1=ST4[pg][i][:, 40:41], scalar2=ST4[pg][i][:, 48:49],
                                    op0=ALU.mult, op1=ALU.add)
                    ln_stats_v(v, U4[pg][i][:], ST4[pg][i], VS)
                return v.tensor_copy(out=mkr[:, 0:1], in_=mkr[:, 0:1])
            sched.op("vector", p3)
            sched.bar()

    def pf0(t):
        return t.transpose(psA[0:1, 0:P], amax[:, 0:1], idf[:])
    sched.op("tensor", pf0)
    sched.bar()

    def pf1(v):
        return v.tensor_reduce(out=scal[:, 0:1], in_=psA[0:1, 0:P], axis=AX.X,
                               op=ALU.max)
    sched.op("vector", pf1)
    sched.bar()

    def pf2(s):
        STD.start(s, pmax, scal[:, 0:1])
    sched.op("sync", pf2)
    sched.bar()

    def pfin(g):
        g.wait_ge(st_sem, STD.count)
    sched.op("gpsimd", pfin)

    streams = sched.build(ph)
    with nc.Block() as block:
        block.sync(streams["sync"])
        block.vector(streams["vector"])
        block.scalar(streams["scalar"])
        block.tensor(streams["tensor"])
        block.gpsimd(streams["gpsimd"])
    ctx.close()
    return nc


def host_prep_l0(inputs, n_cores=8):
    f32 = np.float32
    x = np.ascontiguousarray(np.asarray(inputs["L0_attn_x"], f32))
    A_log = np.asarray(inputs["L0_attn_A_log"], f32)
    B_w = np.asarray(inputs["L0_attn_B_w"], f32)
    C_w = np.asarray(inputs["L0_attn_C_w"], f32)
    Dv = np.asarray(inputs["L0_attn_D"], f32)
    gate_w = np.asarray(inputs["L0_attn_gate_w"], f32)
    gate_b = np.asarray(inputs["L0_attn_gate_b"], f32)
    ln_g = np.asarray(inputs["L0_attn_ln_gamma"], f32)
    ln_b = np.asarray(inputs["L0_attn_ln_beta"], f32)
    w0 = np.asarray(inputs["L0_ffn_weight"], f32)
    g2 = np.asarray(inputs["L0_ffn_gamma"], f32)
    b2 = np.asarray(inputs["L0_ffn_beta"], f32)

    alog_r = np.ascontiguousarray(
        A_log.reshape(DC, P, NST).transpose(1, 0, 2).reshape(P, P))
    shared = {
        "alog_r": alog_r,
        "bwT": np.ascontiguousarray(B_w.T),
        "cwT": np.ascontiguousarray(C_w.T),
        "d_row": np.ascontiguousarray(Dv[None, :]),
        "gate_wT": np.ascontiguousarray(gate_w.T),
        "gate_b_row": np.ascontiguousarray(gate_b[None, :]),
        "ln_g_row": np.ascontiguousarray(ln_g[None, :]),
        "ln_b_row": np.ascontiguousarray(ln_b[None, :]),
        "l0w_T": np.ascontiguousarray(w0.T),
        "g2_row": np.ascontiguousarray(g2[None, :]),
        "b2_row": np.ascontiguousarray(b2[None, :]),
        "ident_f": np.eye(P, dtype=f32),
        "ident_b": np.eye(P, dtype=BF16NP),
        "ones_row": np.ones((1, P), dtype=f32),
    }
    maps = []
    for c in range(n_cores):
        m = dict(shared)
        m["x"] = np.ascontiguousarray(x[c])
        maps.append(m)
    return maps


def run_pmax(inputs, n_cores=8):
    f32 = np.float32
    x = np.asarray(inputs["L0_attn_x"], f32)
    nc = bass.Bass("TRN2", target_bir_lowering=False,
                   detect_race_conditions=False)
    build_pmax(nc)
    shared = {
        "g2_row": np.ascontiguousarray(np.asarray(inputs["L0_ffn_gamma"], f32)[None, :]),
        "b2_row": np.ascontiguousarray(np.asarray(inputs["L0_ffn_beta"], f32)[None, :]),
        "ident_f": np.eye(P, dtype=f32),
    }
    maps = [dict(shared, x=np.ascontiguousarray(x[c])) for c in range(n_cores)]
    res = run_bass_kernel_spmd(nc, maps, core_ids=list(range(n_cores)))
    pm = np.array([[float(np.asarray(r["pmax"]).ravel()[0]) for r in res.results]],
                  np.float32)
    return pm


def run_l0(inputs, n_cores=8, debug=False):
    gmax_row = run_pmax(inputs, n_cores)
    nc = bass.Bass("TRN2", target_bir_lowering=False,
                   detect_race_conditions=False)
    build_l0(nc, n_cores=n_cores, debug=debug)
    maps = host_prep_l0(inputs, n_cores)
    for m in maps:
        m["gmax_row"] = gmax_row
    res = run_bass_kernel_spmd(nc, maps, core_ids=list(range(n_cores)))
    if debug:
        return res
    return np.stack([np.asarray(r["x1"]) for r in res.results], 0)


def kernel(**inputs):
    x1 = run_l0(inputs, n_cores=8)
    return x1.astype(np.float32)


HEADS = 16
DH = 64
LATD = 512
LHEADS = 8
NLAT = 64
WIN = 256
INNER = 4096
IC = INNER // P


def build_l1(nc, debug=False, shared=None):

    def din(name, shape, dt=F32):
        return nc.dram_tensor(name, shape, dt, kind="ExternalInput").ap()

    dsc = ((lambda n, s, dt=F32: nc.dram_tensor(n, s, dt, kind="ExternalOutput").ap())
           if debug else
           (lambda n, s, dt=F32: nc.dram_tensor(n, s, dt, kind="Internal").ap()))

    if shared is None:
        x = din("x", [T, D])
    else:
        x = shared["x1"]
    out = nc.dram_tensor("out", [T, D], F32, kind="ExternalOutput").ap()
    wq_d = din("wq", [D, D])
    wk_d = din("wk", [D, D])
    wv_d = din("wv", [D, D])
    wgq_d = din("wgq", [D, LATD])
    wgk_d = din("wgk", [D, LATD])
    wgv_d = din("wgv", [D, LATD])
    rout_d = din("r_out_w", [LATD, D])
    wout_d = din("w_out", [D, D])
    bdgl_d = din("bd_glw", [D, HEADS])
    bdgr_d = din("bd_grw", [D, HEADS])
    wgT_d = din("w_gT", [D, INNER])
    wuT_d = din("w_uT", [D, INNER])
    wdT_d = din("w_dT", [INNER, D])
    if shared is None:
        ident_f = din("ident_f", [P, P])
        ident_b = din("ident_b", [P, P], BF16)
    else:
        ident_f = shared["ident_f"]
        ident_b = shared["ident_b"]
    poolm_d = din("poolm", [T, NLAT], BF16)
    masks_d = din("masks", [4, P, 512])
    lmask_d = din("lmask", [T, NLAT])

    h_sp = dsc("h_sp", [T, INNER])
    x2a_sp = dsc("x2a_sp", [T, D])
    wdq_sp = dsc("wdq_sp", [INNER, D], BF16)
    if debug:
        dbg_hqT = nc.dram_tensor("dbg_hqT", [P, T], BF16, kind="ExternalOutput").ap()
        dbg_rh = nc.dram_tensor("dbg_rh", [P, NT * 8], F32, kind="ExternalOutput").ap()

    x_t = x.rearrange("(n p) d -> n p d", p=P)
    out_t = out.rearrange("(n p) d -> n p d", p=P)
    h_t = h_sp.rearrange("(n p) d -> n p d", p=P)
    x2a_t = x2a_sp.rearrange("(n p) d -> n p d", p=P)
    lmask_t = lmask_d.rearrange("(n p) l -> n p l", p=P)
    poolm_t = poolm_d.rearrange("(n p) l -> n p l", p=P)

    ctx = contextlib.ExitStack()
    sb = lambda name, shape, dt=F32: ctx.enter_context(nc.sbuf_tensor(name, shape, dt))

    xnT = [sb(f"l1xnT{c}", [P, T], BF16) for c in range(DC)]
    VTK = [sb(f"l1v{c}", [P, T], BF16) for c in range(DC)]
    KQT = [sb(f"l1kq{c}", [P, T], BF16) for c in range(DC)]
    QT = [sb(f"l1qt{c}", [P, T], BF16) for c in range(DC)]
    GQT = [sb(f"l1gqt{c}", [P, T], BF16) for c in range(4)]
    LPT = [sb(f"l1lpt{c}", [P, T], BF16) for c in range(4)]
    WB = [sb(f"l1wb{c}", [P, D], BF16) for c in range(DC)]
    W2 = [sb(f"l1w2_{i}", [P, D]) for i in range(2)]
    U2 = [sb(f"l1u2_{i}", [P, D]) for i in range(2)]
    _v2s = sb("l1v2s", [P, D])
    V2 = [_v2s, _v2s]
    XB2 = [sb(f"l1xb{i}", [P, D], BF16) for i in range(2)]
    HB2 = [sb(f"l1hb{i}", [P, D], BF16) for i in range(2)]
    ST2 = [sb(f"l1st{i}", [P, 72]) for i in range(2)]
    mark = sb("l1mark", [P, 48])
    mkr = sb("l1mkr", [P, 8])
    idf = sb("l1idf", [P, P])
    idb = sb("l1idb", [P, P], BF16)
    RS = sb("l1rs", [P, NT * 8])
    RSQ = sb("l1rsq", [P, NT * 8])
    RH = sb("l1rh", [P, NT * 8])
    RHQ = sb("l1rhq", [P, NT * 8])
    CGU = sb("l1cgu", [P, NT * 8])
    scw = sb("l1scw", [1, 32])
    scwb = sb("l1scwb", [P, 8])
    sc_d2 = dsc("l1sc_d", [1, 8])
    wacc = sb("l1wacc", [P, IC * 8])
    pooledT = [sb(f"l1plT{c}", [P, NLAT], BF16) for c in range(DC)]
    GKT = [sb(f"l1gkt{c}", [P, NLAT], BF16) for c in range(4)]
    GV = sb("l1gv", [P, LATD], BF16)

    LPS = sb("l1lps", [NLAT, 512], BF16)
    glt = [sb(f"l1glt{i}", [P, 2 * HEADS]) for i in range(NT)]
    rc = sb("l1rc", [P, HEADS * 8])
    glr = sb("l1glr", [P, HEADS])
    _msk = sb("l1msk", [P, 512])
    msk = [_msk, _msk]
    _lm = sb("l1lmsk", [P, NLAT])
    lmsk2 = [_lm, _lm]
    pmT2 = [sb(f"l1pmT{i}", [P, NLAT], BF16) for i in range(2)]
    sS = sb("l1sS", [P, 512])
    pS = sb("l1pS", [P, 512], BF16)

    glsb = sb("l1glsb", [HEADS, P])

    def PTv(h):
        return xnT[h // 4][:, (h % 4) * 512:(h % 4) * 512 + 512]

    def rem2v(qc):
        return xnT[6][:, qc * D:(qc + 1) * D]

    def combv(qc):
        return xnT[7][:, qc * D:(qc + 1) * D]

    def cTv(qc, dc):
        return xnT[4][:, (qc * DC + dc) * P:(qc * DC + dc + 1) * P]

    def vtile(tt):
        return VTK[tt // 2][:, (tt % 2) * D:(tt % 2 + 1) * D]

    def hqTv(k, set4):
        return QT[k // 8][:, (k % 8) * 256:(k % 8) * 256 + 256]

    psA = ctx.enter_context(nc.psum_tensor("l1psA", [P, T], F32))
    psB = ctx.enter_context(nc.psum_tensor("l1psB", [P, T], F32))
    psAb = psA[:].bitcast(BF16)
    psBb = psB[:].bitcast(BF16)

    sched = Sched(nc)
    ld_sem = ctx.enter_context(nc.semaphore("LdL1"))
    st_sem = ctx.enter_context(nc.semaphore("StL1"))
    ph = ctx.enter_context(nc.semaphore("PhL1"))
    vs_sem = ctx.enter_context(nc.semaphore("VsL1"))
    LD = Dma(ld_sem)
    STD = Dma(st_sem)
    VS = VSync(vs_sem)

    sched.op("vector", lambda v: v.memset(mkr[:, 0:1], 0.0))

    cw = LD.marker()

    def l1c0(s):
        LD.start(s, idf[:], ident_f)
        LD.start(s, idb[:], ident_b)
        LD.fill(cw)
    sched.op("sync", l1c0)
    sched.bar()

    for g in range(8):
        tiles = [2 * g, 2 * g + 1]
        cnt = LD.marker()

        def a0(s, tiles=tiles, cnt=cnt, g=g):
            for i, tt in enumerate(tiles):
                LD.start(s, W2[i][:], x_t[tt])
                LD.start(s, pmT2[i][:], poolm_t[tt])
            LD.fill(cnt)
        sched.op("sync", a0)
        sched.bar()

        def a1(v, cnt=cnt):
            v.wait_ge(ld_sem, cnt[0])
            for i in range(2):
                ln_stats_v(v, W2[i][:], ST2[i], VS)
            return v.tensor_copy(out=mkr[:, 0:1], in_=mkr[:, 0:1])
        sched.op("vector", a1)
        sched.bar()

        def a2(sc):
            r = None
            for i in range(2):
                r = ln_sqrt_s(sc, ST2[i])
            return r
        sched.op("scalar", a2)
        sched.bar()

        def a4(v, tiles=tiles):
            r = None
            for i, tt in enumerate(tiles):
                ln_coefs_v(v, ST2[i], VS)
                v.tensor_scalar(out=U2[i][:], in0=W2[i][:],
                                scalar1=ST2[i][:, 40:41], scalar2=ST2[i][:, 48:49],
                                op0=ALU.mult, op1=ALU.add)
                v.tensor_copy(out=XB2[i][:], in_=U2[i][:])
                VS.edge(v, v.tensor_reduce(out=RS[:, tt * 8:tt * 8 + 1], in_=U2[i][:],
                                           axis=AX.X, op=ALU.max,
                                           apply_absolute_value=True))
                VS.edge(v, v.tensor_scalar(out=RS[:, tt * 8:tt * 8 + 1],
                                           in0=RS[:, tt * 8:tt * 8 + 1],
                                           scalar1=1e-5, scalar2=1.0 / 127.0,
                                           op0=ALU.max, op1=ALU.mult))
                VS.edge(v, v.reciprocal(out=RSQ[:, tt * 8:tt * 8 + 1],
                                        in_=RS[:, tt * 8:tt * 8 + 1]))
            return v.tensor_copy(out=mkr[:, 0:1], in_=mkr[:, 0:1])
        sched.op("vector", a4)
        sched.bar()

        def a5(v, tiles=tiles):
            r = None
            for i, tt in enumerate(tiles):
                v.tensor_scalar(out=V2[i][:], in0=U2[i][:],
                                scalar1=RSQ[:, tt * 8:tt * 8 + 1], scalar2=127.0,
                                op0=ALU.mult, op1=ALU.min)
                v.tensor_scalar(out=U2[i][:].bitcast(I32), in0=V2[i][:],
                                scalar1=-128.0, scalar2=None, op0=ALU.max)
                v.tensor_copy(out=V2[i][:], in_=U2[i][:].bitcast(I32))
                r = v.tensor_copy(out=W2[i][:].bitcast(BF16)[:, 0:D], in_=V2[i][:])
            return r
        sched.op("vector", a5)

        def a5p(t, tiles=tiles):
            r = None
            for i, tt in enumerate(tiles):
                for half in range(2):
                    r = t.matmul(psB[0:NLAT, half * 512:(half + 1) * 512],
                                 pmT2[i][:], XB2[i][:, half * 512:(half + 1) * 512],
                                 start=(tt == 0), stop=(tt == T // P - 1))
            return r
        sched.op("tensor", a5p)
        sched.bar()

        def a8t(t):
            r = None
            for i in range(2):
                for dc in range(DC):
                    r = t.transpose(psAb[:, (i * DC + dc) * P:(i * DC + dc + 1) * P],
                                    XB2[i][:, dc * P:(dc + 1) * P], idb[:])
            return r
        sched.op("tensor", a8t)
        sched.bar()

        def a9e(v, tiles=tiles):
            r = None
            for i, tt in enumerate(tiles):
                for dc in range(DC):
                    r = v.tensor_copy(out=xnT[dc][:, tt * P:(tt + 1) * P],
                                      in_=psAb[:, (i * DC + dc) * P:(i * DC + dc + 1) * P])
            return r
        sched.op("vector", a9e)
        sched.bar()

        def a10t(t):
            r = None
            for i in range(2):
                for dc in range(DC):
                    r = t.transpose(psAb[:, (i * DC + dc) * P:(i * DC + dc + 1) * P],
                                    W2[i][:].bitcast(BF16)[:, dc * P:(dc + 1) * P], idb[:])
            return r
        sched.op("tensor", a10t)
        sched.bar()

        def a11e(v, tiles=tiles):
            r = None
            for i, tt in enumerate(tiles):
                for dc in range(DC):
                    r = v.tensor_copy(out=KQT[dc][:, tt * P:(tt + 1) * P],
                                      in_=psAb[:, (i * DC + dc) * P:(i * DC + dc + 1) * P])
            return r
        sched.op("vector", a11e)
        sched.bar()

    def b0(v):
        return v.tensor_copy(out=W2[0][:].bitcast(BF16)[0:NLAT, 0:D],
                             in_=psB[0:NLAT, 0:D])
    sched.op("vector", b0)
    sched.bar()

    def b1(t):
        r = None
        for dc in range(DC):
            r = t.transpose(psAb[:, dc * NLAT:(dc + 1) * NLAT],
                            W2[0][:].bitcast(BF16)[0:NLAT, dc * P:(dc + 1) * P],
                            idb[0:NLAT, 0:NLAT])
        return r
    sched.op("tensor", b1)
    sched.bar()

    def b2(v):
        r = None
        for dc in range(DC):
            r = v.tensor_copy(out=pooledT[dc][:], in_=psAb[:, dc * NLAT:(dc + 1) * NLAT])
        return r
    sched.op("vector", b2)
    sched.bar()

    def stage_weight(wten, ncols, tag, dual=True, wmark=None):
        if not dual:
            for grp in range(4):
                cnt = LD.marker()

                def wl(s, grp=grp, cnt=cnt, wten=wten, ncols=ncols):
                    for i in range(2):
                        LD.start(s, W2[i][:, 0:ncols],
                                 wten.rearrange("(c p) o -> c p o", p=P)[grp * 2 + i])
                    LD.fill(cnt)
                sched.op("sync", wl)
                sched.bar()

                def wc(v, grp=grp, cnt=cnt, ncols=ncols):
                    v.wait_ge(ld_sem, cnt[0])
                    r = None
                    for i in range(2):
                        r = v.tensor_copy(out=WB[grp * 2 + i][:, 0:ncols],
                                          in_=W2[i][:, 0:ncols])
                    return r
                sched.op("vector", wc)
                sched.bar()
            return
        bufs = [W2, U2]
        cnts = {}
        for c in range(5):
            if c < 4:
                cnts[c] = LD.marker()

                def wl(s, grp=c, cnt=cnts[c], wten=wten, ncols=ncols, wmark=wmark):
                    if wmark is not None and grp == 0:
                        s.wait_ge(st_sem, wmark[0])
                    for i in range(2):
                        LD.start(s, bufs[grp % 2][i][:, 0:ncols],
                                 wten.rearrange("(c p) o -> c p o", p=P)[grp * 2 + i])
                    LD.fill(cnt)
                sched.op("sync", wl)
            if c > 0:
                def wc(v, grp=c - 1, cnt=cnts[c - 1], ncols=ncols):
                    v.wait_ge(ld_sem, cnt[0])
                    r = None
                    for i in range(2):
                        r = v.tensor_copy(out=WB[grp * 2 + i][:, 0:ncols],
                                          in_=bufs[grp % 2][i][:, 0:ncols])
                    return r
                sched.op("vector", wc)
            sched.bar()

    stage_weight(wgk_d, LATD, "wgk")

    def gk(t):
        r = None
        for mc in range(4):
            for dc in range(DC):
                r = t.matmul(psA[:, mc * NLAT:(mc + 1) * NLAT],
                             WB[dc][:, mc * P:(mc + 1) * P], pooledT[dc][:],
                             start=(dc == 0), stop=(dc == DC - 1))
        return r
    sched.op("tensor", gk)
    sched.bar()

    def gke(v):
        r = None
        for mc in range(4):
            r = v.tensor_copy(out=GKT[mc][:], in_=psA[:, mc * NLAT:(mc + 1) * NLAT])
        return r
    sched.op("vector", gke)
    sched.bar()

    stage_weight(wgv_d, LATD, "wgv")

    def gvm(t):
        r = None
        for dc in range(DC):
            r = t.matmul(psA[0:NLAT, 0:LATD], pooledT[dc][:], WB[dc][:, 0:LATD],
                         start=(dc == 0), stop=(dc == DC - 1))
        return r
    sched.op("tensor", gvm)
    sched.bar()

    def gve(v):
        return v.tensor_copy(out=GV[0:NLAT, :], in_=psA[0:NLAT, 0:LATD])
    sched.op("vector", gve)
    sched.bar()

    gvd_mark = STD.marker()

    def gvd(s, pv=gvd_mark):
        STD.start(s, GV[64:128, :], GV[0:NLAT, :])
        STD.fill(pv)
    sched.op("sync", gvd)
    sched.bar()

    def wabs_wide(wten, widx):
        for grp in range(4):
            for half in range(2):
                cnt = LD.marker()

                def wl3(s, grp=grp, half=half, cnt=cnt, wten=wten):
                    for i in range(2):
                        for seg in range(2):
                            LD.start(s, (W2 if seg == 0 else U2)[i][:, 0:1024],
                                     wten.rearrange("(c p) o -> c p o", p=P)[grp * 2 + i][:, (half * 2 + seg) * 1024:(half * 2 + seg + 1) * 1024])
                    LD.fill(cnt)
                sched.op("sync", wl3)
                sched.bar()

                def wa3(sc, grp=grp, half=half, cnt=cnt, widx=widx):
                    sc.wait_ge(ld_sem, cnt[0])
                    r = None
                    for i in range(2):
                        k = grp * 2 + i
                        r = sc.activation(out=W2[i][:, 0:1024], in_=W2[i][:, 0:1024],
                                          func=AF.Abs,
                                          accum_out=wacc[:, (k * 4 + half * 2) * 8:(k * 4 + half * 2) * 8 + 1])
                        r = sc.activation(out=U2[i][:, 0:1024], in_=U2[i][:, 0:1024],
                                          func=AF.Abs,
                                          accum_out=wacc[:, (k * 4 + half * 2 + 1) * 8:(k * 4 + half * 2 + 1) * 8 + 1])
                    return r
                sched.op("scalar", wa3)
                sched.bar()

        def wred(v, widx=widx):
            VS.edge(v, v.tensor_reduce(
                out=mark[:, 8:9],
                in_=wacc[:].rearrange("p (k e) -> p k e", e=8)[:, :, 0:1].squeeze(2),
                axis=AX.X, op=ALU.add))
            return v.tensor_copy(out=mkr[:, 0:1], in_=mkr[:, 0:1])
        sched.op("vector", wred)
        sched.bar()

        def wtr(t, widx=widx):
            return t.transpose(psA[0:1, 0:P], mark[:, 8:9], idf[:])
        sched.op("tensor", wtr)
        sched.bar()

        def wfin(v, widx=widx):
            VS.edge(v, v.tensor_reduce(out=scw[:, widx * 8:widx * 8 + 1],
                                       in_=psA[0:1, 0:P], axis=AX.X, op=ALU.add))
            VS.edge(v, v.tensor_scalar(out=scw[:, widx * 8:widx * 8 + 1],
                                       in0=scw[:, widx * 8:widx * 8 + 1],
                                       scalar1=1.0 / (D * INNER), scalar2=1e-5,
                                       op0=ALU.mult, op1=ALU.max))
            VS.edge(v, v.reciprocal(out=scw[:, widx * 8 + 1:widx * 8 + 2],
                                    in_=scw[:, widx * 8:widx * 8 + 1]))
            return v.tensor_copy(out=mkr[:, 0:1], in_=mkr[:, 0:1])
        sched.op("vector", wfin)
        sched.bar()

    wabs_wide(wgT_d, 0)
    wabs_wide(wuT_d, 1)

    for grp in range(16):
        cnt = LD.marker()

        def wl4(s, grp=grp, cnt=cnt):
            for i in range(2):
                LD.start(s, W2[i][:, 0:1024],
                         wdT_d.rearrange("(c p) o -> c p o", p=P)[grp * 2 + i])
            LD.fill(cnt)
        sched.op("sync", wl4)
        sched.bar()

        def wa4(sc, grp=grp, cnt=cnt):
            sc.wait_ge(ld_sem, cnt[0])
            r = None
            for i in range(2):
                k = grp * 2 + i
                r = sc.activation(out=W2[i][:, 0:1024], in_=W2[i][:, 0:1024],
                                  func=AF.Abs, accum_out=wacc[:, k * 8:k * 8 + 1])
            return r
        sched.op("scalar", wa4)
        sched.bar()

    def wred4(v):
        VS.edge(v, v.tensor_reduce(
            out=mark[:, 8:9],
            in_=wacc[:].rearrange("p (k e) -> p k e", e=8)[:, :, 0:1].squeeze(2),
            axis=AX.X, op=ALU.add))
        return v.tensor_copy(out=mkr[:, 0:1], in_=mkr[:, 0:1])
    sched.op("vector", wred4)
    sched.bar()

    def wtr4(t):
        return t.transpose(psA[0:1, 0:P], mark[:, 8:9], idf[:])
    sched.op("tensor", wtr4)
    sched.bar()

    def wfin4(v):
        VS.edge(v, v.tensor_reduce(out=scw[:, 16:17], in_=psA[0:1, 0:P],
                                   axis=AX.X, op=ALU.add))
        VS.edge(v, v.tensor_scalar(out=scw[:, 16:17], in0=scw[:, 16:17],
                                   scalar1=1.0 / (D * INNER), scalar2=1e-5,
                                   op0=ALU.mult, op1=ALU.max))
        VS.edge(v, v.reciprocal(out=scw[:, 17:18], in_=scw[:, 16:17]))
        return v.tensor_copy(out=mkr[:, 0:1], in_=mkr[:, 0:1])
    sched.op("vector", wfin4)
    sched.bar()

    cnt_swst = STD.marker()

    def swst(s):
        STD.start(s, sc_d2[:, 0:1], scw[:, 1:2])
        STD.start(s, sc_d2[:, 1:2], scw[:, 9:10])
        STD.start(s, sc_d2[:, 2:3], scw[:, 17:18])
        STD.start(s, sc_d2[:, 3:4], scw[:, 0:1])
        STD.start(s, sc_d2[:, 4:5], scw[:, 8:9])
        STD.start(s, sc_d2[:, 5:6], scw[:, 16:17])
        STD.fill(cnt_swst)
    sched.op("sync", swst)
    sched.bar()
    cnt_swld = LD.marker()

    def swld(s):
        s.wait_ge(st_sem, cnt_swst[0])
        for j in range(6):
            LD.start(s, scwb[:, j:j + 1], sc_d2[:, j:j + 1].squeeze(0).partition_broadcast(P))
        LD.fill(cnt_swld)
    sched.op("sync", swld)
    sched.bar()

    def cgu0(v):
        v.wait_ge(ld_sem, cnt_swld[0])
        r = None
        for tt in range(NT):
            r = VS.edge(v, v.tensor_tensor(out=CGU[:, tt * 8:tt * 8 + 1],
                                           in0=RS[:, tt * 8:tt * 8 + 1],
                                           in1=scwb[:, 3:4], op=ALU.mult))
            r = VS.edge(v, v.tensor_tensor(out=CGU[:, tt * 8 + 1:tt * 8 + 2],
                                           in0=RS[:, tt * 8:tt * 8 + 1],
                                           in1=scwb[:, 4:5], op=ALU.mult))
        return v.tensor_copy(out=mkr[:, 0:1], in_=mkr[:, 0:1])
    sched.op("vector", cgu0)
    sched.bar()

    c2_mark = STD.marker()
    for oc in range(8):
        chunks = [(wsel, wten, grp) for wsel, wten in ((0, wgT_d), (1, wuT_d))
                  for grp in range(4)]
        ccnt = {}
        for c in range(len(chunks) + 1):
            if c < len(chunks):
                wsel, wten, grp = chunks[c]
                ccnt[c] = LD.marker()

                def c2l(s, grp=grp, oc=oc, wten=wten, cnt=ccnt[c], pc=c % 2):
                    for i in range(2):
                        LD.start(s, W2[i][:, pc * 512:pc * 512 + 512],
                                 wten.rearrange("(c p) o -> c p o", p=P)[grp * 2 + i][:, oc * 512:(oc + 1) * 512])
                    LD.fill(cnt)
                sched.op("sync", c2l)

            if c > 0:
                wsel, wten, grp = chunks[c - 1]

                def c2q(v, grp=grp, wsel=wsel, cnt=ccnt[c - 1], pv=c2_mark,
                        pc=(c - 1) % 2):
                    v.wait_ge(ld_sem, cnt[0])
                    v.wait_ge(st_sem, pv[0])
                    r = None
                    for i in range(2):
                        v.tensor_scalar(out=U2[i][:, 0:512],
                                        in0=W2[i][:, pc * 512:pc * 512 + 512],
                                        scalar1=scwb[:, wsel:wsel + 1], scalar2=1.49,
                                        op0=ALU.mult, op1=ALU.min)
                        v.tensor_scalar(out=V2[i][:].bitcast(I32)[:, 0:512],
                                        in0=U2[i][:, 0:512],
                                        scalar1=-1.49, scalar2=None, op0=ALU.max)
                        v.tensor_copy(out=U2[i][:, 0:512], in_=V2[i][:].bitcast(I32)[:, 0:512])
                        r = v.tensor_copy(out=WB[grp * 2 + i][:, wsel * 512:(wsel + 1) * 512],
                                          in_=U2[i][:, 0:512])
                    return r
                sched.op("vector", c2q)
            sched.bar()

        def c2dst(tt):
            j = tt % 4
            if j < 2:
                return U2[j][:, 0:512]
            return _v2s[:, (j - 2) * 512:(j - 1) * 512]

        for s in range(NT + 3):
            if s < NT:
                def c2m(t, tt=s, oc=oc):
                    r = None
                    b = (tt % 4) * 512
                    for dc in range(DC):
                        r = t.matmul(psA[:, b:b + 512],
                                     KQT[dc][:, tt * P:(tt + 1) * P],
                                     WB[dc][:, 0:512],
                                     start=(dc == 0), stop=(dc == DC - 1))
                    for dc in range(DC):
                        r = t.matmul(psB[:, b:b + 512],
                                     KQT[dc][:, tt * P:(tt + 1) * P],
                                     WB[dc][:, 512:1024],
                                     start=(dc == 0), stop=(dc == DC - 1))
                    return r
                sched.op("tensor", c2m)

            if 0 <= s - 1 < NT:
                def c2s(sc, tt=s - 1, pv=c2_mark):
                    sc.wait_ge(st_sem, pv[0])
                    b = (tt % 4) * 512
                    return sc.activation(out=c2dst(tt), in_=psA[:, b:b + 512],
                                         func=AF.Silu, scale=CGU[:, tt * 8:tt * 8 + 1])
                sched.op("scalar", c2s)

            if 0 <= s - 2 < NT:
                def c2h(v, tt=s - 2, oc=oc):
                    b = (tt % 4) * 512
                    dst = c2dst(tt)
                    v.tensor_scalar(out=W2[tt % 2][:, 0:512], in0=psA[:, b:b + 512] if False else psB[:, b:b + 512],
                                    scalar1=CGU[:, tt * 8 + 1:tt * 8 + 2], scalar2=None,
                                    op0=ALU.mult)
                    v.tensor_tensor(out=dst, in0=dst,
                                    in1=W2[tt % 2][:, 0:512], op=ALU.mult)
                    VS.edge(v, v.tensor_reduce(out=mark[:, 16:17], in_=dst,
                                               axis=AX.X, op=ALU.max,
                                               apply_absolute_value=True))
                    if oc == 0:
                        VS.edge(v, v.tensor_copy(out=RH[:, tt * 8:tt * 8 + 1],
                                                 in_=mark[:, 16:17]))
                    else:
                        VS.edge(v, v.tensor_tensor(out=RH[:, tt * 8:tt * 8 + 1],
                                                   in0=RH[:, tt * 8:tt * 8 + 1],
                                                   in1=mark[:, 16:17], op=ALU.max))
                    return v.tensor_copy(out=mkr[:, 0:1], in_=mkr[:, 0:1])
                sched.op("vector", c2h)

            if 0 <= s - 3 < NT:
                c2_mark = STD.marker()

                def c2st(st_, tt=s - 3, oc=oc, pv=c2_mark):
                    STD.start(st_, h_t[tt][:, oc * 512:(oc + 1) * 512], c2dst(tt))
                    STD.fill(pv)
                sched.op("sync", c2st)
            sched.bar()

    def c2f(v):
        r = None
        for tt in range(NT):
            VS.edge(v, v.tensor_scalar(out=RH[:, tt * 8:tt * 8 + 1],
                                       in0=RH[:, tt * 8:tt * 8 + 1],
                                       scalar1=1e-5, scalar2=1.0 / 127.0,
                                       op0=ALU.max, op1=ALU.mult))
            VS.edge(v, v.reciprocal(out=RHQ[:, tt * 8:tt * 8 + 1],
                                    in_=RH[:, tt * 8:tt * 8 + 1]))
            r = VS.edge(v, v.tensor_tensor(out=RH[:, tt * 8 + 1:tt * 8 + 2],
                                           in0=RH[:, tt * 8:tt * 8 + 1],
                                           in1=scwb[:, 5:6], op=ALU.mult))
        return v.tensor_copy(out=mkr[:, 0:1], in_=mkr[:, 0:1])
    sched.op("vector", c2f)
    sched.bar()

    c3_mark = STD.marker()
    for grp in range(16):
        cnt = LD.marker()

        def c3l(s, grp=grp, cnt=cnt):
            for i in range(2):
                LD.start(s, W2[i][:, 0:1024],
                         wdT_d.rearrange("(c p) o -> c p o", p=P)[grp * 2 + i])
            LD.fill(cnt)
        sched.op("sync", c3l)
        sched.bar()

        def c3q(v, grp=grp, cnt=cnt, pv=c3_mark):
            v.wait_ge(ld_sem, cnt[0])
            v.wait_ge(st_sem, pv[0])
            r = None
            for i in range(2):
                v.tensor_scalar(out=U2[i][:], in0=W2[i][:],
                                scalar1=scwb[:, 2:3], scalar2=1.49,
                                op0=ALU.mult, op1=ALU.min)
                v.tensor_scalar(out=V2[i][:].bitcast(I32), in0=U2[i][:],
                                scalar1=-1.49, scalar2=None, op0=ALU.max)
                v.tensor_copy(out=U2[i][:], in_=V2[i][:].bitcast(I32))
                r = v.tensor_copy(out=XB2[i][:], in_=U2[i][:])
            return r
        sched.op("vector", c3q)
        sched.bar()

        c3_mark = STD.marker()

        def c3s(s, grp=grp, pv=c3_mark):
            for i in range(2):
                STD.start(s, wdq_sp.rearrange("(c p) o -> c p o", p=P)[grp * 2 + i],
                          XB2[i][:])
            STD.fill(pv)
        sched.op("sync", c3s)
        sched.bar()

    cnt_wdq_done = STD.marker()

    def c4m0(s):
        STD.fill(cnt_wdq_done)
    sched.op("sync", c4m0)
    sched.bar()

    c4_mark = STD.marker()
    for s2 in range(8):
        tset = [2 * s2, 2 * s2 + 1]
        cnt = LD.marker()

        def c4l(s, tset=tset, cnt=cnt, s2=s2, pv=c4_mark):
            if s2 == 0:
                s.wait_ge(st_sem, cnt_wdq_done[0])
            s.wait_ge(st_sem, pv[0])
            for i, tt in enumerate(tset):
                LD.start(s, W2[i][:], h_t[tt][:, 0:1024])
                LD.start(s, U2[i][:], h_t[tt][:, 1024:2048])
            LD.fill(cnt)
        sched.op("sync", c4l)
        sched.bar()
        cnt2 = LD.marker()
        for half in range(2):
            def c4l2(s, tset=tset, cnt2=cnt2):
                for i, tt in enumerate(tset):
                    LD.start(s, W2[i][:], h_t[tt][:, 2048:3072])
                    LD.start(s, U2[i][:], h_t[tt][:, 3072:4096])
                LD.fill(cnt2)
            def c4q(v, tset=tset, half=half, cnt=cnt, cnt2=cnt2):
                v.wait_ge(ld_sem, (cnt if half == 0 else cnt2)[0])
                r = None
                for i, tt in enumerate(tset):
                    for seg, buf in ((0, W2[i]), (1, U2[i])):
                        v.tensor_scalar(out=V2[i][:], in0=buf[:],
                                        scalar1=RHQ[:, tt * 8:tt * 8 + 1], scalar2=127.0,
                                        op0=ALU.mult, op1=ALU.min)
                        v.tensor_scalar(out=buf[:].bitcast(I32), in0=V2[i][:],
                                        scalar1=-128.0, scalar2=None, op0=ALU.max)
                        v.tensor_copy(out=V2[i][:], in_=buf[:].bitcast(I32))
                        r = v.tensor_copy(out=(XB2 if seg == 0 else HB2)[i][:],
                                          in_=V2[i][:])
                return r
            sched.op("vector", c4q)
            sched.bar()

            def c4t(t, tset=tset, half=half):
                r = None
                for i in range(2):
                    for seg in range(2):
                        srct = (XB2 if seg == 0 else HB2)[i][:]
                        for dc in range(DC):
                            r = t.transpose(
                                psAb[:, ((i * 2 + seg) * DC + dc) * P:((i * 2 + seg) * DC + dc + 1) * P],
                                srct[:, dc * P:(dc + 1) * P], idb[:])
                return r
            sched.op("tensor", c4t)
            if half == 0:
                sched.op("sync", c4l2)
            sched.bar()

            def c4e(v, tset=tset, half=half):
                r = None
                for i in range(2):
                    for seg in range(2):
                        for dc in range(DC):
                            k = (half * 2 + seg) * DC + dc
                            r = v.tensor_copy(
                                out=QT[k // 8][:, (k % 8) * 256 + i * P:(k % 8) * 256 + (i + 1) * P],
                                in_=psAb[:, ((i * 2 + seg) * DC + dc) * P:((i * 2 + seg) * DC + dc + 1) * P])
                return r
            sched.op("vector", c4e)
            sched.bar()
        ck = {0: LD.marker()}

        def c4w0(s, cnt3=ck[0]):
            for c in range(4):
                LD.start(s, WB[c][:], wdq_sp.rearrange("(c p) o -> c p o", p=P)[c])
            LD.fill(cnt3)
        sched.op("sync", c4w0)
        sched.bar()

        for kp in range(16):
            if kp + 2 < 16:
                ck[kp + 2] = LD.marker()

                def c4wl(s, kp=kp, cnt3=ck[kp + 2]):
                    for c in (2 * kp + 4, 2 * kp + 5):
                        LD.start(s, WB[c % 8][:],
                                 wdq_sp.rearrange("(c p) o -> c p o", p=P)[c])
                    LD.fill(cnt3)
                sched.op("sync", c4wl)
            if kp == 12:
                cnt4 = LD.marker()

                def c4xl(s, tset=tset, cnt4=cnt4):
                    for i, tt in enumerate(tset):
                        LD.start(s, W2[i][:], x_t[tt])
                    LD.fill(cnt4)
                sched.op("sync", c4xl)

            def c4mm(t, tset=tset, kp=kp, cnt3=ck[0 if kp < 2 else kp]):
                t.wait_ge(ld_sem, cnt3[0])
                r = None
                for k in (2 * kp, 2 * kp + 1):
                    for i, tt in enumerate(tset):
                        for half_o in range(2):
                            r = t.matmul(
                                psB[:, (i * 2 + half_o) * 512:(i * 2 + half_o + 1) * 512],
                                QT[k // 8][:, (k % 8) * 256 + i * P:(k % 8) * 256 + (i + 1) * P],
                                WB[k % 8][:, half_o * 512:(half_o + 1) * 512],
                                start=(k == 0), stop=(k == 31))
                return r
            sched.op("tensor", c4mm)
            sched.bar()

        def c4f(v, tset=tset, cnt4=cnt4):
            v.wait_ge(ld_sem, cnt4[0])
            r = None
            for i, tt in enumerate(tset):
                r = v.scalar_tensor_tensor(out=U2[i][:], in0=psB[:, i * 1024:(i + 1) * 1024],
                                           scalar=RH[:, tt * 8 + 1:tt * 8 + 2],
                                           in1=W2[i][:], op0=ALU.mult, op1=ALU.add)
            return r
        sched.op("vector", c4f)
        sched.bar()

        c4_mark = STD.marker()

        def c4st(s, tset=tset, pv=c4_mark, s2=s2):
            for i, tt in enumerate(tset):
                STD.start(s, x2a_t[tt], U2[i][:])
            if debug and s2 == 7:
                STD.start(s, dbg_hqT, QT[0][:])
                STD.start(s, dbg_rh, RH[:])
            STD.fill(pv)
        sched.op("sync", c4st)
        sched.bar()

    def qkv(dst_feature, dst, wten):
        stage_weight(wten, D, "w", wmark=c4_mark)
        for oc in range(DC):
            def mm1(t, oc=oc, dst=dst, dst_feature=dst_feature):
                r = None
                if dst_feature:
                    for tq in range(4):
                        for dc in range(DC):
                            r = t.matmul(psA[:, tq * 512:(tq + 1) * 512],
                                         WB[dc][:, oc * P:(oc + 1) * P],
                                         xnT[dc][:, tq * 512:(tq + 1) * 512],
                                         start=(dc == 0), stop=(dc == DC - 1))
                else:
                    for half in range(2):
                        for dc in range(DC):
                            r = t.matmul(psA[:, half * 512:(half + 1) * 512],
                                         xnT[dc][:, (oc * 2) * P:(oc * 2 + 1) * P],
                                         WB[dc][:, half * 512:(half + 1) * 512],
                                         start=(dc == 0), stop=(dc == DC - 1))
                        for dc in range(DC):
                            r = t.matmul(psA[:, 1024 + half * 512:1024 + (half + 1) * 512],
                                         xnT[dc][:, (oc * 2 + 1) * P:(oc * 2 + 2) * P],
                                         WB[dc][:, half * 512:(half + 1) * 512],
                                         start=(dc == 0), stop=(dc == DC - 1))
                return r
            sched.op("tensor", mm1)
            sched.bar()

            def ev1(v, oc=oc, dst=dst, dst_feature=dst_feature):
                r = None
                if dst_feature:
                    r = v.tensor_copy(out=dst[oc][:], in_=psA[:, 0:T])
                else:
                    r = v.tensor_copy(out=vtile(oc * 2), in_=psA[:, 0:D])
                    r = v.tensor_copy(out=vtile(oc * 2 + 1), in_=psA[:, D:2 * D])
                return r
            sched.op("vector", ev1)
            sched.bar()

    qkv(True, QT, wq_d)
    qkv(True, KQT, wk_d)
    qkv(False, None, wv_d)

    stage_weight(wgq_d, LATD, "wgq")
    for mc in range(4):
        def gqm(t, mc=mc):
            r = None
            for tq in range(4):
                for dc in range(DC):
                    r = t.matmul(psA[:, tq * 512:(tq + 1) * 512],
                                 WB[dc][:, mc * P:(mc + 1) * P],
                                 xnT[dc][:, tq * 512:(tq + 1) * 512],
                                 start=(dc == 0), stop=(dc == DC - 1))
            return r
        sched.op("tensor", gqm)
        sched.bar()

        def gqe(v, mc=mc):
            return v.tensor_copy(out=GQT[mc][:], in_=psA[:, 0:T])
        sched.op("vector", gqe)
        sched.bar()

    cnt_bd = LD.marker()

    def bdl(s):
        for dc in range(DC):
            LD.start(s, W2[0][:, dc * HEADS:(dc + 1) * HEADS],
                     bdgl_d.rearrange("(c p) h -> c p h", p=P)[dc])
            LD.start(s, W2[1][:, dc * HEADS:(dc + 1) * HEADS],
                     bdgr_d.rearrange("(c p) h -> c p h", p=P)[dc])
        LD.fill(cnt_bd)
    sched.op("sync", bdl)
    sched.bar()

    def bdc(v):
        v.wait_ge(ld_sem, cnt_bd[0])
        r = v.tensor_copy(out=XB2[0][:, 0:DC * HEADS], in_=W2[0][:, 0:DC * HEADS])
        r = v.tensor_copy(out=XB2[1][:, 0:DC * HEADS], in_=W2[1][:, 0:DC * HEADS])
        return r
    sched.op("vector", bdc)
    sched.bar()

    for sel in range(2):
        def glm(t, sel=sel):
            r = None
            for half in range(4):
                for oc in range(DC):
                    r = t.matmul(psA[0:HEADS, half * 512:(half + 1) * 512],
                                 XB2[sel][:, oc * HEADS:(oc + 1) * HEADS],
                                 QT[oc][:, half * 512:(half + 1) * 512],
                                 start=(oc == 0), stop=(oc == DC - 1))
            return r
        sched.op("tensor", glm)
        sched.bar()

        for tb8 in range(16):
            def gls(sc, sel=sel, tb8=tb8):
                r = None
                for k in range(1):
                    tt = tb8 + k
                    r = sc.activation(out=glsb[:, k * P:(k + 1) * P],
                                      in_=psA[0:HEADS, tt * P:(tt + 1) * P],
                                      func=AF.Sigmoid)
                return r
            sched.op("scalar", gls)
            sched.bar()

            def glt_t(t, sel=sel, tb8=tb8):
                r = None
                for k in range(1):
                    r = t.transpose(psB[:, 1024 + k * HEADS:1024 + (k + 1) * HEADS],
                                    glsb[0:HEADS, k * P:(k + 1) * P],
                                    idf[0:HEADS, 0:HEADS])
                return r
            sched.op("tensor", glt_t)
            sched.bar()

            def glt_e(v, sel=sel, tb8=tb8):
                r = None
                for k in range(1):
                    tt = tb8 + k
                    r = v.tensor_copy(out=glt[tt][:, sel * HEADS:(sel + 1) * HEADS],
                                      in_=psB[:, 1024 + k * HEADS:1024 + (k + 1) * HEADS])
                return r
            sched.op("vector", glt_e)
            sched.bar()

    lps_mark = {0: STD.marker(), 1: STD.marker()}
    NLH = LHEADS
    for tt in range(NT):
        cntl = LD.marker()

        def g0(s, tt=tt, cntl=cntl):
            LD.start(s, lmsk2[0][:], lmask_t[tt])
            LD.fill(cntl)

        for k in range(NLH + 5):
            if k == 0:
                sched.op("sync", g0)
            def gp_t(t, tt=tt, k=k):
                r = None
                lh = k
                if lh < NLH:
                    base = (lh % 2) * 64
                    r = t.matmul(psA[:, (lh % 3) * 512:(lh % 3) * 512 + NLAT],
                                 GQT[lh // 2][base:base + 64, tt * P:(tt + 1) * P],
                                 GKT[lh // 2][base:base + 64, 0:NLAT],
                                 start=True, stop=True)
                lh = k - 4
                if 0 <= lh < NLH:
                    r = t.transpose(psBb[0:NLAT, 2048 + (lh % 2) * 1024:2048 + (lh % 2) * 1024 + P],
                                    pS[:, (lh % 3) * NLAT:(lh % 3 + 1) * NLAT],
                                    idb[:])
                return r
            sched.op("tensor", gp_t)

            wmk = lps_mark[tt % 2]

            def gp_v(v, tt=tt, k=k, cntl=cntl, wmk=wmk):
                r = None
                lh = k - 1
                if 0 <= lh < NLH:
                    v.wait_ge(ld_sem, cntl[0])
                    v.scalar_tensor_tensor(out=sS[:, (lh % 2) * NLAT:(lh % 2) * NLAT + NLAT],
                                           in0=psA[:, (lh % 3) * 512:(lh % 3) * 512 + NLAT],
                                           scalar=0.125, in1=lmsk2[0][:],
                                           op0=ALU.mult, op1=ALU.add)
                    VS.edge(v, v.tensor_reduce(out=mark[:, 16 + (lh % 2) * 16:17 + (lh % 2) * 16],
                                               in_=sS[:, (lh % 2) * NLAT:(lh % 2) * NLAT + NLAT],
                                               axis=AX.X, op=ALU.max))
                    VS.edge(v, v.tensor_scalar(out=mark[:, 24 + (lh % 2) * 16:25 + (lh % 2) * 16],
                                               in0=mark[:, 16 + (lh % 2) * 16:17 + (lh % 2) * 16],
                                               scalar1=-1.0, scalar2=None, op0=ALU.mult))
                lh = k - 3
                if 0 <= lh < NLH:
                    VS.edge(v, v.reciprocal(out=mark[:, 8:9], in_=rc[:, lh * 8:lh * 8 + 1]))
                    VS.edge(v, v.tensor_scalar(out=pS[:, (lh % 3) * NLAT:(lh % 3 + 1) * NLAT],
                                               in0=pS[:, (lh % 3) * NLAT:(lh % 3 + 1) * NLAT],
                                               scalar1=mark[:, 8:9], scalar2=None,
                                               op0=ALU.mult))
                lh = k - 5
                if 0 <= lh < NLH:
                    if lh % 2 == 0:
                        r = v.tensor_copy(out=LPT[lh // 2][0:NLAT, tt * P:(tt + 1) * P],
                                          in_=psBb[0:NLAT, 2048 + (lh % 2) * 1024:2048 + (lh % 2) * 1024 + P])
                    else:
                        v.wait_ge(st_sem, wmk[0])
                        r = v.tensor_copy(out=LPS[0:NLAT, (tt % 2) * P:(tt % 2 + 1) * P],
                                          in_=psBb[0:NLAT, 2048 + (lh % 2) * 1024:2048 + (lh % 2) * 1024 + P])
                if r is None:
                    r = v.tensor_copy(out=mkr[:, 0:1], in_=mkr[:, 0:1])
                return r
            sched.op("vector", gp_v)

            def gp_s(sc, tt=tt, k=k):
                lh = k - 2
                if 0 <= lh < NLH:
                    return sc.activation(out=pS[:, (lh % 3) * NLAT:(lh % 3 + 1) * NLAT],
                                         in_=sS[:, (lh % 2) * NLAT:(lh % 2) * NLAT + NLAT],
                                         func=AF.Exp,
                                         bias=mark[:, 24 + (lh % 2) * 16:25 + (lh % 2) * 16],
                                         accum_out=rc[:, lh * 8:lh * 8 + 1])
                return None
            sched.op("scalar", gp_s)

            lh6 = k - 6
            if 0 <= lh6 < NLH and lh6 % 2 == 1:
                lps_mark[tt % 2] = STD.marker()

                def g7s(s, tt=tt, lh=lh6, pv=lps_mark[tt % 2]):
                    STD.start(s, LPT[lh // 2][64:128, tt * P:(tt + 1) * P],
                              LPS[0:NLAT, (tt % 2) * P:(tt % 2 + 1) * P])
                    STD.fill(pv)
                sched.op("sync", g7s)
            sched.bar()

        lps_mark[tt % 2] = STD.marker()

        def g7z(s, tt=tt, pv=lps_mark[tt % 2]):
            STD.start(s, LPT[3][64:128, tt * P:(tt + 1) * P],
                      LPS[0:NLAT, (tt % 2) * P:(tt % 2 + 1) * P])
            STD.fill(pv)
        sched.op("sync", g7z)
        sched.bar()

    cnt_lp = STD.marker()

    def h0(s):
        STD.fill(cnt_lp)
    sched.op("sync", h0)
    sched.bar()

    h3_mark = STD.marker()
    for lh in range(LHEADS):
        def h1(t, lh=lh):
            r = None
            base = (lh % 2) * 64
            for tq in range(4):
                r = t.matmul(psA[0:64, tq * 512:(tq + 1) * 512],
                             GV[base:base + 64, lh * 64:(lh + 1) * 64],
                             LPT[lh // 2][base:base + 64, tq * 512:(tq + 1) * 512],
                             start=True, stop=True)
            return r
        if lh == 0:
            def h1w(t, lh=lh, pvg=gvd_mark):
                t.wait_ge(st_sem, pvg[0])
                t.wait_ge(st_sem, cnt_lp[0])
                return h1(t)
            sched.op("tensor", h1w)
        else:
            sched.op("tensor", h1)
        sched.bar()

        if lh % 2 == 0:
            def h2(v, lh=lh):
                r = None
                for tq in range(4):
                    r = v.tensor_copy(out=GQT[lh // 2][0:64, tq * 512:(tq + 1) * 512],
                                      in_=psA[0:64, tq * 512:(tq + 1) * 512])
                return r
            sched.op("vector", h2)
            sched.bar()
        else:
            for tq in range(4):
                def h2b(v, lh=lh, tq=tq, pv=h3_mark):
                    v.wait_ge(st_sem, pv[0])
                    return v.tensor_copy(out=LPS[0:64, (tq % 2) * 256:(tq % 2) * 256 + 256],
                                         in_=psA[0:64, tq * 512:tq * 512 + 256])
                sched.op("vector", h2b)
                sched.bar()
                h3_mark = STD.marker()

                def h3(s, lh=lh, tq=tq, pv=h3_mark):
                    STD.start(s, GQT[lh // 2][64:128, tq * 512:tq * 512 + 256],
                              LPS[0:64, (tq % 2) * 256:(tq % 2) * 256 + 256])
                    STD.fill(pv)
                sched.op("sync", h3)
                sched.bar()
                def h2d(v, lh=lh, tq=tq, pv=h3_mark):
                    v.wait_ge(st_sem, pv[0])
                    return v.tensor_copy(out=LPS[0:64, (tq % 2) * 256:(tq % 2) * 256 + 256],
                                         in_=psA[0:64, tq * 512 + 256:tq * 512 + 512])
                sched.op("vector", h2d)
                sched.bar()
                h3_mark = STD.marker()

                def h3b(s, lh=lh, tq=tq, pv=h3_mark):
                    STD.start(s, GQT[lh // 2][64:128, tq * 512 + 256:tq * 512 + 512],
                              LPS[0:64, (tq % 2) * 256:(tq % 2) * 256 + 256])
                    STD.fill(pv)
                sched.op("sync", h3b)
                sched.bar()

    cnt_rem = STD.marker()

    def h4(s):
        STD.fill(cnt_rem)
    sched.op("sync", h4)
    sched.bar()

    for grp in range(2):
        cnt = LD.marker()

        def rol(s, grp=grp, cnt=cnt):
            s.wait_ge(st_sem, cnt_rem[0])
            for i in range(2):
                LD.start(s, W2[i][:], rout_d.rearrange("(c p) o -> c p o", p=P)[grp * 2 + i])
            LD.fill(cnt)
        sched.op("sync", rol)
        sched.bar()

        def roc(v, grp=grp, cnt=cnt):
            v.wait_ge(ld_sem, cnt[0])
            r = None
            for i in range(2):
                r = v.tensor_copy(out=LPT[grp * 2 + i][:, 0:D], in_=W2[i][:])
            return r
        sched.op("vector", roc)
        sched.bar()

    stage_weight(wout_d, D, "wout", dual=False)

    xnT5f = xnT[5][:].bitcast(F32)

    def sSv(h):
        return sS[:, 0:512] if h % 2 == 0 else xnT5f[:, 0:512]

    def pS2(h):
        return pS[:, 0:512] if h % 2 == 0 else xnT[5][:, 1024:1536]

    j16_mark = STD.marker()
    for qb in range(8):
        key0 = 0 if qb == 0 else (qb - 1) * 256
        kc0 = key0 // P

        for qc in range(2):
            tt = qb * 2 + qc
            cntm = LD.marker()

            def j0(s, qb=qb, qc=qc, cntm=cntm):
                v0 = (0 if qb == 0 else 2) + qc
                LD.start(s, msk[0][:], masks_d[v0])
                LD.fill(cntm)

            for k in range(HEADS + 6):
                if k == 0:
                    sched.op("sync", j0)

                def jp_t(t, tt=tt, k=k, kc0=kc0, qc=qc):
                    r = None
                    if k == 0:
                        for kc in range(4):
                            for half in range(2):
                                r = t.matmul(psB[:, half * 512:(half + 1) * 512],
                                             GQT[kc][:, tt * P:(tt + 1) * P],
                                             LPT[kc][:, half * 512:(half + 1) * 512],
                                             start=(kc == 0), stop=(kc == 3))
                    h = k
                    if h < HEADS:
                        hb = (h % 2) * 64
                        r = t.matmul(psA[:, (h % 3) * 512:(h % 3 + 1) * 512],
                                     QT[h // 2][hb:hb + 64, tt * P:(tt + 1) * P],
                                     KQT[h // 2][hb:hb + 64, kc0 * P:kc0 * P + 512],
                                     start=True, stop=True)
                    h = k - 3
                    if 0 <= h < HEADS:
                        for kc in range(4):
                            r = t.transpose(
                                psBb[:, 2048 + (h % 2) * 1024 + kc * P:2048 + (h % 2) * 1024 + (kc + 1) * P],
                                pS2(h)[:, kc * P:(kc + 1) * P], idb[:])
                    h = k - 5
                    if 0 <= h < HEADS:
                        for kc in range(4):
                            r = t.matmul(psB[:, (h % 2) * 512:(h % 2) * 512 + DH],
                                         PTv(h)[:, kc * P:(kc + 1) * P],
                                         vtile(kc0 + kc)[:, h * DH:(h + 1) * DH],
                                         start=(kc == 0), stop=(kc == 3))
                    return r
                sched.op("tensor", jp_t)

                def jp_v(v, tt=tt, k=k, qc=qc, cntm=cntm):
                    r = None
                    if k == 1:
                        v.tensor_copy(out=rem2v(qc), in_=psB[:, 0:D])
                    h = k - 1
                    if 0 <= h < HEADS:
                        v.wait_ge(ld_sem, cntm[0])
                        v.scalar_tensor_tensor(out=sSv(h),
                                               in0=psA[:, (h % 3) * 512:(h % 3 + 1) * 512],
                                               scalar=0.125, in1=msk[0][:],
                                               op0=ALU.mult, op1=ALU.add)
                        VS.edge(v, v.tensor_reduce(
                            out=mark[:, 16 + (h % 2) * 16:17 + (h % 2) * 16],
                            in_=sSv(h), axis=AX.X, op=ALU.max))
                        VS.edge(v, v.tensor_scalar(
                            out=mark[:, 24 + (h % 2) * 16:25 + (h % 2) * 16],
                            in0=mark[:, 16 + (h % 2) * 16:17 + (h % 2) * 16],
                            scalar1=-1.0, scalar2=None, op0=ALU.mult))
                    h = k - 3
                    if 0 <= h < HEADS:
                        VS.edge(v, v.reciprocal(out=rc[:, h * 8:h * 8 + 1],
                                                in_=rc[:, h * 8 + 1:h * 8 + 2]))
                    h = k - 6
                    if 0 <= h < HEADS:
                        v.tensor_scalar(out=xnT5f[:, 768:768 + DH],
                                        in0=psB[:, (h % 2) * 512:(h % 2) * 512 + DH],
                                        scalar1=rc[:, h * 8:h * 8 + 1],
                                        scalar2=glt[tt][:, h:h + 1],
                                        op0=ALU.mult, op1=ALU.mult)
                        v.tensor_scalar(out=xnT5f[:, 832:832 + DH],
                                        in0=rem2v(qc)[:, h * DH:(h + 1) * DH],
                                        scalar1=glt[tt][:, HEADS + h:HEADS + h + 1],
                                        scalar2=None, op0=ALU.mult)
                        r = v.tensor_tensor(out=combv(qc)[:, h * DH:(h + 1) * DH],
                                            in0=xnT5f[:, 768:768 + DH],
                                            in1=xnT5f[:, 832:832 + DH], op=ALU.add)
                    if r is None:
                        r = v.tensor_copy(out=mkr[:, 0:1], in_=mkr[:, 0:1])
                    return r
                sched.op("vector", jp_v)

                def jp_s(sc, k=k):
                    r = None
                    h = k - 2
                    if 0 <= h < HEADS:
                        r = sc.activation(out=pS2(h), in_=sSv(h), func=AF.Exp,
                                          bias=mark[:, 24 + (h % 2) * 16:25 + (h % 2) * 16],
                                          accum_out=rc[:, h * 8 + 1:h * 8 + 2])
                    h = k - 4
                    if 0 <= h < HEADS:
                        r = sc.activation(out=PTv(h),
                                          in_=psBb[:, 2048 + (h % 2) * 1024:2048 + (h % 2) * 1024 + 512],
                                          func=AF.Copy, scale=1.0)
                    return r
                sched.op("scalar", jp_s)
                sched.bar()

            def j11(t, qc=qc):
                r = None
                for dc in range(DC):
                    r = t.transpose(psAb[:, 3 * 1024 + dc * P:3 * 1024 + (dc + 1) * P],
                                    combv(qc)[:, dc * P:(dc + 1) * P], idb[:])
                return r
            sched.op("tensor", j11)
            sched.bar()

            def j12(v, qc=qc):
                r = None
                for dc in range(DC):
                    r = v.tensor_copy(out=cTv(qc, dc),
                                      in_=psAb[:, 3 * 1024 + dc * P:3 * 1024 + (dc + 1) * P])
                return r
            sched.op("vector", j12)
            sched.bar()

            def j13(t, qc=qc):
                r = None
                for half in range(2):
                    for dc in range(DC):
                        r = t.matmul(psB[:, D + half * 512:D + (half + 1) * 512],
                                     cTv(qc, dc), WB[dc][:, half * 512:(half + 1) * 512],
                                     start=(dc == 0), stop=(dc == DC - 1))
                return r
            sched.op("tensor", j13)
            cntx = LD.marker()

            def j14(s, tt=tt, cntx=cntx):
                LD.start(s, W2[0][:], x2a_t[tt])
                LD.fill(cntx)
            sched.op("sync", j14)
            sched.bar()

            def j15(v, tt=tt, cntx=cntx, pv=j16_mark):
                v.wait_ge(ld_sem, cntx[0])
                v.wait_ge(st_sem, pv[0])
                return v.tensor_tensor(out=U2[0][:], in0=psB[:, D:2 * D],
                                       in1=W2[0][:], op=ALU.add)
            sched.op("vector", j15)
            sched.bar()

            j16_mark = STD.marker()

            def j16(s, tt=tt, pv=j16_mark):
                STD.start(s, out_t[tt], U2[0][:])
                STD.fill(pv)
            sched.op("sync", j16)
            sched.bar()

    def fin(g):
        g.wait_ge(st_sem, STD.count)
    sched.op("gpsimd", fin)

    streams = sched.build(ph)
    with nc.Block() as block:
        block.sync(streams["sync"])
        block.vector(streams["vector"])
        block.scalar(streams["scalar"])
        block.tensor(streams["tensor"])
        block.gpsimd(streams["gpsimd"])
    ctx.close()
    return nc


def host_prep_l1(inputs, x1, n_cores=8):
    f32 = np.float32
    gl_w = np.asarray(inputs["L1_attn_gate_local_w"], f32)
    gr_w = np.asarray(inputs["L1_attn_gate_remote_w"], f32)
    bd_gl = np.zeros((D, HEADS), f32)
    bd_gr = np.zeros((D, HEADS), f32)
    for h in range(HEADS):
        bd_gl[h * DH:(h + 1) * DH, h] = gl_w[h]
        bd_gr[h * DH:(h + 1) * DH, h] = gr_w[h]
    poolm = np.zeros((T, NLAT), f32)
    for l in range(NLAT):
        poolm[l * 32:(l + 1) * 32, l] = 1.0 / 32.0
    NEG = np.float32(-1e30)
    masks = np.full((4, P, 512), NEG, f32)
    for qc in range(2):
        for i in range(P):
            iq = qc * P + i
            lo = iq + 1
            hi = iq + 256
            masks[2 + qc, i, lo:hi + 1] = 0.0
            masks[qc, i, 0:iq + 1] = 0.0
    lmask = np.full((T, NLAT), NEG, f32)
    for t_ in range(T):
        lmask[t_, 0:t_ // 32 + 1] = 0.0
    shared = {
        "wq": np.ascontiguousarray(np.asarray(inputs["L1_attn_wq"], f32)),
        "wk": np.ascontiguousarray(np.asarray(inputs["L1_attn_wk"], f32)),
        "wv": np.ascontiguousarray(np.asarray(inputs["L1_attn_wv"], f32)),
        "wgq": np.ascontiguousarray(np.asarray(inputs["L1_attn_wgq"], f32)),
        "wgk": np.ascontiguousarray(np.asarray(inputs["L1_attn_wgk"], f32)),
        "wgv": np.ascontiguousarray(np.asarray(inputs["L1_attn_wgv"], f32)),
        "r_out_w": np.ascontiguousarray(np.asarray(inputs["L1_attn_r_out_w"], f32)),
        "w_out": np.ascontiguousarray(np.asarray(inputs["L1_attn_w_out"], f32)),
        "bd_glw": bd_gl,
        "bd_grw": bd_gr,
        "w_gT": np.ascontiguousarray(np.asarray(inputs["L1_ffn_w_gate"], f32).T),
        "w_uT": np.ascontiguousarray(np.asarray(inputs["L1_ffn_w_up"], f32).T),
        "w_dT": np.ascontiguousarray(np.asarray(inputs["L1_ffn_w_down"], f32).T),
        "ident_f": np.eye(P, dtype=f32),
        "ident_b": np.eye(P, dtype=BF16NP),
        "poolm": poolm.astype(BF16NP),
        "masks": masks,
        "lmask": lmask,
    }
    maps = []
    for c in range(n_cores):
        m = dict(shared)
        xc = np.asarray(x1[c], f32)
        if xc.shape == (T, D):
            m["x"] = np.ascontiguousarray(xc)
        maps.append(m)
    return maps


def run_l1(inputs, x1, n_cores=8, debug=False):
    nc = bass.Bass("TRN2", target_bir_lowering=False,
                   detect_race_conditions=False,
                   dynamic_dma_scratch_size=2048)
    build_l1(nc, debug=debug)
    maps = host_prep_l1(inputs, x1, n_cores)
    res = run_bass_kernel_spmd(nc, maps, core_ids=list(range(n_cores)))
    if debug:
        return res
    return np.stack([np.asarray(r["out"]) for r in res.results], 0)


def run_fused(inputs, n_cores=8):
    gmax_row = run_pmax(inputs, n_cores)
    nc = bass.Bass("TRN2", target_bir_lowering=False,
                   detect_race_conditions=False,
                   dynamic_dma_scratch_size=2048)
    shared = build_l0(nc, n_cores=n_cores, store_out=False, fused=True)
    build_l1(nc, shared=shared)
    maps0 = host_prep_l0(inputs, n_cores)
    maps1 = host_prep_l1(inputs, np.zeros((n_cores, 1, 1), np.float32), n_cores)
    maps = []
    for c in range(n_cores):
        m = dict(maps0[c])
        m1 = dict(maps1[c])
        for k in ("x", "ident_f", "ident_b"):
            m1.pop(k, None)
        m.update(m1)
        m["gmax_row"] = gmax_row
        maps.append(m)
    res = run_bass_kernel_spmd(nc, maps, core_ids=list(range(n_cores)))
    return np.stack([np.asarray(r["out"]) for r in res.results], 0)


def kernel(**inputs):
    out = run_fused(inputs, n_cores=8)
    return out.astype(np.float32)
